# version 19
# speedup vs baseline: 1.0396x; 1.0160x over previous
import sys
import numpy as np

sys.path.insert(0, "/opt/trn_rl_repo")
import concourse.bass as bass
import concourse.mybir as mybir
from concourse.bass_utils import run_bass_kernel_spmd
from contextlib import ExitStack

F32 = mybir.dt.float32
BF16 = mybir.dt.bfloat16
AF = mybir.ActivationFunctionType
ALU = mybir.AluOpType
AX = mybir.AxisListType

NCORES = 8
STRICT = False
D = 1024
NF = D // 128
T_FULL = 4096



class Prog:

    def __init__(self, nc, stack):
        self.nc = nc
        self.stack = stack
        self.SQ = stack.enter_context(nc.semaphore("SQ"))
        self.GQ = stack.enter_context(nc.semaphore("GQ"))
        self.sched = []
        self.const_tile = None
        self.const_map = {}
        self._uid = 0
        self._eng_last_clock = [[0] * 5 for _ in range(5)]
        self._count_to_op = [dict() for _ in range(5)]

    def sb(self, name, shape, dtype):
        return self.stack.enter_context(self.nc.sbuf_tensor(self.uname(name), shape, dtype))

    def ps(self, name, shape, dtype=F32):
        return self.stack.enter_context(self.nc.psum_tensor(name, shape, dtype))

    def op(self, eng, fn):
        self.sched.append((eng, fn, "c"))

    def dma(self, out, in_):
        self.sched.append(("sync", lambda e: e.dma_start(out=out, in_=in_), "d"))

    def gdma(self, out, in_):
        self.sched.append(("gpsimd", lambda e: e.dma_start(out=out, in_=in_), "g"))

    def uname(self, n):
        self._uid += 1
        return f"{n}__{self._uid}"

    def constf(self, val, parts=128):
        val = float(val)
        if self.const_tile is None:
            self.const_tile = self.sb("const_pool", [128, 16], F32)
        if val not in self.const_map:
            j = len(self.const_map)
            assert j < 16
            self.const_map[val] = j
            col = self.const_tile[:, j:j + 1]
            self.v(lambda e: e.memset(col, val))
        j = self.const_map[val]
        return self.const_tile[0:parts, j:j + 1]

    def t(self, fn):
        self.op("tensor", fn)

    def v(self, fn):
        self.op("vector", fn)

    def s(self, fn):
        self.op("scalar", fn)


    ENGS = ("sync", "vector", "scalar", "tensor", "gpsimd")
    NDMASLOT = 8

    def finalize(self, strict=False):
        nc = self.nc
        engines = {"sync": nc.sync, "vector": nc.vector, "scalar": nc.scalar,
                   "tensor": nc.tensor, "gpsimd": nc.gpsimd}

        mloc_cache = {}

        def mloc_info(name):
            if name not in mloc_cache:
                try:
                    m = nc.lookup_mloc(name)
                    base = int(m.addr)
                    if str(m.type) == "PSUM":
                        base += int(m.bank) * 2048
                    mloc_cache[name] = (str(m.type), base)
                except Exception:
                    mloc_cache[name] = ("DRAM", 0)
            return mloc_cache[name]

        def intervals(args):
            out = []
            for a in args:
                name = getattr(a, "memref", None)
                ap = getattr(a, "ap", None)
                if name is None:
                    t = getattr(a, "tensor", None)
                    if t is None:
                        continue
                    name = t.name
                if ap is None:
                    continue
                try:
                    off = int(a.offset)
                except (TypeError, ValueError):
                    off = 0
                span = 1
                for step, count in list(ap):
                    span += int(abs(step)) * (int(count) - 1)
                dsz = mybir.dt.size(a.dtype) if getattr(a, "dtype", None) is not None else 4
                space, base = mloc_info(name)
                if space == "DRAM":
                    out.append((name, off * dsz, (off + span) * dsz))
                else:
                    out.append((space, base + off * dsz, base + (off + span) * dsz))
            return out

        NS = self.NDMASLOT
        chan_names = list(self.ENGS)
        dma_chan = {}
        for q in ("sync", "gpsimd"):
            for r in range(NS):
                dma_chan[(q, r)] = len(chan_names)
                chan_names.append(f"dq_{q}{r}")
        NCH = len(chan_names)
        ei = {e: i for i, e in enumerate(self.ENGS)}
        te = ei["tensor"]

        rw = []
        dry_sem = self.stack.enter_context(nc.semaphore("S_dry"))
        with nc.discard():
            for eng, fn, kind in self.sched:
                bi = fn(engines[eng])
                if kind in ("d", "g"):
                    bi.then_inc(dry_sem, 16)
                inst = bi.ins
                rw.append((intervals(inst.ins), intervals(inst.outs)))

        ecnt = [0] * NCH
        opdep = []
        opchan = []
        opcount = []
        opclock = []
        count_to_op = [dict() for _ in range(NCH)]
        eng_last_clock = [[0] * NCH for _ in range(5)]
        last_access = {}
        dma_idx = {"sync": 0, "gpsimd": 0}
        for i, (eng, fn, kind) in enumerate(self.sched):
            e = ei[eng]
            if kind in ("d", "g"):
                q = eng
                slot = dma_idx[q] % NS
                dma_idx[q] += 1
                ch = dma_chan[(q, slot)]
            else:
                ch = e
            reads, writes = rw[i]
            deps = {}
            if kind in ("d", "g") and ecnt[ch]:
                deps[ch] = ecnt[ch]
            for (name, lo, hi) in reads:
                for ent in last_access.get(name, ()):
                    if ent[3] and ent[0] < hi and lo < ent[1]:
                        j = ent[2]
                        jc = opcount[j]
                        jch = opchan[j]
                        if deps.get(jch, -1) < jc:
                            deps[jch] = jc
            for (name, lo, hi) in writes:
                for ent in last_access.get(name, ()):
                    if ent[0] < hi and lo < ent[1]:
                        j = ent[2]
                        jc = opcount[j]
                        jch = opchan[j]
                        if deps.get(jch, -1) < jc:
                            deps[jch] = jc
            clk = list(eng_last_clock[e])
            needed = []
            for jch, jc in sorted(deps.items()):
                if clk[jch] >= jc:
                    continue
                jop = count_to_op[jch].get(jc)
                if jop is not None:
                    jclk = opclock[jop]
                    for k in range(NCH):
                        if jclk[k] > clk[k]:
                            clk[k] = jclk[k]
                if not (jch == e == te):
                    needed.append((jch, jc))
                if jc > clk[jch]:
                    clk[jch] = jc
            inc = 16 if kind in ("d", "g") else 1
            ecnt[ch] += inc
            opchan.append(ch)
            opcount.append(ecnt[ch])
            count_to_op[ch][ecnt[ch]] = i
            cclk = list(clk)
            cclk[ch] = ecnt[ch]
            opclock.append(cclk)
            eng_last_clock[e] = clk
            opdep.append(needed)
            for (name, lo, hi) in writes:
                lst = last_access.setdefault(name, [])
                lst[:] = [ent for ent in lst if not (lo <= ent[0] and ent[1] <= hi)]
                lst.append([lo, hi, i, True])
            for (name, lo, hi) in reads:
                last_access.setdefault(name, []).append([lo, hi, i, False])

        block = self.stack.enter_context(nc.Block())
        sems = []
        for cn in chan_names:
            sems.append(self.stack.enter_context(nc.semaphore(f"S_{cn}")))
        total = list(ecnt)
        sched = self.sched

        def runner(en):
            def go(eng):
                for i, (name, fn, kind) in enumerate(sched):
                    if name != en:
                        continue
                    deps = opdep[i]
                    for (jch, jc) in (deps[:-1] if len(deps) > 1 else []):
                        eng.wait_ge(sems[jch], jc)
                    inst = fn(eng)
                    if deps:
                        jch, jc = deps[-1]
                        inst._wait_ge(sems[jch], jc)
                    inst.then_inc(sems[opchan[i]], 16 if kind in ("d", "g") else 1)
                if en == "sync":
                    for k in range(NCH):
                        if total[k]:
                            eng.wait_ge(sems[k], total[k])
            return go

        block.sync(runner("sync"))
        block.vector(runner("vector"))
        block.scalar(runner("scalar"))
        block.tensor(runner("tensor"))
        block.gpsimd(runner("gpsimd"))


def _ln_rows(p, psum_sum, psum_sq, n, eps, rows, nfeat=D):
    mean = rows[0][0:1, 0:n]
    var = rows[1][0:1, 0:n]
    msh = rows[2][0:1, 0:n]
    tmp = rows[3][0:1, 0:n]
    p.v(lambda e: e.tensor_scalar_mul(mean, psum_sum, 1.0 / nfeat))
    p.v(lambda e: e.tensor_scalar_mul(tmp, psum_sq, 1.0 / nfeat))
    p.v(lambda e: e.tensor_mul(var, mean, mean))
    p.v(lambda e: e.tensor_sub(var, tmp, var))
    p.s(lambda e: e.activation(tmp, var, AF.Ln, bias=np.float32(eps)))
    p.s(lambda e: e.activation(var, tmp, AF.Exp, scale=-0.5))
    p.v(lambda e: e.scalar_tensor_tensor(msh, mean, -1.0, var, ALU.mult, ALU.mult))


def ceil_div(a, b):
    return (a + b - 1) // b



def emit_mamba(p, io, T=T_FULL, Tc_a=512, Tc_b=1024, Tc_d=256):
    nc = p.nc
    DI = 2048
    ND = DI // 128
    nchunks_a = T // Tc_a

    with ExitStack() as st:
        sb = lambda n, s, d: st.enter_context(nc.sbuf_tensor(p.uname(n), s, d))
        ps = lambda n, s, d=F32: st.enter_context(nc.psum_tensor(p.uname(n), s, d))

        ones_bf = sb("ones_bf", [128, 1], BF16)
        ones_f32 = sb("ones_f32", [128, 1], F32)
        ones1_f32 = sb("ones1_f32", [1, 128], F32)
        rmax = max(Tc_a, Tc_d)
        rows = (sb("r_mean", [1, rmax], F32), sb("r_rstd", [1, rmax], F32),
                sb("r_mshift", [1, rmax], F32), sb("r_tmp", [1, rmax], F32))
        p.v(lambda e: e.memset(ones_bf[:], 1.0))
        p.v(lambda e: e.memset(ones_f32[:], 1.0))
        p.v(lambda e: e.memset(ones1_f32[:], 1.0))

        ln1g = sb("t_ln1g", [128, NF], F32)
        ln1b = sb("t_ln1b", [128, NF], F32)
        p.dma(ln1g[:], io["ln1g"])
        p.dma(ln1b[:], io["ln1b"])
        convw = sb("t_convw", [128, 16, 4], F32)
        convb = sb("t_convb", [128, 16], F32)
        p.dma(convw[:], io["convw"])
        p.dma(convb[:], io["convb"])
        dtbias = sb("t_dtbias", [128, 16], F32)
        p.dma(dtbias[:], io["dtbias"])
        xprojT_dt = sb("t_xprojT_dt", [128, 16, 64], BF16)
        xprojT_B = sb("t_xprojT_B", [128, 16, 16], BF16)
        xprojT_C = sb("t_xprojT_C", [128, 16, 16], BF16)
        p.gdma(xprojT_dt[:], io["xprojT_dt"])
        p.gdma(xprojT_B[:], io["xprojT_B"])
        p.gdma(xprojT_C[:], io["xprojT_C"])
        dtprojT = sb("t_dtprojT", [64, DI], BF16)
        p.gdma(dtprojT[:], io["dtprojT"])

        with ExitStack() as stA:
            sbA = lambda n, s, d: stA.enter_context(nc.sbuf_tensor(p.uname(n), s, d))
            psA = lambda n, s, d=F32: stA.enter_context(nc.psum_tensor(p.uname(n), s, d))
            w_inp = sbA("w_inp", [128, 8, 2 * DI], BF16)
            for kt in range(8):
                p.gdma(w_inp[:, kt, :], io["inprojT"][kt * 128:(kt + 1) * 128, :])

            Tca = Tc_a
            x_c = sbA("x_c", [128, NF, Tca], BF16)
            xsq_s = sbA("xsq", [128, Tca], BF16)
            xsq = [xsq_s, xsq_s]
            ln_c = sbA("ln_c", [128, NF, Tca], BF16)
            lntmp_s = sbA("lntmp", [128, Tca], BF16)
            lntmp = [lntmp_s, lntmp_s]
            xi_c = sbA("xi_c", [128, 16, Tca + 3], BF16)
            sz_c = [sbA("sz_t", [128, Tca], BF16) for _ in range(2)]
            xc_s = sbA("xc_c", [128, 16, Tca], BF16)
            xc_c = [xc_s, xc_s]
            cv_s = sbA("cv", [128, Tca], F32)
            cv = [cv_s, cv_s]
            dtraw = sbA("dtraw", [64, Tca], BF16)
            Brow = [sbA("Brow", [16, Tca], BF16) for _ in range(2)]
            Crow = [sbA("Crow", [16, Tca], BF16) for _ in range(2)]
            dt_s = sbA("dt_c", [128, 16, Tca], BF16)
            dt_c = [dt_s, dt_s]
            sp1_s = sbA("sp1", [128, Tca], F32)
            sp1 = [sp1_s, sp1_s]
            sp2 = [cv_s, cv_s]

            p_sumA = psA("p_sumA", [1, Tca])
            p_sqA = psA("p_sqA", [1, Tca])
            p_b1 = psA("p_b1", [128, Tca])
            p_b2 = psA("p_b2", [128, Tca])
            p_mm = [psA("p_mm", [128, Tca]) for _ in range(2)]
            p_dbl = psA("p_dbl", [96, Tca])
            p_bc = psA("p_bc", [128, Tca])

            SEL16 = sbA("t_SEL16", [16, 2048], BF16)
            p.gdma(SEL16[:], io["SEL16"])
            negA = sbA("negA", [128, 16, 16], F32)
            p.dma(negA[:], io["Alog_fm"])
            p.s(lambda e: e.activation(negA[:], negA[:], AF.Exp))
            p.v(lambda e: e.tensor_scalar_mul(negA[:], negA[:], -1.0))
            dtxc_all = sbA("dtxc_all", [128, 16, Tca], BF16)
            NDEEP = 3
            a_t = [sbA("a_t", [128, Tca], BF16) for _ in range(NDEEP)]
            b_t = [sbA("b_t", [128, Tca], BF16) for _ in range(2)]
            h_t = [sbA("h_t", [128, Tca], BF16) for _ in range(NDEEP)]
            ym_t = [sbA("ym_t", [128, Tca], BF16) for _ in range(2)]
            yacc = sbA("yacc", [128, 16, Tca], BF16)
            carry = sbA("carry", [128, 256], F32)
            Bb_s = sbA("Bb", [128, Tca], BF16)
            Cb_s = sbA("Cb", [128, Tca], BF16)
            Bb_ = [Bb_s, Bb_s]
            Cb_ = [Cb_s, Cb_s]
            p.v(lambda e: e.memset(carry[:], 0.0))

            carry_q = []

            def flush_carry(lag=0):
                while len(carry_q) > lag:
                    dst, src = carry_q.pop(0)
                    p.s(lambda e, dst=dst, src=src: e.activation(dst, src, AF.Copy))

            for c in range(T // Tca):
                t0 = c * Tca
                cp = c % 2
                xcc, dtc, Brw, Crw = xc_c[cp], dt_c[cp], Brow[cp], Crow[cp]
                for ft in range(NF):
                    p.gdma(x_c[:, ft, :], io["xT"][ft * 128:(ft + 1) * 128, t0:t0 + Tca])
                for ft in range(NF):
                    p.t(lambda e, ft=ft: e.matmul(p_sumA[:], ones_bf[:], x_c[:, ft, :],
                                                  start=(ft == 0), stop=(ft == NF - 1)))
                for ft in range(NF):
                    p.s(lambda e, ft=ft: e.activation(xsq[ft % 2][:], x_c[:, ft, :], AF.Square))
                    p.t(lambda e, ft=ft: e.matmul(p_sqA[:], ones_bf[:], xsq[ft % 2][:],
                                                  start=(ft == 0), stop=(ft == NF - 1)))
                _ln_rows(p, p_sumA[:], p_sqA[:], Tca, 1e-5, rows)
                p.t(lambda e: e.matmul(p_b1[:], ones1_f32[:], rows[1][0:1, 0:Tca], start=True, stop=True))
                p.t(lambda e: e.matmul(p_b2[:], ones1_f32[:], rows[2][0:1, 0:Tca], start=True, stop=True))
                for ft in range(NF):
                    lt = lntmp[ft % 2]
                    p.v(lambda e, ft=ft, lt=lt: e.tensor_tensor(lt[:], x_c[:, ft, :], p_b1[:], ALU.mult))
                    p.v(lambda e, lt=lt: e.tensor_tensor(lt[:], lt[:], p_b2[:], ALU.add))
                    p.v(lambda e, ft=ft, lt=lt: e.tensor_scalar(ln_c[:, ft, :], lt[:],
                        ln1g[:, ft:ft + 1], ln1b[:, ft:ft + 1], ALU.mult, ALU.add))
                if c == 0:
                    p.v(lambda e: e.memset(xi_c[:, :, 0:3], 0.0))
                else:
                    p.v(lambda e: e.tensor_copy(xi_c[:, :, 0:3], xi_c[:, :, Tca:Tca + 3]))
                for mt in range(32):
                    pm = p_mm[mt % 2]
                    for kt in range(NF):
                        p.t(lambda e, mt=mt, kt=kt, pm=pm: e.matmul(
                            pm[:], w_inp[:, kt, mt * 128:(mt + 1) * 128], ln_c[:, kt, :],
                            start=(kt == 0), stop=(kt == NF - 1)))
                    if mt < 16:
                        p.v(lambda e, mt=mt, pm=pm: e.tensor_copy(xi_c[:, mt, 3:3 + Tca], pm[:]))
                    else:
                        szt = sz_c[mt % 2]
                        p.s(lambda e, mt=mt, pm=pm, szt=szt: e.activation(szt[:], pm[:], AF.Silu))
                        p.dma(io["scr_sz"][:, mt - 16, t0:t0 + Tca], szt[:])
                for dti in range(16):
                    cvt = cv[dti % 2]
                    p.v(lambda e, dti=dti, cvt=cvt: e.tensor_scalar_mul(cvt[:], xi_c[:, dti, 0:Tca],
                                                               convw[:, dti, 0:1]))
                    for k in range(1, 4):
                        p.v(lambda e, dti=dti, k=k, cvt=cvt: e.scalar_tensor_tensor(
                            cvt[:], xi_c[:, dti, k:k + Tca], convw[:, dti, k:k + 1], cvt[:],
                            ALU.mult, ALU.add))
                    p.s(lambda e, dti=dti, cvt=cvt, xcc=xcc: e.activation(xcc[:, dti, :], cvt[:], AF.Silu,
                                                        bias=convb[:, dti:dti + 1]))
                for kt in range(16):
                    p.t(lambda e, kt=kt, xcc=xcc: e.matmul(p_dbl[0:64, :], xprojT_dt[:, kt, :], xcc[:, kt, :],
                                                  start=(kt == 0), stop=(kt == 15)))
                p.v(lambda e: e.tensor_copy(dtraw[:], p_dbl[0:64, :]))
                for kt in range(16):
                    p.t(lambda e, kt=kt, xcc=xcc: e.matmul(p_dbl[64:80, :], xprojT_B[:, kt, :], xcc[:, kt, :],
                                                  start=(kt == 0), stop=(kt == 15)))
                p.v(lambda e, Brw=Brw: e.tensor_copy(Brw[:], p_dbl[64:80, :]))
                for kt in range(16):
                    p.t(lambda e, kt=kt, xcc=xcc: e.matmul(p_dbl[0:16, :], xprojT_C[:, kt, :], xcc[:, kt, :],
                                                  start=(kt == 0), stop=(kt == 15)))
                p.v(lambda e, Crw=Crw: e.tensor_copy(Crw[:], p_dbl[0:16, :]))
                for dti in range(16):
                    pm, s1, s2_ = p_mm[dti % 2], sp1[dti % 2], sp2[dti % 2]
                    p.t(lambda e, dti=dti, pm=pm: e.matmul(pm[:], dtprojT[:, dti * 128:(dti + 1) * 128],
                                                    dtraw[:], start=True, stop=True))
                    p.s(lambda e, dti=dti, pm=pm, s1=s1: e.activation(s1[:], pm[:], AF.Abs,
                                                        bias=dtbias[:, dti:dti + 1]))
                    p.s(lambda e, s1=s1: e.activation(s1[:], s1[:], AF.Exp, scale=-1.0))
                    p.s(lambda e, s1=s1: e.activation(s1[:], s1[:], AF.Ln, bias=np.float32(1.0)))
                    p.v(lambda e, dti=dti, pm=pm, s2_=s2_: e.tensor_scalar(s2_[:], pm[:], dtbias[:, dti:dti + 1],
                                                           0.0, ALU.add, ALU.max))
                    p.v(lambda e, dti=dti, s1=s1, s2_=s2_, dtc=dtc: e.tensor_tensor(dtc[:, dti, :], s1[:], s2_[:], ALU.add))
                p.dma(io["scr_xc"][:, :, t0:t0 + Tca], xcc[:, :, :])
                for dti in range(16):
                    p.v(lambda e, dti=dti, dtc=dtc, xcc=xcc: e.tensor_tensor(
                        dtxc_all[:, dti, :], dtc[:, dti, :], xcc[:, dti, :], ALU.mult))
                for n in range(16):
                    Bb, Cb = Bb_[n % 2], Cb_[n % 2]
                    p.t(lambda e, n=n, Brw=Brw: e.matmul(
                        p_bc[:], SEL16[:, n * 128:(n + 1) * 128], Brw[:], start=True, stop=True))
                    p.v(lambda e, Bb=Bb: e.tensor_copy(Bb[:], p_bc[:]))
                    p.t(lambda e, n=n, Crw=Crw: e.matmul(
                        p_bc[:], SEL16[:, n * 128:(n + 1) * 128], Crw[:], start=True, stop=True))
                    p.v(lambda e, Cb=Cb: e.tensor_copy(Cb[:], p_bc[:]))
                    for dti in range(16):
                        it = n * 16 + dti
                        pa, pb, ph, pym = (a_t[it % NDEEP], b_t[dti % 2],
                                           h_t[it % NDEEP], ym_t[dti % 2])
                        gg = (dti % 2 == 1)
                        eng = (lambda f: p.op("gpsimd", f)) if gg else p.v
                        p.s(lambda e, dti=dti, n=n, pa=pa, dtc=dtc: e.activation(
                            pa[:], dtc[:, dti, :], AF.Exp, scale=negA[:, dti, n:n + 1]))
                        flush_carry(lag=2)
                        eng(lambda e, dti=dti, pb=pb, Bb=Bb: e.tensor_tensor(
                            pb[:], dtxc_all[:, dti, :], Bb[:], ALU.mult))
                        p.v(lambda e, dti=dti, n=n, pa=pa, pb=pb, ph=ph: e.tensor_tensor_scan(
                            ph[:], pa[:], pb[:], carry[:, dti * 16 + n:dti * 16 + n + 1],
                            ALU.mult, ALU.add))
                        carry_q.append((carry[:, dti * 16 + n:dti * 16 + n + 1],
                                        ph[:, Tca - 1:Tca]))
                        if n == 0:
                            eng(lambda e, dti=dti, ph=ph, Cb=Cb: e.tensor_tensor(yacc[:, dti, :], ph[:], Cb[:], ALU.mult))
                        else:
                            eng(lambda e, ph=ph, pym=pym, Cb=Cb: e.tensor_tensor(
                                pym[:], ph[:], Cb[:], ALU.mult))
                            eng(lambda e, dti=dti, pym=pym: e.tensor_tensor(yacc[:, dti, :], yacc[:, dti, :],
                                                                   pym[:], ALU.add))
                flush_carry(lag=0)
                p.dma(io["scr_y"][:, :, t0:t0 + Tca], yacc[:, :, :])

        with ExitStack() as stC:
            sbC = lambda n, s, d: stC.enter_context(nc.sbuf_tensor(p.uname(n), s, d))
            psC = lambda n, s, d=F32: stC.enter_context(nc.psum_tensor(p.uname(n), s, d))
            Tc = Tc_a
            w_outp = sbC("w_outp", [128, 16, D], BF16)
            for kt in range(16):
                p.gdma(w_outp[:, kt, :], io["outprojT"][kt * 128:(kt + 1) * 128, :])
            D_fm = sbC("t_D_fm", [128, 16], F32)
            p.dma(D_fm[:], io["D_fm"])
            y_c = sbC("y_c", [128, 16, Tc], BF16)
            xc2 = sbC("xc2", [128, 16, Tc], BF16)
            sz2 = sbC("sz2", [128, 16, Tc], BF16)
            x_c2 = sbC("x_c2", [128, NF, Tc], BF16)
            yg = sbC("yg", [128, 16, Tc], BF16)
            gtmp = [sbC("gtmp", [128, Tc], F32) for _ in range(2)]
            r1_c = sbC("r1_c", [128, NF, Tc], BF16)
            s2_c = sbC("s2_c", [128, NF, Tc], BF16)
            p_o = [psC("p_o", [128, Tc]) for _ in range(2)]
            for c in range(T // Tc):
                t0 = c * Tc
                p.dma(y_c[:], io["scr_y"][:, :, t0:t0 + Tc])
                p.dma(xc2[:], io["scr_xc"][:, :, t0:t0 + Tc])
                p.dma(sz2[:], io["scr_sz"][:, :, t0:t0 + Tc])
                for ft in range(NF):
                    p.gdma(x_c2[:, ft, :], io["xT"][ft * 128:(ft + 1) * 128, t0:t0 + Tc])
                for dti in range(16):
                    gt = gtmp[dti % 2]
                    p.v(lambda e, dti=dti, gt=gt: e.scalar_tensor_tensor(
                        gt[:], xc2[:, dti, :], D_fm[:, dti:dti + 1], y_c[:, dti, :],
                        ALU.mult, ALU.add))
                    p.op("gpsimd", lambda e, dti=dti, gt=gt: e.tensor_tensor(yg[:, dti, :], gt[:],
                                                           sz2[:, dti, :], ALU.mult))
                for mt in range(NF):
                    po = p_o[mt % 2]
                    for kt in range(16):
                        p.t(lambda e, mt=mt, kt=kt, po=po: e.matmul(
                            po[:], w_outp[:, kt, mt * 128:(mt + 1) * 128], yg[:, kt, :],
                            start=(kt == 0), stop=(kt == 15)))
                    p.v(lambda e, mt=mt, po=po: e.tensor_tensor(r1_c[:, mt, :], x_c2[:, mt, :],
                                                         po[:], ALU.add))
                    p.op("gpsimd", lambda e, mt=mt: e.tensor_tensor(s2_c[:, mt, :], r1_c[:, mt, :],
                                                         x_c2[:, mt, :], ALU.add))
                p.dma(io["scr_r1"][:, :, t0:t0 + Tc], r1_c[:, :, :])
                p.dma(io["scr_s2"][:, :, t0:t0 + Tc], s2_c[:, :, :])

        with ExitStack() as stD:
            sbD = lambda n, s, d: stD.enter_context(nc.sbuf_tensor(p.uname(n), s, d))
            psD = lambda n, s, d=F32: stD.enter_context(nc.psum_tensor(p.uname(n), s, d))
            Tc = Tc_d
            w1 = sbD("w1", [128, NF, 4 * D], BF16)
            for kt in range(NF):
                p.gdma(w1[:, kt, :], io["fw1T"][kt * 128:(kt + 1) * 128, :])
            w2 = sbD("w2", [128, 32, D], BF16)
            for kt in range(32):
                p.gdma(w2[:, kt, :], io["fw2T"][kt * 128:(kt + 1) * 128, :])
            ln2g = sbD("t_ln2g", [128, NF], F32)
            ln2b = sbD("t_ln2b", [128, NF], F32)
            fb1 = sbD("t_fb1", [128, 32], F32)
            fb2 = sbD("t_fb2", [128, NF], F32)
            p.dma(ln2g[:], io["ln2g"])
            p.dma(ln2b[:], io["ln2b"])
            p.dma(fb1[:], io["fb1"])
            p.dma(fb2[:], io["fb2"])
            r1_d = sbD("r1_d", [128, NF, Tc], BF16)
            s2_d = sbD("s2_d", [128, NF, Tc], BF16)
            h_d = sbD("h_d", [128, NF, Tc], BF16)
            hr = sbD("hr", [128, 32, Tc], BF16)
            u_d = sbD("u_d", [128, NF, Tc], F32)
            xo = sbD("xo", [128, NF, Tc], BF16)
            xsqd = [sbD("xsqd", [128, Tc], F32) for _ in range(2)]
            lntmp2 = [sbD("lntmp2", [128, Tc], F32) for _ in range(2)]
            p_sumD = psD("p_sumD", [1, Tc])
            p_sqD = psD("p_sqD", [1, Tc])
            p_b1d = psD("p_b1d", [128, Tc])
            p_b2d = psD("p_b2d", [128, Tc])
            p_m = [psD("p_m", [128, Tc]) for _ in range(2)]
            for c in range(T // Tc):
                t0 = c * Tc
                p.dma(r1_d[:], io["scr_r1"][:, :, t0:t0 + Tc])
                p.dma(s2_d[:], io["scr_s2"][:, :, t0:t0 + Tc])
                for ft in range(NF):
                    p.t(lambda e, ft=ft: e.matmul(p_sumD[:], ones_bf[:], r1_d[:, ft, :],
                                                  start=(ft == 0), stop=(ft == NF - 1)))
                for ft in range(NF):
                    p.s(lambda e, ft=ft: e.activation(xsqd[ft % 2][:], r1_d[:, ft, :], AF.Square))
                    p.t(lambda e, ft=ft: e.matmul(p_sqD[:], ones_f32[:], xsqd[ft % 2][:],
                                                  start=(ft == 0), stop=(ft == NF - 1)))
                _ln_rows(p, p_sumD[:], p_sqD[:], Tc, 1e-5, rows)
                p.t(lambda e: e.matmul(p_b1d[:], ones1_f32[:], rows[1][0:1, 0:Tc], start=True, stop=True))
                p.t(lambda e: e.matmul(p_b2d[:], ones1_f32[:], rows[2][0:1, 0:Tc], start=True, stop=True))
                for ft in range(NF):
                    lt = lntmp2[ft % 2]
                    p.v(lambda e, ft=ft, lt=lt: e.tensor_tensor(lt[:], r1_d[:, ft, :], p_b1d[:], ALU.mult))
                    p.v(lambda e, lt=lt: e.tensor_tensor(lt[:], lt[:], p_b2d[:], ALU.add))
                    p.v(lambda e, ft=ft, lt=lt: e.tensor_scalar(h_d[:, ft, :], lt[:],
                        ln2g[:, ft:ft + 1], ln2b[:, ft:ft + 1], ALU.mult, ALU.add))
                for mt in range(32):
                    pm = p_m[mt % 2]
                    for kt in range(NF):
                        p.t(lambda e, mt=mt, kt=kt, pm=pm: e.matmul(
                            pm[:], w1[:, kt, mt * 128:(mt + 1) * 128], h_d[:, kt, :],
                            start=(kt == 0), stop=(kt == NF - 1)))
                    p.v(lambda e, mt=mt, pm=pm: e.tensor_scalar(hr[:, mt, :], pm[:], fb1[:, mt:mt + 1],
                                                         0.0, ALU.add, ALU.max))
                for mt in range(NF):
                    pm = p_m[mt % 2]
                    for kt in range(32):
                        p.t(lambda e, mt=mt, kt=kt, pm=pm: e.matmul(
                            pm[:], w2[:, kt, mt * 128:(mt + 1) * 128], hr[:, kt, :],
                            start=(kt == 0), stop=(kt == 31)))
                    p.v(lambda e, mt=mt, pm=pm: e.scalar_tensor_tensor(
                        u_d[:, mt, :], pm[:], fb2[:, mt:mt + 1], s2_d[:, mt, :],
                        ALU.add, ALU.add))
                for ft in range(NF):
                    p.t(lambda e, ft=ft: e.matmul(p_sumD[:], ones_f32[:], u_d[:, ft, :],
                                                  start=(ft == 0), stop=(ft == NF - 1)))
                for ft in range(NF):
                    p.s(lambda e, ft=ft: e.activation(xsqd[ft % 2][:], u_d[:, ft, :], AF.Square))
                    p.t(lambda e, ft=ft: e.matmul(p_sqD[:], ones_f32[:], xsqd[ft % 2][:],
                                                  start=(ft == 0), stop=(ft == NF - 1)))
                _ln_rows(p, p_sumD[:], p_sqD[:], Tc, 1e-5, rows)
                p.t(lambda e: e.matmul(p_b1d[:], ones1_f32[:], rows[1][0:1, 0:Tc], start=True, stop=True))
                p.t(lambda e: e.matmul(p_b2d[:], ones1_f32[:], rows[2][0:1, 0:Tc], start=True, stop=True))
                for ft in range(NF):
                    lt = lntmp2[ft % 2]
                    p.v(lambda e, ft=ft, lt=lt: e.tensor_tensor(lt[:], u_d[:, ft, :], p_b1d[:], ALU.mult))
                    p.v(lambda e, ft=ft, lt=lt: e.tensor_tensor(xo[:, ft, :], lt[:], p_b2d[:], ALU.add))
                p.dma(io["xout"][:, :, t0:t0 + Tc], xo[:, :, :])



def prep_mamba_inputs(xb, w):
    c = np.ascontiguousarray
    DI = 2048
    SEL16 = np.zeros((16, 16 * 128), np.float32)
    for n in range(16):
        SEL16[n, n * 128:(n + 1) * 128] = 1.0
    fm = lambda v, nt: c(v.reshape(nt, 128).T)
    fm3 = lambda m, nt: c(m.reshape(nt, 128, m.shape[-1]).transpose(1, 0, 2))
    return {
        "xT": c(xb.T),
        "inprojT": c(w["L0_attn_in_proj_w"].T),
        "convw": fm3(w["L0_attn_conv_w"][:, 0, :], 16),
        "convb": fm(w["L0_attn_conv_b"], 16),
        "xprojT_dt": fm3(c(w["L0_attn_x_proj_w"][0:64].T), 16),
        "xprojT_B": fm3(c(w["L0_attn_x_proj_w"][64:80].T), 16),
        "xprojT_C": fm3(c(w["L0_attn_x_proj_w"][80:96].T), 16),
        "dtprojT": c(w["L0_attn_dt_proj_w"].T),
        "dtbias": fm(w["L0_attn_dt_proj_b"], 16),
        "Alog_fm": fm3(w["L0_attn_A_log"], 16),
        "D_fm": fm(w["L0_attn_D"], 16),
        "outprojT": c(w["L0_attn_out_proj_w"].T),
        "ln1g": fm(w["L0_attn_ln1_g"], NF),
        "ln1b": fm(w["L0_attn_ln1_b"], NF),
        "ln2g": fm(w["L0_attn_ln2_g"], NF),
        "ln2b": fm(w["L0_attn_ln2_b"], NF),
        "fw1T": c(w["L0_attn_ffn_w1"].T),
        "fb1": fm(w["L0_attn_ffn_b1"], 32),
        "fw2T": c(w["L0_attn_ffn_w2"].T),
        "fb2": fm(w["L0_attn_ffn_b2"], NF),
        "SEL16": SEL16,
    }


def build_mamba_program(T=T_FULL):
    nc = bass.Bass("TRN2", target_bir_lowering=False)
    DI = 2048
    dram = lambda n, s, dt, k: nc.dram_tensor(n, s, dt, kind=k).ap()
    io = {
        "xT": dram("xT", [D, T], F32, "ExternalInput"),
        "inprojT": dram("inprojT", [D, 2 * DI], F32, "ExternalInput"),
        "convw": dram("convw", [128, 16, 4], F32, "ExternalInput"),
        "convb": dram("convb", [128, 16], F32, "ExternalInput"),
        "xprojT_dt": dram("xprojT_dt", [128, 16, 64], F32, "ExternalInput"),
        "xprojT_B": dram("xprojT_B", [128, 16, 16], F32, "ExternalInput"),
        "xprojT_C": dram("xprojT_C", [128, 16, 16], F32, "ExternalInput"),
        "dtprojT": dram("dtprojT", [64, DI], F32, "ExternalInput"),
        "dtbias": dram("dtbias", [128, 16], F32, "ExternalInput"),
        "Alog_fm": dram("Alog_fm", [128, 16, 16], F32, "ExternalInput"),
        "D_fm": dram("D_fm", [128, 16], F32, "ExternalInput"),
        "outprojT": dram("outprojT", [DI, D], F32, "ExternalInput"),
        "ln1g": dram("ln1g", [128, NF], F32, "ExternalInput"),
        "ln1b": dram("ln1b", [128, NF], F32, "ExternalInput"),
        "ln2g": dram("ln2g", [128, NF], F32, "ExternalInput"),
        "ln2b": dram("ln2b", [128, NF], F32, "ExternalInput"),
        "fw1T": dram("fw1T", [D, 4 * D], F32, "ExternalInput"),
        "fb1": dram("fb1", [128, 32], F32, "ExternalInput"),
        "fw2T": dram("fw2T", [4 * D, D], F32, "ExternalInput"),
        "fb2": dram("fb2", [128, NF], F32, "ExternalInput"),
        "SEL16": dram("SEL16", [16, 2048], F32, "ExternalInput"),
        "scr_xc": dram("scr_xc", [128, 16, T], BF16, "Internal"),
        "scr_dt": dram("scr_dt", [128, 16, T], BF16, "Internal"),
        "scr_sz": dram("scr_sz", [128, 16, T], BF16, "Internal"),
        "scr_B": dram("scr_B", [16, T], BF16, "Internal"),
        "scr_C": dram("scr_C", [16, T], BF16, "Internal"),
        "scr_y": dram("scr_y", [128, 16, T], BF16, "Internal"),
        "scr_r1": dram("scr_r1", [128, NF, T], BF16, "Internal"),
        "scr_s2": dram("scr_s2", [128, NF, T], BF16, "Internal"),
        "xout": dram("xout", [128, NF, T], BF16, "ExternalOutput"),
    }
    with ExitStack() as stack:
        p = Prog(nc, stack)
        emit_mamba(p, io, T=T, Tc_a=min(512, T), Tc_b=min(1024, T), Tc_d=min(256, T))
        p.finalize(strict=STRICT)
    return nc





def emit_moe(p, io, T=T_FULL, Tc=512):
    nc = p.nc
    NE, DE = 6, 512
    GC1 = 0.0713548162726, 1.5957691216057308

    with ExitStack() as st:
        sb = lambda n, s, d: st.enter_context(nc.sbuf_tensor(p.uname(n), s, d))
        ps = lambda n, s, d=F32: st.enter_context(nc.psum_tensor(p.uname(n), s, d))

        ones_bf = sb("m_ones_bf", [128, 1], BF16)
        ones_f32 = sb("m_ones_f32", [128, 1], F32)
        ones1_f32 = sb("m_ones1_f32", [1, 128], F32)
        p.v(lambda e: e.memset(ones_bf[:], 1.0))
        p.v(lambda e: e.memset(ones_f32[:], 1.0))
        p.v(lambda e: e.memset(ones1_f32[:], 1.0))
        rows = (sb("m_r0", [1, Tc], F32), sb("m_r1", [1, Tc], F32),
                sb("m_r2", [1, Tc], F32), sb("m_r3", [1, Tc], F32))

        w1 = sb("m_w1", [128, NE, NF, DE], BF16)
        w2 = sb("m_w2", [128, NE, 4, D], BF16)
        for e_ in range(NE):
            p.gdma(w1[:, e_, :, :], io["moe_w1"][:, e_, :, :])
            p.gdma(w2[:, e_, :, :], io["moe_w2"][:, e_, :, :])
        projT = sb("m_projT", [128, NF, NE], BF16)
        p.gdma(projT[:], io["moe_projT"])
        anchT = sb("m_anchT", [NE, NE], F32)
        p.dma(anchT[:], io["moe_anchT"])
        SEL6 = sb("m_SEL6", [NE, NE * 128], BF16)
        p.gdma(SEL6[:], io["SEL6"])
        identf = sb("m_identf", [128, 128], F32)
        p.dma(identf[:], io["IDENTF"])
        temp1 = sb("m_temp1", [1, 1], F32)
        p.dma(temp1[:], io["moe_temp"])
        rthc = sb("m_rthc", [128, 1], F32)
        zeros6 = sb("m_zeros6", [128, NE], F32)
        p.v(lambda e: e.memset(zeros6[:], 0.0))

        p.v(lambda e: e.tensor_single_scalar(temp1[:], temp1[:], 0.1, ALU.max))
        p.v(lambda e: e.reciprocal(temp1[:], temp1[:]))
        p.v(lambda e: e.tensor_scalar_mul(temp1[:], temp1[:], 0.5))
        p_tr = ps("m_p_tr", [128, 128])
        p.t(lambda e: e.matmul(p_tr[:, 0:1], ones1_f32[:], temp1[:], start=True, stop=True))
        p.v(lambda e: e.tensor_copy(rthc[:], p_tr[:, 0:1]))

        x_c = sb("m_x", [128, NF, Tc], BF16)
        qrow = sb("m_qrow", [NE, Tc], F32)
        dotrow = sb("m_dotrow", [NE, Tc], F32)
        dotT = sb("m_dotT", [128, ceil_div(Tc, 128), NE], F32)
        eqa = sb("m_eqa", [128, NE], F32)
        eqb = sb("m_eqb", [128, NE], F32)
        pre = sb("m_pre", [128, NE], F32)
        first1 = sb("m_first1", [128, NE], F32)
        first2 = sb("m_first2", [128, NE], F32)
        m1c = sb("m_m1c", [128, 1], F32)
        m2c = sb("m_m2c", [128, 1], F32)
        wac = sb("m_wac", [128, 1], F32)
        wbc = sb("m_wbc", [128, 1], F32)
        ewT = sb("m_ewT", [128, ceil_div(Tc, 128), NE], F32)
        ew_bf = sb("m_ewbf", [NE, Tc], BF16)
        ewb = [sb("m_ewb", [128, Tc], BF16) for _ in range(2)]
        g1 = [sb("m_g1", [128, Tc], F32) for _ in range(2)]
        g2 = [sb("m_g2", [128, Tc], F32) for _ in range(2)]
        g3 = [sb("m_g3", [128, Tc], F32) for _ in range(2)]
        u_all = sb("m_u", [128, NE, 4, Tc], BF16)
        upre = sb("m_upre", [128, NF, Tc], F32)
        xo = sb("m_xo", [128, NF, Tc], BF16)
        xsq = [sb("m_xsq", [128, Tc], F32) for _ in range(2)]
        lnt = [sb("m_lnt", [128, Tc], F32) for _ in range(2)]

        p_z = ps("m_p_z", [NE, Tc])
        p_dot = ps("m_p_dot", [NE, Tc])
        p_h = [ps("m_p_h", [128, Tc]) for _ in range(2)]
        p_ew = ps("m_p_ew", [128, Tc])
        p_out = [ps("m_p_out", [128, Tc]) for _ in range(2)]
        p_b1 = p_h[0]
        p_b2 = p_h[1]
        p_sum = p_z[0:1, :]
        p_sq = p_dot[0:1, :]

        ntt = Tc // 128
        for c in range(T // Tc):
            t0 = c * Tc
            p.dma(x_c[:], io["xin"][:, :, t0:t0 + Tc])
            for kt in range(NF):
                p.t(lambda e, kt=kt: e.matmul(p_z[:], projT[:, kt, :], x_c[:, kt, :],
                                              start=(kt == 0), stop=(kt == NF - 1)))
            p.s(lambda e: e.activation(qrow[:], p_z[:], AF.Tanh, scale=np.float32(1.0 / 0.3)))
            p.t(lambda e: e.matmul(p_dot[:], anchT[:], qrow[:], start=True, stop=True))
            p.v(lambda e: e.tensor_copy(dotrow[:], p_dot[:]))
            for tt in range(ntt):
                p.t(lambda e, tt=tt: e.transpose(p_tr[:, 0:NE], dotrow[0:NE, tt * 128:(tt + 1) * 128],
                                                 identf[0:NE, 0:NE]))
                p.v(lambda e, tt=tt: e.tensor_copy(dotT[:, tt, :], p_tr[:, 0:NE]))
            for tt in range(ntt):
                d6 = dotT[:, tt, :]
                p.v(lambda e, d6=d6: e.reduce_max(m1c[:], d6, AX.X))
                p.v(lambda e, d6=d6: e.tensor_scalar(eqa[:], d6, m1c[:], None, ALU.is_ge))
                p.v(lambda e: e.tensor_tensor_scan(pre[:], eqa[:], zeros6[:], 0.0, ALU.add, ALU.add))
                p.v(lambda e: e.tensor_single_scalar(pre[:], pre[:], 1.0, ALU.is_equal))
                p.v(lambda e: e.tensor_mul(first1[:], eqa[:], pre[:]))
                p.v(lambda e, d6=d6: e.scalar_tensor_tensor(eqb[:], first1[:], -1e30, d6,
                                                            ALU.mult, ALU.add))
                p.v(lambda e: e.reduce_max(m2c[:], eqb[:], AX.X))
                p.v(lambda e: e.tensor_scalar(eqb[:], eqb[:], m2c[:], None, ALU.is_ge))
                p.v(lambda e: e.tensor_tensor_scan(pre[:], eqb[:], zeros6[:], 0.0, ALU.add, ALU.add))
                p.v(lambda e: e.tensor_single_scalar(pre[:], pre[:], 1.0, ALU.is_equal))
                p.v(lambda e: e.tensor_mul(first2[:], eqb[:], pre[:]))
                p.v(lambda e: e.tensor_sub(m1c[:], m1c[:], m2c[:]))
                p.s(lambda e: e.activation(wac[:], m1c[:], AF.Sigmoid, scale=rthc[:]))
                p.v(lambda e: e.tensor_scalar(wbc[:], wac[:], -1.0, 1.0, ALU.mult, ALU.add))
                p.v(lambda e: e.tensor_scalar(first1[:], first1[:], wac[:], None, ALU.mult))
                p.v(lambda e, tt=tt: e.scalar_tensor_tensor(ewT[:, tt, :], first2[:], wbc[:],
                                                            first1[:], ALU.mult, ALU.add))
            for tt in range(ntt):
                p.t(lambda e, tt=tt: e.transpose(p_tr[0:NE, 0:128], ewT[:, tt, :], identf[:]))
                p.v(lambda e, tt=tt: e.tensor_copy(ew_bf[:, tt * 128:(tt + 1) * 128],
                                                   p_tr[0:NE, 0:128]))
            for e_ in range(NE):
                eb = ewb[e_ % 2]
                p.t(lambda e, e_=e_: e.matmul(p_ew[:], SEL6[:, e_ * 128:(e_ + 1) * 128],
                                              ew_bf[:], start=True, stop=True))
                p.v(lambda e, eb=eb: e.tensor_copy(eb[:], p_ew[:]))
                for jt in range(4):
                    par = (e_ * 4 + jt) % 2
                    ph, a1, a2, a3 = p_h[par], g1[par], g2[par], g3[par]
                    for kt in range(NF):
                        p.t(lambda e, e_=e_, jt=jt, kt=kt, ph=ph: e.matmul(
                            ph[:], w1[:, e_, kt, jt * 128:(jt + 1) * 128], x_c[:, kt, :],
                            start=(kt == 0), stop=(kt == NF - 1)))
                    p.s(lambda e, ph=ph, a1=a1: e.activation(a1[:], ph[:], AF.Square))
                    p.v(lambda e, a1=a1: e.tensor_scalar(a1[:], a1[:], GC1[0], GC1[1], ALU.mult, ALU.add))
                    p.v(lambda e, ph=ph, a1=a1, a2=a2: e.tensor_tensor(a2[:], a1[:], ph[:], ALU.mult))
                    p.s(lambda e, a2=a2: e.activation(a2[:], a2[:], AF.Sigmoid))
                    p.v(lambda e, ph=ph, a2=a2, a3=a3: e.tensor_tensor(a3[:], a2[:], ph[:], ALU.mult))
                    p.op("gpsimd", lambda e, e_=e_, jt=jt, a3=a3, eb=eb: e.tensor_tensor(
                        u_all[:, e_, jt, :], a3[:], eb[:], ALU.mult))
            for mt in range(NF):
                po = p_out[mt % 2]
                first = True
                for e_ in range(NE):
                    for jt in range(4):
                        p.t(lambda e, e_=e_, jt=jt, mt=mt, first=first, po=po: e.matmul(
                            po[:], w2[:, e_, jt, mt * 128:(mt + 1) * 128], u_all[:, e_, jt, :],
                            start=first, stop=(e_ == NE - 1 and jt == 3)))
                        first = False
                p.v(lambda e, mt=mt, po=po: e.tensor_tensor(upre[:, mt, :], x_c[:, mt, :], po[:],
                                                     ALU.add))
            for ft in range(NF):
                p.t(lambda e, ft=ft: e.matmul(p_sum, ones_f32[:], upre[:, ft, :],
                                              start=(ft == 0), stop=(ft == NF - 1)))
            for ft in range(NF):
                p.s(lambda e, ft=ft: e.activation(xsq[ft % 2][:], upre[:, ft, :], AF.Square))
                p.t(lambda e, ft=ft: e.matmul(p_sq, ones_f32[:], xsq[ft % 2][:],
                                              start=(ft == 0), stop=(ft == NF - 1)))
            _ln_rows(p, p_sum, p_sq, Tc, 1e-5, rows)
            p.t(lambda e: e.matmul(p_b1[:], ones1_f32[:], rows[1][0:1, 0:Tc], start=True, stop=True))
            p.t(lambda e: e.matmul(p_b2[:], ones1_f32[:], rows[2][0:1, 0:Tc], start=True, stop=True))
            for ft in range(NF):
                lt = lnt[ft % 2]
                p.v(lambda e, ft=ft, lt=lt: e.tensor_tensor(lt[:], upre[:, ft, :], p_b1[:], ALU.mult))
                p.v(lambda e, ft=ft, lt=lt: e.tensor_tensor(xo[:, ft, :], lt[:], p_b2[:], ALU.add))
            p.dma(io["xout"][:, :, t0:t0 + Tc], xo[:, :, :])


def prep_moe_inputs(w):
    c = np.ascontiguousarray
    SEL6 = np.zeros((6, 6 * 128), np.float32)
    for n in range(6):
        SEL6[n, n * 128:(n + 1) * 128] = 1.0
    return {
        "moe_w1": c(w["L0_ffn_w1"].reshape(6, 8, 128, 512).transpose(2, 0, 1, 3)),
        "moe_w2": c(w["L0_ffn_w2"].reshape(6, 4, 128, 1024).transpose(2, 0, 1, 3)),
        "moe_projT": c(w["L0_ffn_proj_w"].T.reshape(8, 128, 6).transpose(1, 0, 2)),
        "moe_anchT": c(w["L0_ffn_anchors"].T),
        "moe_temp": np.asarray(w["L0_ffn_routing_temp"], np.float32).reshape(1, 1),
        "SEL6": SEL6,
        "IDENTF": np.eye(128, dtype=np.float32),
    }


def build_moe_program(T=T_FULL):
    nc = bass.Bass("TRN2", target_bir_lowering=False)
    dram = lambda n, s, dt, k: nc.dram_tensor(n, s, dt, kind=k).ap()
    io = {
        "xin": dram("xin", [128, NF, T], BF16, "ExternalInput"),
        "moe_w1": dram("moe_w1", [128, 6, NF, 512], F32, "ExternalInput"),
        "moe_w2": dram("moe_w2", [128, 6, 4, D], F32, "ExternalInput"),
        "moe_projT": dram("moe_projT", [128, NF, 6], F32, "ExternalInput"),
        "moe_anchT": dram("moe_anchT", [6, 6], F32, "ExternalInput"),
        "moe_temp": dram("moe_temp", [1, 1], F32, "ExternalInput"),
        "SEL6": dram("SEL6", [6, 768], F32, "ExternalInput"),
        "IDENTF": dram("IDENTF", [128, 128], F32, "ExternalInput"),
        "xout": dram("xout", [128, NF, T], BF16, "ExternalOutput"),
    }
    with ExitStack() as stack:
        p = Prog(nc, stack)
        emit_moe(p, io, T=T, Tc=min(512, T))
        p.finalize(strict=STRICT)
    return nc




def emit_attn(p, io, T=T_FULL, Tc=512):
    nc = p.nc
    NH, DH = 16, 64
    nq = T // Tc

    with ExitStack() as st:
        sb = lambda n, s, d: st.enter_context(nc.sbuf_tensor(p.uname(n), s, d))
        ps = lambda n, s, d=F32: st.enter_context(nc.psum_tensor(p.uname(n), s, d))

        ones_f32 = sb("a_ones_f32", [128, 1], F32)
        ones1_f32 = sb("a_ones1_f32", [1, 128], F32)
        p.v(lambda e: e.memset(ones_f32[:], 1.0))
        p.v(lambda e: e.memset(ones1_f32[:], 1.0))
        rows = (sb("a_r0", [1, Tc], F32), sb("a_r1", [1, Tc], F32),
                sb("a_r2", [1, Tc], F32), sb("a_r3", [1, Tc], F32))

        x_res = sb("a_x", [128, NF, T], BF16)
        p.dma(x_res[:], io["xin"][:, :, :])
        qwT = sb("a_qwT", [128, NF, D], BF16)
        kwT = sb("a_kwT", [128, NF, D], BF16)
        vwT = sb("a_vwT", [128, NF, D], BF16)
        for kt in range(NF):
            p.gdma(qwT[:, kt, :], io["qwT"][kt * 128:(kt + 1) * 128, :])
            p.gdma(kwT[:, kt, :], io["kwT"][kt * 128:(kt + 1) * 128, :])
            p.gdma(vwT[:, kt, :], io["vwT"][kt * 128:(kt + 1) * 128, :])
        qb_c = sb("a_qb", [64, NH], F32)
        kb_c = sb("a_kb", [64, NH], F32)
        p.dma(qb_c[:], io["qb_c"])
        p.dma(kb_c[:], io["kb_c"])
        vb_rep = sb("a_vbrep", [128, NH, DH], BF16)
        p.gdma(vb_rep[:], io["vb_rep"])

        sh = ExitStack()
        psh = lambda n, s_, d=F32: sh.enter_context(nc.psum_tensor(p.uname(n), s_, d))
        sbh = lambda n, s_, d: sh.enter_context(nc.sbuf_tensor(p.uname(n), s_, d))
        Q_h = sbh("a_Q", [64, T], BF16)
        K_h = sbh("a_K", [64, T], BF16)
        V_h = sbh("a_V", [128, T // 128, DH + 1], BF16)
        qsq = [sbh("a_qsq", [64, Tc], F32) for _ in range(2)]
        mrow = sbh("a_mrow", [1, 2 * (T // Tc)], F32)
        bscal = sbh("a_bscal", [1, 2], F32)
        negb = sbh("a_negb", [128, 1], F32)
        P_T = [sbh("a_PT", [128, Tc], BF16) for _ in range(2)]
        rb = [sbh("a_rb", [64, Tc], F32) for _ in range(2)]
        ctx_bf = [sbh("a_ctx", [64, Tc], BF16) for _ in range(2)]
        rrec = [sbh("a_rrec", [1, Tc], F32) for _ in range(2)]
        p_kq = [psh("a_p_kq", [64, Tc]) for _ in range(2)]
        p_v2t = psh("a_p_v", [128, 2, DH])
        p_v = [p_v2t[:, 0, :], p_v2t[:, 1, :]]
        p_n = psh("a_p_n", [1, Tc])
        p_s = [psh("a_p_s", [128, Tc]) for _ in range(2)]
        p_ctx = [psh("a_p_ctx", [65, Tc]) for _ in range(2)]

        p.v(lambda e: e.memset(V_h[:, :, DH:DH + 1], 1.0))

        nchunk = T // Tc
        for h in range(NH):
            hc = slice(h * DH, (h + 1) * DH)
            for c in range(nchunk):
                t0 = c * Tc
                pk, pq = p_kq[0], p_kq[1]
                for kt in range(NF):
                    p.t(lambda e, kt=kt, t0=t0, hc=hc, pk=pk: e.matmul(
                        pk[:], kwT[:, kt, hc], x_res[:, kt, t0:t0 + Tc],
                        start=(kt == 0), stop=(kt == NF - 1)))
                p.v(lambda e, t0=t0, h=h, pk=pk: e.tensor_scalar(
                    K_h[:, t0:t0 + Tc], pk[:], kb_c[:, h:h + 1], None, ALU.add))
                for kt in range(NF):
                    p.t(lambda e, kt=kt, t0=t0, hc=hc, pq=pq: e.matmul(
                        pq[:], qwT[:, kt, hc], x_res[:, kt, t0:t0 + Tc],
                        start=(kt == 0), stop=(kt == NF - 1)))
                p.v(lambda e, t0=t0, h=h, pq=pq: e.tensor_scalar(
                    Q_h[:, t0:t0 + Tc], pq[:], qb_c[:, h:h + 1], None, ALU.add))
            for tt in range(T // 128):
                pv = p_v[tt % 2]
                for kt in range(NF):
                    p.t(lambda e, kt=kt, tt=tt, hc=hc, pv=pv: e.matmul(
                        pv[:], x_res[:, kt, tt * 128:(tt + 1) * 128], vwT[:, kt, hc],
                        start=(kt == 0), stop=(kt == NF - 1)))
                p.v(lambda e, tt=tt, h=h, pv=pv: e.tensor_tensor(
                    V_h[:, tt, 0:DH], pv[:], vb_rep[:, h, :], ALU.add))
            for (src, off) in ((Q_h, 0), (K_h, nchunk)):
                for c in range(nchunk):
                    t0 = c * Tc
                    qs_ = qsq[c % 2]
                    p.s(lambda e, src=src, t0=t0, qs_=qs_: e.activation(qs_[:], src[:, t0:t0 + Tc], AF.Square))
                    p.t(lambda e, qs_=qs_: e.matmul(p_n[:], ones_f32[0:64, :], qs_[:], start=True, stop=True))
                    p.v(lambda e, c=c, off=off: e.reduce_max(mrow[0:1, off + c:off + c + 1],
                                                             p_n[:], AX.X))
            p.v(lambda e: e.reduce_max(bscal[0:1, 0:1], mrow[0:1, 0:nchunk], AX.X))
            p.v(lambda e: e.reduce_max(bscal[0:1, 1:2], mrow[0:1, nchunk:2 * nchunk], AX.X))
            p.v(lambda e: e.tensor_mul(bscal[0:1, 0:1], bscal[0:1, 0:1], bscal[0:1, 1:2]))
            p.s(lambda e: e.activation(bscal[0:1, 0:1], bscal[0:1, 0:1], AF.Ln))
            p.s(lambda e: e.activation(bscal[0:1, 0:1], bscal[0:1, 0:1], AF.Exp, scale=np.float32(0.5)))
            p.v(lambda e: e.tensor_scalar_mul(bscal[0:1, 0:1], bscal[0:1, 0:1], -0.125))
            p.t(lambda e: e.matmul(p_s[0][:, 0:1], ones1_f32[:], bscal[0:1, 0:1], start=True, stop=True))
            p.v(lambda e: e.tensor_copy(negb[:], p_s[0][:, 0:1]))
            for c in range(nchunk):
                t0 = c * Tc
                pc = p_ctx[c % 2]
                nkt = T // 128
                pend = None
                for kt in range(nkt):
                    ps_, pt_ = p_s[kt % 2], P_T[kt % 2]
                    p.t(lambda e, kt=kt, t0=t0, ps_=ps_: e.matmul(
                        ps_[:], K_h[:, kt * 128:(kt + 1) * 128], Q_h[:, t0:t0 + Tc],
                        start=True, stop=True))
                    p.s(lambda e, ps_=ps_, pt_=pt_: e.activation(pt_[:], ps_[:], AF.Exp,
                                               scale=np.float32(0.125), bias=negb[:]))
                    if pend is not None:
                        jkt, jpt = pend
                        p.t(lambda e, jkt=jkt, jpt=jpt, pc=pc: e.matmul(
                            pc[:], V_h[:, jkt, :], jpt[:],
                            start=(jkt == 0), stop=False))
                    pend = (kt, pt_)
                jkt, jpt = pend
                p.t(lambda e, jkt=jkt, jpt=jpt, pc=pc: e.matmul(
                    pc[:], V_h[:, jkt, :], jpt[:], start=(jkt == 0), stop=True))
                rr = rrec[c % 2]
                p.s(lambda e, pc=pc, rr=rr: e.activation(rr[0:1, 0:Tc], pc[64:65, :], AF.Ln))
                p.s(lambda e, rr=rr: e.activation(rr[0:1, 0:Tc], rr[0:1, 0:Tc], AF.Exp,
                                                  scale=np.float32(-1.0)))
                pk2 = p_kq[c % 2]
                p.t(lambda e, rr=rr, pk2=pk2: e.matmul(pk2[:], ones1_f32[0:1, 0:64], rr[0:1, 0:Tc],
                                       start=True, stop=True))
                rbc = rb[c % 2]
                p.v(lambda e, rbc=rbc, pk2=pk2: e.tensor_copy(rbc[:], pk2[:]))
                cbc = ctx_bf[c % 2]
                p.v(lambda e, pc=pc, rbc=rbc, cbc=cbc: e.tensor_tensor(cbc[:], pc[0:64, :], rbc[:], ALU.mult))
                p.dma(io["scr_ctx"][(h % 2) * 64:(h % 2) * 64 + 64, h // 2, t0:t0 + Tc], cbc[:])

        sh.close()
        with ExitStack() as st2:
            sb2 = lambda n, s, d: st2.enter_context(nc.sbuf_tensor(p.uname(n), s, d))
            outT = sb2("a_outT", [128, NF, D], BF16)
            for kt in range(NF):
                p.gdma(outT[:, kt, :], io["outwT"][kt * 128:(kt + 1) * 128, :])
            outb = sb2("a_outb", [128, NF], F32)
            p.dma(outb[:], io["outb_c"])
            ctx_c = sb2("a_ctxc", [128, NF, Tc], BF16)
            upre = sb2("a_upre", [128, NF, Tc], F32)
            xo = sb2("a_xo", [128, NF, Tc], BF16)
            xsq2 = [sb2("a_xsq2", [128, Tc], F32) for _ in range(2)]
            lnt = [sb2("a_lnt", [128, Tc], F32) for _ in range(2)]
            ps2 = lambda n, s_, d=F32: st2.enter_context(nc.psum_tensor(p.uname(n), s_, d))
            p_o = [ps2("a_p_o", [128, Tc]) for _ in range(2)]
            p_sum = ps2("a_p_sum2", [1, Tc])
            p_sq = ps2("a_p_sq2", [1, Tc])
            p_b1f = ps2("a_p_b1f", [128, Tc])
            p_b2f = ps2("a_p_b2f", [128, Tc])
            for c in range(nq):
                t0 = c * Tc
                p.dma(ctx_c[:], io["scr_ctx"][:, :, t0:t0 + Tc])
                for mt in range(NF):
                    po = p_o[mt % 2]
                    for kt in range(NF):
                        p.t(lambda e, mt=mt, kt=kt, po=po: e.matmul(
                            po[:], outT[:, kt, mt * 128:(mt + 1) * 128], ctx_c[:, kt, :],
                            start=(kt == 0), stop=(kt == NF - 1)))
                    p.v(lambda e, mt=mt, t0=t0, po=po: e.scalar_tensor_tensor(
                        upre[:, mt, :], po[:], outb[:, mt:mt + 1], x_res[:, mt, t0:t0 + Tc],
                        ALU.add, ALU.add))
                for ft in range(NF):
                    p.t(lambda e, ft=ft: e.matmul(p_sum[:], ones_f32[:], upre[:, ft, :],
                                                  start=(ft == 0), stop=(ft == NF - 1)))
                for ft in range(NF):
                    p.s(lambda e, ft=ft: e.activation(xsq2[ft % 2][:], upre[:, ft, :], AF.Square))
                    p.t(lambda e, ft=ft: e.matmul(p_sq[:], ones_f32[:], xsq2[ft % 2][:],
                                                  start=(ft == 0), stop=(ft == NF - 1)))
                _ln_rows(p, p_sum[:], p_sq[:], Tc, 1e-5, rows)
                p.t(lambda e: e.matmul(p_b1f[:], ones1_f32[:], rows[1][0:1, 0:Tc], start=True, stop=True))
                p.t(lambda e: e.matmul(p_b2f[:], ones1_f32[:], rows[2][0:1, 0:Tc], start=True, stop=True))
                for ft in range(NF):
                    lt = lnt[ft % 2]
                    p.v(lambda e, ft=ft, lt=lt: e.tensor_tensor(lt[:], upre[:, ft, :], p_b1f[:], ALU.mult))
                    p.v(lambda e, ft=ft, lt=lt: e.tensor_tensor(xo[:, ft, :], lt[:], p_b2f[:], ALU.add))
                p.dma(io["xout"][:, :, t0:t0 + Tc], xo[:, :, :])


def prep_attn_inputs(w):
    c = np.ascontiguousarray
    qb = w["L1_attn_q_b"]
    kb = w["L1_attn_k_b"]
    vb = w["L1_attn_v_b"]
    return {
        "qwT": c(w["L1_attn_q_w"].T),
        "kwT": c(w["L1_attn_k_w"].T),
        "vwT": c(w["L1_attn_v_w"].T),
        "qb_c": c(qb.reshape(16, 64).T),
        "kb_c": c(kb.reshape(16, 64).T),
        "vb_rep": c(np.broadcast_to(vb.reshape(1, 16, 64), (128, 16, 64))),
        "outwT": c(w["L1_attn_out_w"].T),
        "outb_c": c(w["L1_attn_out_b"].reshape(8, 128).T),
    }


def build_attn_program(T=T_FULL):
    nc = bass.Bass("TRN2", target_bir_lowering=False)
    dram = lambda n, s, dt, k: nc.dram_tensor(n, s, dt, kind=k).ap()
    io = {
        "xin": dram("xin", [128, NF, T], BF16, "ExternalInput"),
        "qwT": dram("qwT", [D, D], F32, "ExternalInput"),
        "kwT": dram("kwT", [D, D], F32, "ExternalInput"),
        "vwT": dram("vwT", [D, D], F32, "ExternalInput"),
        "qb_c": dram("qb_c", [64, 16], F32, "ExternalInput"),
        "kb_c": dram("kb_c", [64, 16], F32, "ExternalInput"),
        "vb_rep": dram("vb_rep", [128, 16, 64], F32, "ExternalInput"),
        "outwT": dram("outwT", [D, D], F32, "ExternalInput"),
        "outb_c": dram("outb_c", [128, NF], F32, "ExternalInput"),
        "scr_ctx": dram("scr_ctx", [128, NF, T], BF16, "Internal"),
        "xout": dram("xout", [128, NF, T], BF16, "ExternalOutput"),
    }
    with ExitStack() as stack:
        p = Prog(nc, stack)
        emit_attn(p, io, T=T, Tc=min(512, T))
        p.finalize(strict=STRICT)
    return nc




def emit_l1ffn_layer(p, io, li, T=T_FULL, Tc=512, Tcf=512):
    nc = p.nc
    NH, DH, W = 16, 64, 256
    nw = T // W
    nwp = nw // 2
    key = lambda nm: io[f"l{li}_{nm}"]

    with ExitStack() as st:
        sb = lambda n, s, d: st.enter_context(nc.sbuf_tensor(p.uname(n), s, d))
        ps = lambda n, s, d=F32: st.enter_context(nc.psum_tensor(p.uname(n), s, d))
        ones_f32 = sb("f_ones_f32", [128, 1], F32)
        ones_bf = sb("f_ones_bf", [128, 1], BF16)
        ones1_f32 = sb("f_ones1_f32", [1, 128], F32)
        p.v(lambda e: e.memset(ones_f32[:], 1.0))
        p.v(lambda e: e.memset(ones_bf[:], 1.0))
        p.v(lambda e: e.memset(ones1_f32[:], 1.0))
        rows = (sb("f_r0", [1, max(Tc, Tcf)], F32), sb("f_r1", [1, max(Tc, Tcf)], F32),
                sb("f_r2", [1, max(Tc, Tcf)], F32), sb("f_r3", [1, max(Tc, Tcf)], F32))

        sx = ExitStack()
        x = sx.enter_context(nc.sbuf_tensor(p.uname("f_x"), [128, NF, T + 1], BF16))
        p.v(lambda e: e.memset(x[:, :, 0:1], 0.0))
        p.dma(x[:, :, 1:T + 1], io["xin"][:, :, :])
        xs = [x[:, kt, 1:T + 1] if kt < 4 else x[:, kt, 0:T] for kt in range(NF)]

        with ExitStack() as sa:
            sbA = lambda n, s, d: sa.enter_context(nc.sbuf_tensor(p.uname(n), s, d))
            psA = lambda n, s, d=F32: sa.enter_context(nc.psum_tensor(p.uname(n), s, d))

            wqkv = sbA("f_wqkv", [128, NF, 3 * D], BF16)
            for kt in range(NF):
                p.gdma(wqkv[:, kt, :], key("WqkvT")[kt * 128:(kt + 1) * 128, :])
            CS2 = sbA("f_CS2", [128, T], BF16)
            SS2 = sbA("f_SS2", [128, T], BF16)
            CI2 = sbA("f_CI2", [128, T], BF16)
            SI2 = sbA("f_SI2", [128, T], BF16)
            p.gdma(CS2[:], io["CS2"])
            p.gdma(SS2[:], io["SS2"])
            p.gdma(CI2[:], io["CI2"])
            p.gdma(SI2[:], io["SI2"])
            MASK6 = sbA("f_MASK6", [128, 6, 2 * W], BF16)
            p.gdma(MASK6[:], io["MASK6"])
            HONES = sbA("f_HONES", [128, 2], BF16)
            p.gdma(HONES[:], io["HONES"])
            SEL2 = sbA("f_SEL2", [2, 128], F32)
            p.dma(SEL2[:], io["SEL2"])
            qs2 = sbA("f_qs2", [128, 1], F32)
            ks2 = sbA("f_ks2", [128, 1], F32)
            p.dma(qs2[:], key("qs2"))
            p.dma(ks2[:], key("ks2"))
            qs_row = sbA("f_qs_row", [1, DH], BF16)
            ks_row = sbA("f_ks_row", [1, DH], BF16)
            p.gdma(qs_row[:], key("qs_row"))
            p.gdma(ks_row[:], key("ks_row"))
            negb = sbA("f_negb", [128, 1], F32)
            bs = sbA("f_bs", [1, 2], F32)
            p_qk1 = psA("f_p_qk", [128, Tc])
            p_qk = [p_qk1, p_qk1]
            p_rb2 = psA("f_p_rb2", [128, Tc])
            p_v2 = psA("f_p_v2", [128, 256])
            p_sl = [psA("f_p_sl", [128, 2 * W]) for _ in range(3)]
            p_cl = [psA("f_p_cl", [65, 2 * W]) for _ in range(2)]
            p_n2 = p_rb2[0:2, :]
            p_rbl = p_rb2[0:64, :]
            p_misc = p_sl[0]

            p.v(lambda e: e.reduce_max(bs[0:1, 0:1], qs_row[:], AX.X, apply_absolute_value=True))
            p.v(lambda e: e.reduce_max(bs[0:1, 1:2], ks_row[:], AX.X, apply_absolute_value=True))
            p.v(lambda e: e.tensor_mul(bs[0:1, 0:1], bs[0:1, 0:1], bs[0:1, 1:2]))
            p.v(lambda e: e.tensor_scalar_mul(bs[0:1, 0:1], bs[0:1, 0:1], -8.0))
            p.t(lambda e: e.matmul(p_misc[:, 0:1], ones1_f32[:], bs[0:1, 0:1], start=True, stop=True))
            p.v(lambda e: e.tensor_copy(negb[:], p_misc[:, 0:1]))

            qraw = [sbA("f_qraw", [128, Tc], BF16) for _ in range(2)]
            qn = [sbA("f_qn", [128, Tc], BF16) for _ in range(2)]
            rotq = [sbA("f_rotq", [128, Tc], BF16) for _ in range(2)]
            qp = sbA("f_qp", [128, T], BF16)
            kp = sbA("f_kp", [128, T], BF16)
            V2 = sbA("f_V2", [128, T // 128, 260], BF16)
            sqf_s = sbA("f_sqf", [128, Tc], BF16)
            sqf = [sqf_s, sqf_s]
            rrow_s = sbA("f_rrow", [2, Tc], F32)
            rrow = [rrow_s, rrow_s]

            P_l = [sbA("f_P", [128, 2 * W], BF16) for _ in range(3)]
            rbl_s = sbA("f_rbl", [64, 2 * W], F32)
            rbl = [rbl_s, rbl_s]
            den_s = rows[3][0:1, 0:2 * W]
            den = [den_s, den_s]
            ctxl = [sbA("f_ctxl", [64, 2 * W], BF16) for _ in range(2)]

            for sl4 in range(4):
                p.v(lambda e, sl4=sl4: e.memset(V2[:, :, sl4 * 65 + 64:sl4 * 65 + 65], 1.0))

            nchunk = T // Tc
            for hp in range(NH // 2):
                h0 = 2 * hp
                if hp % 2 == 0:
                    for tt in range(T // 128):
                        for kt in range(NF):
                            p.t(lambda e, kt=kt, tt=tt, h0=h0: e.matmul(
                                p_v2[:], xs[kt][:, tt * 128:(tt + 1) * 128],
                                wqkv[:, kt, 2 * D + h0 * DH:2 * D + h0 * DH + 256],
                                start=(kt == 0), stop=(kt == NF - 1)))
                        for sl4 in range(4):
                            p.v(lambda e, tt=tt, sl4=sl4: e.tensor_copy(
                                V2[:, tt, sl4 * 65:sl4 * 65 + 64],
                                p_v2[:, sl4 * 64:sl4 * 64 + 64]))
                for (base, dstp, scol, CT, ST) in (
                        (0, qp, qs2, CS2, SS2), (D, kp, ks2, CI2, SI2)):
                    cc = slice(base + h0 * DH, base + h0 * DH + 128)
                    raw = qraw
                    for c in range(nchunk):
                        t0 = c * Tc
                        sl = slice(t0, t0 + Tc)
                        pqk, rw_, sq_, rr_, qn_, rq_ = (p_qk[c % 2], raw[c % 2],
                            sqf[c % 2], rrow[c % 2], qn[c % 2], rotq[c % 2])
                        for kt in range(NF):
                            p.t(lambda e, kt=kt, t0=t0, cc=cc, pqk=pqk: e.matmul(
                                pqk[:], wqkv[:, kt, cc], xs[kt][:, t0:t0 + Tc],
                                start=(kt == 0), stop=(kt == NF - 1)))
                        p.v(lambda e, rw_=rw_, pqk=pqk: e.tensor_copy(rw_[:], pqk[:]))
                        p.s(lambda e, rw_=rw_, sq_=sq_: e.activation(sq_[:], rw_[:], AF.Square))
                        p.t(lambda e, sq_=sq_: e.matmul(p_n2[:], HONES[:], sq_[:], start=True, stop=True))
                        p.v(lambda e, rr_=rr_: e.tensor_single_scalar(rr_[:], p_n2[:], 1e-24, ALU.max))
                        p.s(lambda e, rr_=rr_: e.activation(rr_[:], rr_[:], AF.Ln))
                        p.s(lambda e, rr_=rr_: e.activation(rr_[:], rr_[:], AF.Exp, scale=np.float32(-0.5)))
                        p.t(lambda e, rr_=rr_: e.matmul(p_rb2[:], SEL2[:], rr_[:],
                                               start=True, stop=True))
                        p.v(lambda e, rw_=rw_, qn_=qn_, scol=scol: e.scalar_tensor_tensor(
                            qn_[:], rw_[:], scol[:], p_rb2[:], ALU.mult, ALU.mult))
                        p.v(lambda e, qn_=qn_, rq_=rq_: e.tensor_scalar_mul(rq_[0:32, :], qn_[32:64, :], -1.0))
                        p.v(lambda e, qn_=qn_, rq_=rq_: e.tensor_copy(rq_[32:64, :], qn_[0:32, :]))
                        p.v(lambda e, qn_=qn_, rq_=rq_: e.tensor_scalar_mul(rq_[64:96, :], qn_[96:128, :], -1.0))
                        p.v(lambda e, qn_=qn_, rq_=rq_: e.tensor_copy(rq_[96:128, :], qn_[64:96, :]))
                        p.op("gpsimd", lambda e, dstp=dstp, CT=CT, sl=sl, qn_=qn_: e.tensor_tensor(
                            dstp[:, sl], qn_[:], CT[:, sl], ALU.mult))
                        p.v(lambda e, ST=ST, sl=sl, rq_=rq_: e.tensor_tensor(rq_[:], rq_[:], ST[:, sl], ALU.mult))
                        p.v(lambda e, dstp=dstp, sl=sl, rq_=rq_: e.tensor_tensor(
                            dstp[:, sl], dstp[:, sl], rq_[:], ALU.add))
                for hh in range(2):
                    h = h0 + hh
                    hs = slice(hh * 64, hh * 64 + 64)
                    vs = slice((hp % 2) * 130 + hh * 65, (hp % 2) * 130 + hh * 65 + 65)
                    for wp in range(nwp):
                        q0 = wp * 2 * W
                        kts = range(2, 6) if wp == 0 else range(6)
                        kl = list(kts)
                        pcw = p_cl[wp % 2]
                        pend = []

                        def ctx_emit(pcw=pcw, vs=vs):
                            jgt, jpp, jfirst, jlast = pend.pop(0)
                            p.t(lambda e, jgt=jgt, jpp=jpp, jfirst=jfirst, jlast=jlast: e.matmul(
                                pcw[:], V2[:, jgt, vs], jpp[:],
                                start=jfirst, stop=jlast))

                        for idx, kt in enumerate(kl):
                            gt = (4 * wp - 2) + kt
                            psl, pp = p_sl[idx % 3], P_l[idx % 3]
                            p.t(lambda e, gt=gt, q0=q0, hs=hs, psl=psl: e.matmul(
                                psl[:], kp[hs, gt * 128:(gt + 1) * 128], qp[hs, q0:q0 + 2 * W],
                                start=True, stop=True))
                            p.v(lambda e, kt=kt, psl=psl: e.tensor_tensor(psl[:], psl[:],
                                                                 MASK6[:, kt, :], ALU.add))
                            p.s(lambda e, psl=psl, pp=pp: e.activation(pp[:], psl[:], AF.Exp,
                                                       scale=np.float32(8.0), bias=negb[:]))
                            pend.append((gt, pp, idx == 0, idx == len(kl) - 1))
                            if len(pend) > 2:
                                ctx_emit()
                        while pend:
                            ctx_emit()
                        rl, cl_, dn = rbl[wp % 2], ctxl[wp % 2], den[wp % 2]
                        p.s(lambda e, pcw=pcw, dn=dn: e.activation(dn[:], pcw[64:65, :], AF.Ln))
                        p.s(lambda e, dn=dn: e.activation(dn[:], dn[:],
                                                   AF.Exp, scale=np.float32(-1.0)))
                        p.t(lambda e, dn=dn: e.matmul(p_rbl[:], ones1_f32[0:1, 0:64],
                                               dn[:], start=True, stop=True))
                        p.v(lambda e, rl=rl: e.tensor_copy(rl[:], p_rbl[:]))
                        p.v(lambda e, pcw=pcw, rl=rl, cl_=cl_: e.tensor_tensor(cl_[:], pcw[0:64, :], rl[:], ALU.mult))
                        p.dma(io["scr_ctx"][(h % 2) * 64:(h % 2) * 64 + 64, h // 2, q0:q0 + 2 * W],
                              cl_[:])
        with ExitStack() as so:
            sbO = lambda n, s, d: so.enter_context(nc.sbuf_tensor(p.uname(n), s, d))
            p_wo = [so.enter_context(nc.psum_tensor(p.uname("f_p_wo"), [128, Tc], F32)) for _ in range(2)]
            wo = sbO("f_wo", [128, NF, D], BF16)
            for kt in range(NF):
                p.gdma(wo[:, kt, :], key("WoT")[kt * 128:(kt + 1) * 128, :])
            ctx_c = sbO("f_ctxc", [128, NF, Tc], BF16)
            for c in range(T // Tc):
                t0 = c * Tc
                p.dma(ctx_c[:], io["scr_ctx"][:, :, t0:t0 + Tc])
                for mt in range(NF):
                    pw = p_wo[mt % 2]
                    for kt in range(NF):
                        p.t(lambda e, mt=mt, kt=kt, pw=pw: e.matmul(
                            pw[:], wo[:, kt, mt * 128:(mt + 1) * 128], ctx_c[:, kt, :],
                            start=(kt == 0), stop=(kt == NF - 1)))
                    p.v(lambda e, mt=mt, t0=t0, pw=pw: e.tensor_tensor(
                        x[:, mt, 1 + t0:1 + t0 + Tc], x[:, mt, 1 + t0:1 + t0 + Tc],
                        pw[:], ALU.add))
            p.dma(io["scr_x"][:, :, :], x[:, :, :])
        sx.close()

        with ExitStack() as sf:
            sbF = lambda n, s, d: sf.enter_context(nc.sbuf_tensor(p.uname(n), s, d))
            psF = lambda n, s, d=F32: sf.enter_context(nc.psum_tensor(p.uname(n), s, d))
            NJ = 22
            w1 = sbF("f_w1", [128, NF, 5460], BF16)
            for kt in range(NF):
                p.gdma(w1[:, kt, :], key("W1")[kt * 128:(kt + 1) * 128, :])
            w2 = sbF("f_w2", [128, NJ, D], BF16)
            for kt in range(NJ):
                p.gdma(w2[:, kt, :], key("W2p")[kt * 128:(kt + 1) * 128, :])
            lng = sbF("f_lng", [128, NF], F32)
            lnb = sbF("f_lnb", [128, NF], F32)
            b1a = sbF("f_b1a", [128, NJ], F32)
            b1g = sbF("f_b1g", [128, NJ], F32)
            b2 = sbF("f_b2", [128, NF], F32)
            p.dma(lng[:], key("lng"))
            p.dma(lnb[:], key("lnb"))
            p.dma(b1a[:], key("b1a"))
            p.dma(b1g[:], key("b1g"))
            p.dma(b2[:], key("b2"))
            xw = sbF("f_xw", [128, NF, Tcf + 1], BF16)
            xn = sbF("f_xn", [128, NF, Tcf], BF16)
            gel = [sbF("f_gel", [128, Tcf], BF16) for _ in range(2)]
            u = sbF("f_u", [128, NJ, Tcf], BF16)
            xo = sbF("f_xo", [128, NF, Tcf], BF16)
            sqf2 = [sbF("f_sqf2", [128, Tcf], F32) for _ in range(2)]
            lnt2 = [sbF("f_lnt2", [128, Tcf], F32) for _ in range(2)]
            p_sum = psF("f_p_sum", [1, Tcf])
            p_sq = psF("f_p_sq", [1, Tcf])
            p_a = [psF("f_p_a", [128, Tcf]) for _ in range(2)]
            p_g = [psF("f_p_g", [128, Tcf]) for _ in range(2)]
            p_m = [psF("f_p_m", [128, Tcf]) for _ in range(2)]
            p_b1 = p_m[0]
            p_b2 = p_m[1]
            for c in range(T // Tcf):
                t0 = c * Tcf
                p.dma(xw[:], io["scr_x"][:, :, t0:t0 + Tcf + 1])
                xsw = [xw[:, kt, 1:Tcf + 1] if kt < 4 else xw[:, kt, 0:Tcf] for kt in range(NF)]
                for kt in range(NF):
                    p.t(lambda e, kt=kt: e.matmul(p_sum[:], ones_bf[:], xsw[kt],
                                                  start=(kt == 0), stop=(kt == NF - 1)))
                for kt in range(NF):
                    p.s(lambda e, kt=kt: e.activation(sqf2[kt % 2][:], xsw[kt], AF.Square))
                    p.t(lambda e, kt=kt: e.matmul(p_sq[:], ones_f32[:], sqf2[kt % 2][:],
                                                  start=(kt == 0), stop=(kt == NF - 1)))
                _ln_rows(p, p_sum[:], p_sq[:], Tcf, 1e-5, rows)
                p.t(lambda e: e.matmul(p_b1[:], ones1_f32[:], rows[1][0:1, 0:Tcf], start=True, stop=True))
                p.t(lambda e: e.matmul(p_b2[:], ones1_f32[:], rows[2][0:1, 0:Tcf], start=True, stop=True))
                for kt in range(NF):
                    lt = lnt2[kt % 2]
                    p.v(lambda e, kt=kt, lt=lt: e.tensor_tensor(lt[:], xsw[kt], p_b1[:], ALU.mult))
                    p.v(lambda e, lt=lt: e.tensor_tensor(lt[:], lt[:], p_b2[:], ALU.add))
                    p.v(lambda e, kt=kt, lt=lt: e.tensor_scalar(xn[:, kt, :], lt[:],
                                                         lng[:, kt:kt + 1], lnb[:, kt:kt + 1],
                                                         ALU.mult, ALU.add))
                for jt in range(NJ):
                    mj = 128 if jt < NJ - 1 else 42
                    pa, pg, gl = p_a[jt % 2], p_g[jt % 2], gel[jt % 2]
                    for kt in range(NF):
                        p.t(lambda e, jt=jt, kt=kt, mj=mj, pa=pa: e.matmul(
                            pa[0:mj, :], w1[:, kt, jt * 128:jt * 128 + mj], xn[:, kt, :],
                            start=(kt == 0), stop=(kt == NF - 1)))
                    for kt in range(NF):
                        p.t(lambda e, jt=jt, kt=kt, mj=mj, pg=pg: e.matmul(
                            pg[0:mj, :], w1[:, kt, 2730 + jt * 128:2730 + jt * 128 + mj],
                            xn[:, kt, :], start=(kt == 0), stop=(kt == NF - 1)))
                    p.s(lambda e, jt=jt, mj=mj, pg=pg, gl=gl: e.activation(gl[0:mj, :], pg[0:mj, :], AF.Gelu,
                                                             bias=b1g[0:mj, jt:jt + 1]))
                    p.v(lambda e, jt=jt, mj=mj, pa=pa, gl=gl: e.scalar_tensor_tensor(
                        u[0:mj, jt, :], pa[0:mj, :], b1a[0:mj, jt:jt + 1], gl[0:mj, :],
                        ALU.add, ALU.mult))
                for mt in range(NF):
                    pm = p_m[mt % 2]
                    for jt in range(NJ):
                        mj = 128 if jt < NJ - 1 else 42
                        p.t(lambda e, mt=mt, jt=jt, mj=mj, pm=pm: e.matmul(
                            pm[:], w2[0:mj, jt, mt * 128:(mt + 1) * 128], u[0:mj, jt, :],
                            start=(jt == 0), stop=(jt == NJ - 1)))
                    p.v(lambda e, mt=mt, pm=pm: e.scalar_tensor_tensor(
                        xo[:, mt, :], pm[:], b2[:, mt:mt + 1], xw[:, mt, 1:Tcf + 1],
                        ALU.add, ALU.add))
                p.dma(io["xout"][:, :, t0:t0 + Tcf], xo[:, :, :])


def emit_addln(p, io, T=T_FULL, Tc=512):
    nc = p.nc
    with ExitStack() as st:
        sb = lambda n, s, d: st.enter_context(nc.sbuf_tensor(p.uname(n), s, d))
        ps = lambda n, s, d=F32: st.enter_context(nc.psum_tensor(p.uname(n), s, d))
        ones_f32 = sb("g_ones_f32", [128, 1], F32)
        ones1_f32 = sb("g_ones1_f32", [1, 128], F32)
        identf = sb("g_identf", [128, 128], F32)
        p.dma(identf[:], io["IDENTF"])
        p.v(lambda e: e.memset(ones_f32[:], 1.0))
        p.v(lambda e: e.memset(ones1_f32[:], 1.0))
        rows = (sb("g_r0", [1, Tc], F32), sb("g_r1", [1, Tc], F32),
                sb("g_r2", [1, Tc], F32), sb("g_r3", [1, Tc], F32))
        a_c = sb("g_a", [128, NF, Tc], BF16)
        b_c = sb("g_b", [128, NF, Tc], BF16)
        u_c = sb("g_u", [128, NF, Tc], F32)
        xo = sb("g_xo", [128, NF, Tc], F32)
        sq = sb("g_sq", [128, Tc], F32)
        otile = sb("g_otile", [128, 128], F32)
        p_sum = ps("g_p_sum", [1, Tc])
        p_sq = ps("g_p_sq", [1, Tc])
        p_b1 = ps("g_p_b1", [128, Tc])
        p_b2 = ps("g_p_b2", [128, Tc])
        p_tr = ps("g_p_tr", [128, 128])
        for c in range(T // Tc):
            t0 = c * Tc
            p.dma(a_c[:], io["ain"][:, :, t0:t0 + Tc])
            p.dma(b_c[:], io["bin"][:, :, t0:t0 + Tc])
            for ft in range(NF):
                p.v(lambda e, ft=ft: e.tensor_tensor(u_c[:, ft, :], a_c[:, ft, :],
                                                     b_c[:, ft, :], ALU.add))
            for ft in range(NF):
                p.t(lambda e, ft=ft: e.matmul(p_sum[:], ones_f32[:], u_c[:, ft, :],
                                              start=(ft == 0), stop=(ft == NF - 1)))
            for ft in range(NF):
                p.s(lambda e, ft=ft: e.activation(sq[:], u_c[:, ft, :], AF.Square))
                p.t(lambda e, ft=ft: e.matmul(p_sq[:], ones_f32[:], sq[:],
                                              start=(ft == 0), stop=(ft == NF - 1)))
            _ln_rows(p, p_sum[:], p_sq[:], Tc, 1e-5, rows)
            p.t(lambda e: e.matmul(p_b1[:], ones1_f32[:], rows[1][0:1, 0:Tc], start=True, stop=True))
            p.t(lambda e: e.matmul(p_b2[:], ones1_f32[:], rows[2][0:1, 0:Tc], start=True, stop=True))
            for ft in range(NF):
                p.v(lambda e, ft=ft: e.tensor_tensor(sq[:], u_c[:, ft, :], p_b1[:], ALU.mult))
                p.v(lambda e, ft=ft: e.tensor_tensor(xo[:, ft, :], sq[:], p_b2[:], ALU.add))
            for ft in range(NF):
                for tt in range(Tc // 128):
                    p.t(lambda e, ft=ft, tt=tt: e.transpose(
                        p_tr[:], xo[:, ft, tt * 128:(tt + 1) * 128], identf[:]))
                    p.v(lambda e: e.tensor_copy(otile[:], p_tr[:]))
                    p.dma(io["out"][t0 + tt * 128:t0 + (tt + 1) * 128,
                                    ft * 128:(ft + 1) * 128], otile[:])


def _rotary_tables(T):
    DIMH, W = 64, 256
    half = np.arange(0, DIMH, 2, dtype=np.float64)
    inv_freq = 1.0 / 10000.0 ** (half / DIMH)
    t = np.arange(T, dtype=np.float64)
    freqs = t[:, None] * inv_freq[None, :]
    freqs = np.concatenate([freqs, freqs], -1)
    base_scale = (half + 0.4 * DIMH) / (1.4 * DIMH)
    power = (t - T // 2) / (W // 2)
    scale = base_scale[None, :] ** power[:, None]
    scale = np.concatenate([scale, scale], -1)
    cos, sin = np.cos(freqs), np.sin(freqs)
    CS = (cos * scale).T.astype(np.float32)
    SS = (sin * scale).T.astype(np.float32)
    CI = (cos / scale).T.astype(np.float32)
    SI = (sin / scale).T.astype(np.float32)
    two = lambda M: np.ascontiguousarray(np.concatenate([M, M], 0))
    return two(CS), two(SS), two(CI), two(SI)


def _l1ffn_masks():
    W = 256
    M = np.zeros((128, 6, 2 * W), np.float32)
    for kt in range(6):
        ks = kt * 128 + np.arange(128)
        qi = np.arange(2 * W)
        ok = (qi[None, :] <= ks[:, None]) & (qi[None, :] >= ks[:, None] - W)
        M[:, kt, :] = np.where(ok, 0.0, -1e30)
    return M


def prep_l1ffn_inputs(w, T=T_FULL):
    c = np.ascontiguousarray
    CS2, SS2, CI2, SI2 = _rotary_tables(T)
    HONES = np.zeros((128, 2), np.float32)
    HONES[0:64, 0] = 1.0
    HONES[64:128, 1] = 1.0
    SEL2 = np.zeros((2, 128), np.float32)
    SEL2[0, 0:64] = 1.0
    SEL2[1, 64:128] = 1.0
    d = {
        "CS2": CS2, "SS2": SS2, "CI2": CI2, "SI2": SI2,
        "MASK6": _l1ffn_masks(), "HONES": HONES, "SEL2": SEL2,
        "IDENTF": np.eye(128, dtype=np.float32),
    }
    for li in range(6):
        W2p = np.zeros((22 * 128, 1024), np.float32)
        W2p[0:2730] = w["L1_ffn_W2"][li]
        b1 = w["L1_ffn_b1"][li]
        b1a = np.zeros((22 * 128,), np.float32)
        b1a[0:2730] = b1[0:2730]
        b1g = np.zeros((22 * 128,), np.float32)
        b1g[0:2730] = b1[2730:5460]
        qs = w["L1_ffn_q_scale"][li]
        ks = w["L1_ffn_k_scale"][li]
        d.update({
            f"l{li}_WqkvT": c(w["L1_ffn_Wqkv"][li]),
            f"l{li}_WoT": c(w["L1_ffn_Wo"][li]),
            f"l{li}_W1": c(w["L1_ffn_W1"][li]),
            f"l{li}_W2p": W2p,
            f"l{li}_qs2": c(np.concatenate([qs, qs]).reshape(128, 1)),
            f"l{li}_ks2": c(np.concatenate([ks, ks]).reshape(128, 1)),
            f"l{li}_qs_row": qs.reshape(1, 64).copy(),
            f"l{li}_ks_row": ks.reshape(1, 64).copy(),
            f"l{li}_lng": c(w["L1_ffn_ln_g"][li].reshape(8, 128).T),
            f"l{li}_lnb": c(w["L1_ffn_ln_b"][li].reshape(8, 128).T),
            f"l{li}_b1a": c(b1a.reshape(22, 128).T),
            f"l{li}_b1g": c(b1g.reshape(22, 128).T),
            f"l{li}_b2": c(w["L1_ffn_b2"][li].reshape(8, 128).T),
        })
    return d


def build_l1ffn_program(li="X", T=T_FULL):
    nc = bass.Bass("TRN2", target_bir_lowering=False)
    dram = lambda n, s, dt, k: nc.dram_tensor(n, s, dt, kind=k).ap()
    io = {
        "xin": dram("xin", [128, NF, T], BF16, "ExternalInput"),
        "CS2": dram("CS2", [128, T], F32, "ExternalInput"),
        "SS2": dram("SS2", [128, T], F32, "ExternalInput"),
        "CI2": dram("CI2", [128, T], F32, "ExternalInput"),
        "SI2": dram("SI2", [128, T], F32, "ExternalInput"),
        "MASK6": dram("MASK6", [128, 6, 512], F32, "ExternalInput"),
        "HONES": dram("HONES", [128, 2], F32, "ExternalInput"),
        "SEL2": dram("SEL2", [2, 128], F32, "ExternalInput"),
        "scr_ctx": dram("scr_ctx", [128, NF, T], BF16, "Internal"),
        "scr_x": dram("scr_x", [128, NF, T + 1], BF16, "Internal"),
        "xout": dram("xout", [128, NF, T], BF16, "ExternalOutput"),
    }
    li_keys = ["WqkvT", "WoT", "W1", "W2p", "qs2", "ks2", "qs_row", "ks_row",
               "lng", "lnb", "b1a", "b1g", "b2"]
    shapes = {
        "WqkvT": [D, 3 * D], "WoT": [D, D], "W1": [D, 5460], "W2p": [22 * 128, D],
        "qs2": [128, 1], "ks2": [128, 1], "qs_row": [1, 64], "ks_row": [1, 64],
        "lng": [128, NF], "lnb": [128, NF], "b1a": [128, 22], "b1g": [128, 22],
        "b2": [128, NF],
    }
    for nm in li_keys:
        io[f"l{li}_{nm}"] = dram(f"l{li}_{nm}", shapes[nm], F32, "ExternalInput")
    with ExitStack() as stack:
        p = Prog(nc, stack)
        emit_l1ffn_layer(p, io, li, T=T, Tc=min(512, T), Tcf=min(512, T))
        p.finalize(strict=STRICT)
    return nc

def build_addln_program(T=T_FULL):
    nc = bass.Bass("TRN2", target_bir_lowering=False)
    dram = lambda n, s, dt, k: nc.dram_tensor(n, s, dt, kind=k).ap()
    io = {
        "ain": dram("ain", [128, NF, T], BF16, "ExternalInput"),
        "bin": dram("bin", [128, NF, T], BF16, "ExternalInput"),
        "IDENTF": dram("IDENTF", [128, 128], F32, "ExternalInput"),
        "out": dram("out", [T, D], F32, "ExternalOutput"),
    }
    with ExitStack() as stack:
        p = Prog(nc, stack)
        emit_addln(p, io, T=T, Tc=min(512, T))
        p.finalize(strict=STRICT)
    return nc





def build_fused_program(T=T_FULL):
    nc = bass.Bass("TRN2", target_bir_lowering=False)
    DI = 2048
    dram = lambda n, s, dt, k: nc.dram_tensor(n, s, dt, kind=k).ap()

    def stream(name):
        return dram(name, [128, NF, T], BF16, "Internal")

    io_m = {
        "xT": dram("xT", [D, T], F32, "ExternalInput"),
        "inprojT": dram("inprojT", [D, 2 * DI], F32, "ExternalInput"),
        "convw": dram("convw", [128, 16, 4], F32, "ExternalInput"),
        "convb": dram("convb", [128, 16], F32, "ExternalInput"),
        "xprojT_dt": dram("xprojT_dt", [128, 16, 64], F32, "ExternalInput"),
        "xprojT_B": dram("xprojT_B", [128, 16, 16], F32, "ExternalInput"),
        "xprojT_C": dram("xprojT_C", [128, 16, 16], F32, "ExternalInput"),
        "dtprojT": dram("dtprojT", [64, DI], F32, "ExternalInput"),
        "dtbias": dram("dtbias", [128, 16], F32, "ExternalInput"),
        "Alog_fm": dram("Alog_fm", [128, 16, 16], F32, "ExternalInput"),
        "D_fm": dram("D_fm", [128, 16], F32, "ExternalInput"),
        "outprojT": dram("outprojT", [DI, D], F32, "ExternalInput"),
        "ln1g": dram("ln1g", [128, NF], F32, "ExternalInput"),
        "ln1b": dram("ln1b", [128, NF], F32, "ExternalInput"),
        "ln2g": dram("ln2g", [128, NF], F32, "ExternalInput"),
        "ln2b": dram("ln2b", [128, NF], F32, "ExternalInput"),
        "fw1T": dram("fw1T", [D, 4 * D], F32, "ExternalInput"),
        "fb1": dram("fb1", [128, 32], F32, "ExternalInput"),
        "fw2T": dram("fw2T", [4 * D, D], F32, "ExternalInput"),
        "fb2": dram("fb2", [128, NF], F32, "ExternalInput"),
        "SEL16": dram("SEL16", [16, 2048], F32, "ExternalInput"),
        "scr_xc": dram("scr_xc", [128, 16, T], BF16, "Internal"),
        "scr_dt": dram("scr_dt", [128, 16, T], BF16, "Internal"),
        "scr_sz": dram("scr_sz", [128, 16, T], BF16, "Internal"),
        "scr_B": dram("scr_B", [16, T], BF16, "Internal"),
        "scr_C": dram("scr_C", [16, T], BF16, "Internal"),
        "scr_y": dram("scr_y", [128, 16, T], BF16, "Internal"),
        "scr_r1": dram("scr_r1", [128, NF, T], BF16, "Internal"),
        "scr_s2": dram("scr_s2", [128, NF, T], BF16, "Internal"),
        "xout": stream("str0"),
    }
    identf = dram("IDENTF", [128, 128], F32, "ExternalInput")
    io_moe = {
        "xin": io_m["xout"],
        "moe_w1": dram("moe_w1", [128, 6, NF, 512], F32, "ExternalInput"),
        "moe_w2": dram("moe_w2", [128, 6, 4, D], F32, "ExternalInput"),
        "moe_projT": dram("moe_projT", [128, NF, 6], F32, "ExternalInput"),
        "moe_anchT": dram("moe_anchT", [6, 6], F32, "ExternalInput"),
        "moe_temp": dram("moe_temp", [1, 1], F32, "ExternalInput"),
        "SEL6": dram("SEL6", [6, 768], F32, "ExternalInput"),
        "IDENTF": identf,
        "xout": stream("str1"),
    }
    io_a = {
        "xin": io_moe["xout"],
        "qwT": dram("qwT", [D, D], F32, "ExternalInput"),
        "kwT": dram("kwT", [D, D], F32, "ExternalInput"),
        "vwT": dram("vwT", [D, D], F32, "ExternalInput"),
        "qb_c": dram("qb_c", [64, 16], F32, "ExternalInput"),
        "kb_c": dram("kb_c", [64, 16], F32, "ExternalInput"),
        "vb_rep": dram("vb_rep", [128, 16, 64], F32, "ExternalInput"),
        "outwT": dram("outwT", [D, D], F32, "ExternalInput"),
        "outb_c": dram("outb_c", [128, NF], F32, "ExternalInput"),
        "scr_ctx": dram("a_scr_ctx", [128, NF, T], BF16, "Internal"),
        "xout": stream("str2"),
    }
    io_f = {
        "CS2": dram("CS2", [128, T], F32, "ExternalInput"),
        "SS2": dram("SS2", [128, T], F32, "ExternalInput"),
        "CI2": dram("CI2", [128, T], F32, "ExternalInput"),
        "SI2": dram("SI2", [128, T], F32, "ExternalInput"),
        "MASK6": dram("MASK6", [128, 6, 512], F32, "ExternalInput"),
        "HONES": dram("HONES", [128, 2], F32, "ExternalInput"),
        "SEL2": dram("SEL2", [2, 128], F32, "ExternalInput"),
        "scr_ctx": dram("f_scr_ctx", [128, NF, T], BF16, "Internal"),
        "scr_x": dram("f_scr_x", [128, NF, T + 1], BF16, "Internal"),
    }
    li_keys = ["WqkvT", "WoT", "W1", "W2p", "qs2", "ks2", "qs_row", "ks_row",
               "lng", "lnb", "b1a", "b1g", "b2"]
    shapes = {
        "WqkvT": [D, 3 * D], "WoT": [D, D], "W1": [D, 5460], "W2p": [22 * 128, D],
        "qs2": [128, 1], "ks2": [128, 1], "qs_row": [1, 64], "ks_row": [1, 64],
        "lng": [128, NF], "lnb": [128, NF], "b1a": [128, 22], "b1g": [128, 22],
        "b2": [128, NF],
    }
    for li in range(6):
        for nm in li_keys:
            io_f[f"l{li}_{nm}"] = dram(f"l{li}_{nm}", shapes[nm], F32, "ExternalInput")
    lstreams = [io_a["xout"]] + [stream(f"str{3 + i}") for i in range(6)]
    io_g = {
        "ain": io_a["xout"],
        "bin": lstreams[6],
        "IDENTF": identf,
        "out": dram("out", [T, D], F32, "ExternalOutput"),
    }

    with ExitStack() as stack:
        p = Prog(nc, stack)
        emit_mamba(p, io_m, T=T, Tc_a=min(512, T), Tc_b=min(1024, T), Tc_d=min(256, T))
        emit_moe(p, io_moe, T=T, Tc=min(512, T))
        emit_attn(p, io_a, T=T, Tc=min(512, T))
        for li in range(6):
            iof = dict(io_f)
            iof["xin"] = lstreams[li]
            iof["xout"] = lstreams[li + 1]
            emit_l1ffn_layer(p, iof, li, T=T, Tc=min(512, T), Tcf=min(512, T))
        emit_addln(p, io_g, T=T, Tc=min(512, T))
        p.finalize(strict=STRICT)
    return nc


def kernel(**inputs):
    w = inputs
    T = T_FULL
    _ = np.asarray(inputs["L1_attn_mask"])
    x = np.asarray(w["L0_attn_x"], np.float32)
    nc = _get_prog("fused", lambda: build_fused_program(T=T))
    base = {}
    base.update(prep_moe_inputs(w))
    base.update(prep_attn_inputs(w))
    base.update(prep_l1ffn_inputs(w, T=T))
    maps = []
    for b in range(NCORES):
        m = dict(base)
        m.update(prep_mamba_inputs(np.ascontiguousarray(x[b]), w))
        maps.append(m)
    r = _launch(nc, maps)
    return np.stack([np.asarray(rr["out"], dtype=np.float32) for rr in r], axis=0)


_prog_cache = {}


def _get_prog(name, builder):
    if name not in _prog_cache:
        _prog_cache[name] = builder()
    return _prog_cache[name]


def _launch(nc, in_maps):
    res = run_bass_kernel_spmd(nc, in_maps, core_ids=list(range(NCORES)))
    return res.results


def _filter(nc, m):
    names = set()
    for t in nc.m.tensors:
        names.add(t.name)
    return m


def kernel_unfused(**inputs):
    w = inputs
    T = T_FULL
    _ = np.asarray(inputs["L1_attn_mask"])
    x = np.asarray(w["L0_attn_x"], np.float32)

    nc1 = _get_prog("mamba", lambda: build_mamba_program(T=T))
    maps = []
    shared1 = None
    for b in range(NCORES):
        m = prep_mamba_inputs(np.ascontiguousarray(x[b]), w)
        maps.append(m)
    r = _launch(nc1, maps)
    streams = [np.asarray(rr["xout"]) for rr in r]

    nc2 = _get_prog("moe", lambda: build_moe_program(T=T))
    base2 = prep_moe_inputs(w)
    maps = [dict(base2, xin=streams[b]) for b in range(NCORES)]
    r = _launch(nc2, maps)
    streams = [np.asarray(rr["xout"]) for rr in r]

    nc3 = _get_prog("attn", lambda: build_attn_program(T=T))
    base3 = prep_attn_inputs(w)
    maps = [dict(base3, xin=streams[b]) for b in range(NCORES)]
    r = _launch(nc3, maps)
    streams = [np.asarray(rr["xout"]) for rr in r]
    pre_l1ffn = streams

    nc4 = _get_prog("l1ffn", lambda: build_l1ffn_program("X", T=T))
    base4 = prep_l1ffn_inputs(w, T=T)
    shared_keys = ["CS2", "SS2", "CI2", "SI2", "MASK6", "HONES", "SEL2"]
    li_keys = ["WqkvT", "WoT", "W1", "W2p", "qs2", "ks2", "qs_row", "ks_row",
               "lng", "lnb", "b1a", "b1g", "b2"]
    for li in range(6):
        m0 = {k: base4[k] for k in shared_keys}
        for nm in li_keys:
            m0[f"lX_{nm}"] = base4[f"l{li}_{nm}"]
        maps = [dict(m0, xin=streams[b]) for b in range(NCORES)]
        r = _launch(nc4, maps)
        streams = [np.asarray(rr["xout"]) for rr in r]

    nc5 = _get_prog("addln", lambda: build_addln_program(T=T))
    ident = np.eye(128, dtype=np.float32)
    maps = [dict(ain=pre_l1ffn[b], bin=streams[b], IDENTF=ident)
            for b in range(NCORES)]
    r = _launch(nc5, maps)
    out = np.stack([np.asarray(rr["out"], dtype=np.float32) for rr in r], axis=0)
    return out


# version 20
# speedup vs baseline: 1.0773x; 1.0363x over previous
import sys
import numpy as np

sys.path.insert(0, "/opt/trn_rl_repo")
import concourse.bass as bass
import concourse.mybir as mybir
from concourse.bass_utils import run_bass_kernel_spmd
from contextlib import ExitStack

F32 = mybir.dt.float32
BF16 = mybir.dt.bfloat16
AF = mybir.ActivationFunctionType
ALU = mybir.AluOpType
AX = mybir.AxisListType

NCORES = 8
STRICT = False
D = 1024
NF = D // 128
T_FULL = 4096



class Prog:

    def __init__(self, nc, stack):
        self.nc = nc
        self.stack = stack
        self.SQ = stack.enter_context(nc.semaphore("SQ"))
        self.GQ = stack.enter_context(nc.semaphore("GQ"))
        self.sched = []
        self.const_tile = None
        self.const_map = {}
        self._uid = 0
        self._eng_last_clock = [[0] * 5 for _ in range(5)]
        self._count_to_op = [dict() for _ in range(5)]

    def sb(self, name, shape, dtype):
        return self.stack.enter_context(self.nc.sbuf_tensor(self.uname(name), shape, dtype))

    def ps(self, name, shape, dtype=F32):
        return self.stack.enter_context(self.nc.psum_tensor(name, shape, dtype))

    def op(self, eng, fn):
        self.sched.append((eng, fn, "c"))

    def dma(self, out, in_):
        self.sched.append(("sync", lambda e: e.dma_start(out=out, in_=in_), "d"))

    def gdma(self, out, in_):
        self.sched.append(("gpsimd", lambda e: e.dma_start(out=out, in_=in_), "g"))

    def uname(self, n):
        self._uid += 1
        return f"{n}__{self._uid}"

    def constf(self, val, parts=128):
        val = float(val)
        if self.const_tile is None:
            self.const_tile = self.sb("const_pool", [128, 16], F32)
        if val not in self.const_map:
            j = len(self.const_map)
            assert j < 16
            self.const_map[val] = j
            col = self.const_tile[:, j:j + 1]
            self.v(lambda e: e.memset(col, val))
        j = self.const_map[val]
        return self.const_tile[0:parts, j:j + 1]

    def t(self, fn):
        self.op("tensor", fn)

    def v(self, fn):
        self.op("vector", fn)

    def s(self, fn):
        self.op("scalar", fn)


    ENGS = ("sync", "vector", "scalar", "tensor", "gpsimd")
    NDMASLOT = 8

    def finalize(self, strict=False):
        nc = self.nc
        engines = {"sync": nc.sync, "vector": nc.vector, "scalar": nc.scalar,
                   "tensor": nc.tensor, "gpsimd": nc.gpsimd}

        mloc_cache = {}

        def mloc_info(name):
            if name not in mloc_cache:
                try:
                    m = nc.lookup_mloc(name)
                    base = int(m.addr)
                    if str(m.type) == "PSUM":
                        base += int(m.bank) * 2048
                    mloc_cache[name] = (str(m.type), base)
                except Exception:
                    mloc_cache[name] = ("DRAM", 0)
            return mloc_cache[name]

        def intervals(args):
            out = []
            for a in args:
                name = getattr(a, "memref", None)
                ap = getattr(a, "ap", None)
                if name is None:
                    t = getattr(a, "tensor", None)
                    if t is None:
                        continue
                    name = t.name
                if ap is None:
                    continue
                try:
                    off = int(a.offset)
                except (TypeError, ValueError):
                    off = 0
                span = 1
                for step, count in list(ap):
                    span += int(abs(step)) * (int(count) - 1)
                dsz = mybir.dt.size(a.dtype) if getattr(a, "dtype", None) is not None else 4
                space, base = mloc_info(name)
                if space == "DRAM":
                    out.append((name, off * dsz, (off + span) * dsz))
                else:
                    out.append((space, base + off * dsz, base + (off + span) * dsz))
            return out

        NS = self.NDMASLOT
        chan_names = list(self.ENGS)
        dma_chan = {}
        for q in ("sync", "gpsimd"):
            for r in range(NS):
                dma_chan[(q, r)] = len(chan_names)
                chan_names.append(f"dq_{q}{r}")
        NCH = len(chan_names)
        ei = {e: i for i, e in enumerate(self.ENGS)}
        te = ei["tensor"]

        rw = []
        dry_sem = self.stack.enter_context(nc.semaphore("S_dry"))
        with nc.discard():
            for eng, fn, kind in self.sched:
                bi = fn(engines[eng])
                if kind in ("d", "g"):
                    bi.then_inc(dry_sem, 16)
                inst = bi.ins
                rw.append((intervals(inst.ins), intervals(inst.outs)))

        ecnt = [0] * NCH
        opdep = []
        opchan = []
        opcount = []
        opclock = []
        count_to_op = [dict() for _ in range(NCH)]
        eng_last_clock = [[0] * NCH for _ in range(5)]
        last_access = {}
        dma_idx = {"sync": 0, "gpsimd": 0}
        for i, (eng, fn, kind) in enumerate(self.sched):
            e = ei[eng]
            if kind in ("d", "g"):
                q = eng
                slot = dma_idx[q] % NS
                dma_idx[q] += 1
                ch = dma_chan[(q, slot)]
            else:
                ch = e
            reads, writes = rw[i]
            deps = {}
            if kind in ("d", "g") and ecnt[ch]:
                deps[ch] = ecnt[ch]
            for (name, lo, hi) in reads:
                for ent in last_access.get(name, ()):
                    if ent[3] and ent[0] < hi and lo < ent[1]:
                        j = ent[2]
                        jc = opcount[j]
                        jch = opchan[j]
                        if deps.get(jch, -1) < jc:
                            deps[jch] = jc
            for (name, lo, hi) in writes:
                for ent in last_access.get(name, ()):
                    if ent[0] < hi and lo < ent[1]:
                        j = ent[2]
                        jc = opcount[j]
                        jch = opchan[j]
                        if deps.get(jch, -1) < jc:
                            deps[jch] = jc
            clk = list(eng_last_clock[e])
            needed = []
            for jch, jc in sorted(deps.items()):
                if clk[jch] >= jc:
                    continue
                jop = count_to_op[jch].get(jc)
                if jop is not None:
                    jclk = opclock[jop]
                    for k in range(NCH):
                        if jclk[k] > clk[k]:
                            clk[k] = jclk[k]
                if not (jch == e == te):
                    needed.append((jch, jc))
                if jc > clk[jch]:
                    clk[jch] = jc
            inc = 16 if kind in ("d", "g") else 1
            ecnt[ch] += inc
            opchan.append(ch)
            opcount.append(ecnt[ch])
            count_to_op[ch][ecnt[ch]] = i
            cclk = list(clk)
            cclk[ch] = ecnt[ch]
            opclock.append(cclk)
            eng_last_clock[e] = clk
            opdep.append(needed)
            for (name, lo, hi) in writes:
                lst = last_access.setdefault(name, [])
                lst[:] = [ent for ent in lst if not (lo <= ent[0] and ent[1] <= hi)]
                lst.append([lo, hi, i, True])
            for (name, lo, hi) in reads:
                last_access.setdefault(name, []).append([lo, hi, i, False])

        block = self.stack.enter_context(nc.Block())
        sems = []
        for cn in chan_names:
            sems.append(self.stack.enter_context(nc.semaphore(f"S_{cn}")))
        total = list(ecnt)
        sched = self.sched

        def runner(en):
            def go(eng):
                for i, (name, fn, kind) in enumerate(sched):
                    if name != en:
                        continue
                    deps = opdep[i]
                    for (jch, jc) in (deps[:-1] if len(deps) > 1 else []):
                        eng.wait_ge(sems[jch], jc)
                    inst = fn(eng)
                    if deps:
                        jch, jc = deps[-1]
                        inst._wait_ge(sems[jch], jc)
                    inst.then_inc(sems[opchan[i]], 16 if kind in ("d", "g") else 1)
                if en == "sync":
                    for k in range(NCH):
                        if total[k]:
                            eng.wait_ge(sems[k], total[k])
            return go

        block.sync(runner("sync"))
        block.vector(runner("vector"))
        block.scalar(runner("scalar"))
        block.tensor(runner("tensor"))
        block.gpsimd(runner("gpsimd"))


def _ln_rows(p, psum_sum, psum_sq, n, eps, rows, nfeat=D):
    mean = rows[0][0:1, 0:n]
    var = rows[1][0:1, 0:n]
    msh = rows[2][0:1, 0:n]
    tmp = rows[3][0:1, 0:n]
    p.v(lambda e: e.tensor_scalar_mul(mean, psum_sum, 1.0 / nfeat))
    p.v(lambda e: e.tensor_scalar_mul(tmp, psum_sq, 1.0 / nfeat))
    p.v(lambda e: e.tensor_mul(var, mean, mean))
    p.v(lambda e: e.tensor_sub(var, tmp, var))
    p.s(lambda e: e.activation(tmp, var, AF.Ln, bias=np.float32(eps)))
    p.s(lambda e: e.activation(var, tmp, AF.Exp, scale=-0.5))
    p.v(lambda e: e.scalar_tensor_tensor(msh, mean, -1.0, var, ALU.mult, ALU.mult))


def ceil_div(a, b):
    return (a + b - 1) // b



def emit_mamba(p, io, T=T_FULL, Tc_a=512, Tc_b=1024, Tc_d=256):
    nc = p.nc
    DI = 2048
    ND = DI // 128
    nchunks_a = T // Tc_a

    with ExitStack() as st:
        sb = lambda n, s, d: st.enter_context(nc.sbuf_tensor(p.uname(n), s, d))
        ps = lambda n, s, d=F32: st.enter_context(nc.psum_tensor(p.uname(n), s, d))

        ones_bf = sb("ones_bf", [128, 1], BF16)
        ones_f32 = sb("ones_f32", [128, 1], F32)
        ones1_f32 = sb("ones1_f32", [1, 128], F32)
        rmax = max(Tc_a, Tc_d)
        rows = (sb("r_mean", [1, rmax], F32), sb("r_rstd", [1, rmax], F32),
                sb("r_mshift", [1, rmax], F32), sb("r_tmp", [1, rmax], F32))
        p.v(lambda e: e.memset(ones_bf[:], 1.0))
        p.v(lambda e: e.memset(ones_f32[:], 1.0))
        p.v(lambda e: e.memset(ones1_f32[:], 1.0))

        ln1g = sb("t_ln1g", [128, NF], F32)
        ln1b = sb("t_ln1b", [128, NF], F32)
        p.dma(ln1g[:], io["ln1g"])
        p.dma(ln1b[:], io["ln1b"])
        convw = sb("t_convw", [128, 16, 4], F32)
        convb = sb("t_convb", [128, 16], F32)
        p.dma(convw[:], io["convw"])
        p.dma(convb[:], io["convb"])
        dtbias = sb("t_dtbias", [128, 16], F32)
        p.dma(dtbias[:], io["dtbias"])
        xprojT_dt = sb("t_xprojT_dt", [128, 16, 64], BF16)
        xprojT_B = sb("t_xprojT_B", [128, 16, 16], BF16)
        xprojT_C = sb("t_xprojT_C", [128, 16, 16], BF16)
        p.gdma(xprojT_dt[:], io["xprojT_dt"])
        p.gdma(xprojT_B[:], io["xprojT_B"])
        p.gdma(xprojT_C[:], io["xprojT_C"])
        dtprojT = sb("t_dtprojT", [64, DI], BF16)
        p.gdma(dtprojT[:], io["dtprojT"])

        with ExitStack() as stA:
            sbA = lambda n, s, d: stA.enter_context(nc.sbuf_tensor(p.uname(n), s, d))
            psA = lambda n, s, d=F32: stA.enter_context(nc.psum_tensor(p.uname(n), s, d))
            w_inp = sbA("w_inp", [128, 8, 2 * DI], BF16)
            for kt in range(8):
                p.gdma(w_inp[:, kt, :], io["inprojT"][kt * 128:(kt + 1) * 128, :])

            Tca = Tc_a
            x_c = sbA("x_c", [128, NF, Tca], BF16)
            xsq_s = sbA("xsq", [128, Tca], BF16)
            xsq = [xsq_s, xsq_s]
            ln_c = sbA("ln_c", [128, NF, Tca], BF16)
            lntmp_s = sbA("lntmp", [128, Tca], BF16)
            lntmp = [lntmp_s, lntmp_s]
            xi_c = sbA("xi_c", [128, 16, Tca + 3], BF16)
            sz_c = [sbA("sz_t", [128, Tca], BF16) for _ in range(2)]
            xc_s = sbA("xc_c", [128, 16, Tca], BF16)
            xc_c = [xc_s, xc_s]
            cv_s = sbA("cv", [128, Tca], F32)
            cv = [cv_s, cv_s]
            dtraw = sbA("dtraw", [64, Tca], BF16)
            Brow = [sbA("Brow", [16, Tca], BF16) for _ in range(2)]
            Crow = [sbA("Crow", [16, Tca], BF16) for _ in range(2)]
            dt_s = sbA("dt_c", [128, 16, Tca], BF16)
            dt_c = [dt_s, dt_s]
            sp1_s = sbA("sp1", [128, Tca], F32)
            sp1 = [sp1_s, sp1_s]
            sp2 = [cv_s, cv_s]

            p_sumA = psA("p_sumA", [1, Tca])
            p_sqA = psA("p_sqA", [1, Tca])
            p_b1 = psA("p_b1", [128, Tca])
            p_b2 = psA("p_b2", [128, Tca])
            p_mm = [psA("p_mm", [128, Tca]) for _ in range(2)]
            p_dbl = psA("p_dbl", [96, Tca])
            p_bc = psA("p_bc", [128, Tca])

            SEL16 = sbA("t_SEL16", [16, 2048], BF16)
            p.gdma(SEL16[:], io["SEL16"])
            negA = sbA("negA", [128, 16, 16], F32)
            p.dma(negA[:], io["Alog_fm"])
            p.s(lambda e: e.activation(negA[:], negA[:], AF.Exp))
            p.v(lambda e: e.tensor_scalar_mul(negA[:], negA[:], -1.0))
            dtxc_all = sbA("dtxc_all", [128, 16, Tca], BF16)
            NDEEP = 3
            a_t = [sbA("a_t", [128, Tca], BF16) for _ in range(NDEEP)]
            b_t = [sbA("b_t", [128, Tca], BF16) for _ in range(2)]
            h_t = [sbA("h_t", [128, Tca], BF16) for _ in range(NDEEP)]
            ym_t = [sbA("ym_t", [128, Tca], BF16) for _ in range(2)]
            yacc = sbA("yacc", [128, 16, Tca], BF16)
            carry = sbA("carry", [128, 256], F32)
            Bb_s = sbA("Bb", [128, Tca], BF16)
            Cb_s = sbA("Cb", [128, Tca], BF16)
            Bb_ = [Bb_s, Bb_s]
            Cb_ = [Cb_s, Cb_s]
            p.v(lambda e: e.memset(carry[:], 0.0))

            carry_q = []

            def flush_carry(lag=0):
                while len(carry_q) > lag:
                    dst, src = carry_q.pop(0)
                    p.s(lambda e, dst=dst, src=src: e.activation(dst, src, AF.Copy))

            for c in range(T // Tca):
                t0 = c * Tca
                cp = c % 2
                xcc, dtc, Brw, Crw = xc_c[cp], dt_c[cp], Brow[cp], Crow[cp]
                for ft in range(NF):
                    p.gdma(x_c[:, ft, :], io["xT"][ft * 128:(ft + 1) * 128, t0:t0 + Tca])
                for ft in range(NF):
                    p.t(lambda e, ft=ft: e.matmul(p_sumA[:], ones_bf[:], x_c[:, ft, :],
                                                  start=(ft == 0), stop=(ft == NF - 1)))
                for ft in range(NF):
                    p.s(lambda e, ft=ft: e.activation(xsq[ft % 2][:], x_c[:, ft, :], AF.Square))
                    p.t(lambda e, ft=ft: e.matmul(p_sqA[:], ones_bf[:], xsq[ft % 2][:],
                                                  start=(ft == 0), stop=(ft == NF - 1)))
                _ln_rows(p, p_sumA[:], p_sqA[:], Tca, 1e-5, rows)
                p.t(lambda e: e.matmul(p_b1[:], ones1_f32[:], rows[1][0:1, 0:Tca], start=True, stop=True))
                p.t(lambda e: e.matmul(p_b2[:], ones1_f32[:], rows[2][0:1, 0:Tca], start=True, stop=True))
                for ft in range(NF):
                    lt = lntmp[ft % 2]
                    p.v(lambda e, ft=ft, lt=lt: e.tensor_tensor(lt[:], x_c[:, ft, :], p_b1[:], ALU.mult))
                    p.v(lambda e, lt=lt: e.tensor_tensor(lt[:], lt[:], p_b2[:], ALU.add))
                    p.v(lambda e, ft=ft, lt=lt: e.tensor_scalar(ln_c[:, ft, :], lt[:],
                        ln1g[:, ft:ft + 1], ln1b[:, ft:ft + 1], ALU.mult, ALU.add))
                if c == 0:
                    p.v(lambda e: e.memset(xi_c[:, :, 0:3], 0.0))
                else:
                    p.v(lambda e: e.tensor_copy(xi_c[:, :, 0:3], xi_c[:, :, Tca:Tca + 3]))
                for mt in range(32):
                    pm = p_mm[mt % 2]
                    for kt in range(NF):
                        p.t(lambda e, mt=mt, kt=kt, pm=pm: e.matmul(
                            pm[:], w_inp[:, kt, mt * 128:(mt + 1) * 128], ln_c[:, kt, :],
                            start=(kt == 0), stop=(kt == NF - 1)))
                    if mt < 16:
                        p.v(lambda e, mt=mt, pm=pm: e.tensor_copy(xi_c[:, mt, 3:3 + Tca], pm[:]))
                    else:
                        szt = sz_c[mt % 2]
                        p.s(lambda e, mt=mt, pm=pm, szt=szt: e.activation(szt[:], pm[:], AF.Silu))
                        p.dma(io["scr_sz"][:, mt - 16, t0:t0 + Tca], szt[:])
                for dti in range(16):
                    cvt = cv[dti % 2]
                    p.v(lambda e, dti=dti, cvt=cvt: e.tensor_scalar_mul(cvt[:], xi_c[:, dti, 0:Tca],
                                                               convw[:, dti, 0:1]))
                    for k in range(1, 4):
                        p.v(lambda e, dti=dti, k=k, cvt=cvt: e.scalar_tensor_tensor(
                            cvt[:], xi_c[:, dti, k:k + Tca], convw[:, dti, k:k + 1], cvt[:],
                            ALU.mult, ALU.add))
                    p.s(lambda e, dti=dti, cvt=cvt, xcc=xcc: e.activation(xcc[:, dti, :], cvt[:], AF.Silu,
                                                        bias=convb[:, dti:dti + 1]))
                for kt in range(16):
                    p.t(lambda e, kt=kt, xcc=xcc: e.matmul(p_dbl[0:64, :], xprojT_dt[:, kt, :], xcc[:, kt, :],
                                                  start=(kt == 0), stop=(kt == 15)))
                p.v(lambda e: e.tensor_copy(dtraw[:], p_dbl[0:64, :]))
                for kt in range(16):
                    p.t(lambda e, kt=kt, xcc=xcc: e.matmul(p_dbl[64:80, :], xprojT_B[:, kt, :], xcc[:, kt, :],
                                                  start=(kt == 0), stop=(kt == 15)))
                p.v(lambda e, Brw=Brw: e.tensor_copy(Brw[:], p_dbl[64:80, :]))
                for kt in range(16):
                    p.t(lambda e, kt=kt, xcc=xcc: e.matmul(p_dbl[0:16, :], xprojT_C[:, kt, :], xcc[:, kt, :],
                                                  start=(kt == 0), stop=(kt == 15)))
                p.v(lambda e, Crw=Crw: e.tensor_copy(Crw[:], p_dbl[0:16, :]))
                for dti in range(16):
                    pm, s1, s2_ = p_mm[dti % 2], sp1[dti % 2], sp2[dti % 2]
                    p.t(lambda e, dti=dti, pm=pm: e.matmul(pm[:], dtprojT[:, dti * 128:(dti + 1) * 128],
                                                    dtraw[:], start=True, stop=True))
                    p.s(lambda e, dti=dti, pm=pm, s1=s1: e.activation(s1[:], pm[:], AF.Abs,
                                                        bias=dtbias[:, dti:dti + 1]))
                    p.s(lambda e, s1=s1: e.activation(s1[:], s1[:], AF.Exp, scale=-1.0))
                    p.s(lambda e, s1=s1: e.activation(s1[:], s1[:], AF.Ln, bias=np.float32(1.0)))
                    p.v(lambda e, dti=dti, pm=pm, s2_=s2_: e.tensor_scalar(s2_[:], pm[:], dtbias[:, dti:dti + 1],
                                                           0.0, ALU.add, ALU.max))
                    p.v(lambda e, dti=dti, s1=s1, s2_=s2_, dtc=dtc: e.tensor_tensor(dtc[:, dti, :], s1[:], s2_[:], ALU.add))
                p.dma(io["scr_xc"][:, :, t0:t0 + Tca], xcc[:, :, :])
                for dti in range(16):
                    p.v(lambda e, dti=dti, dtc=dtc, xcc=xcc: e.tensor_tensor(
                        dtxc_all[:, dti, :], dtc[:, dti, :], xcc[:, dti, :], ALU.mult))
                for n in range(16):
                    Bb, Cb = Bb_[n % 2], Cb_[n % 2]
                    p.t(lambda e, n=n, Brw=Brw: e.matmul(
                        p_bc[:], SEL16[:, n * 128:(n + 1) * 128], Brw[:], start=True, stop=True))
                    p.v(lambda e, Bb=Bb: e.tensor_copy(Bb[:], p_bc[:]))
                    p.t(lambda e, n=n, Crw=Crw: e.matmul(
                        p_bc[:], SEL16[:, n * 128:(n + 1) * 128], Crw[:], start=True, stop=True))
                    p.v(lambda e, Cb=Cb: e.tensor_copy(Cb[:], p_bc[:]))
                    for dti in range(16):
                        it = n * 16 + dti
                        pa, pb, ph, pym = (a_t[it % NDEEP], b_t[dti % 2],
                                           h_t[it % NDEEP], ym_t[dti % 2])
                        gg = (dti % 2 == 1)
                        eng = (lambda f: p.op("gpsimd", f)) if gg else p.v
                        p.s(lambda e, dti=dti, n=n, pa=pa, dtc=dtc: e.activation(
                            pa[:], dtc[:, dti, :], AF.Exp, scale=negA[:, dti, n:n + 1]))
                        flush_carry(lag=2)
                        p.v(lambda e, dti=dti, pb=pb, Bb=Bb: e.tensor_tensor(
                            pb[:], dtxc_all[:, dti, :], Bb[:], ALU.mult))
                        p.v(lambda e, dti=dti, n=n, pa=pa, pb=pb, ph=ph: e.tensor_tensor_scan(
                            ph[:], pa[:], pb[:], carry[:, dti * 16 + n:dti * 16 + n + 1],
                            ALU.mult, ALU.add))
                        carry_q.append((carry[:, dti * 16 + n:dti * 16 + n + 1],
                                        ph[:, Tca - 1:Tca]))
                        if n == 0:
                            eng(lambda e, dti=dti, ph=ph, Cb=Cb: e.tensor_tensor(yacc[:, dti, :], ph[:], Cb[:], ALU.mult))
                        else:
                            eng(lambda e, ph=ph, pym=pym, Cb=Cb: e.tensor_tensor(
                                pym[:], ph[:], Cb[:], ALU.mult))
                            eng(lambda e, dti=dti, pym=pym: e.tensor_tensor(yacc[:, dti, :], yacc[:, dti, :],
                                                                   pym[:], ALU.add))
                flush_carry(lag=0)
                p.dma(io["scr_y"][:, :, t0:t0 + Tca], yacc[:, :, :])

        with ExitStack() as stC:
            sbC = lambda n, s, d: stC.enter_context(nc.sbuf_tensor(p.uname(n), s, d))
            psC = lambda n, s, d=F32: stC.enter_context(nc.psum_tensor(p.uname(n), s, d))
            Tc = Tc_a
            w_outp = sbC("w_outp", [128, 16, D], BF16)
            for kt in range(16):
                p.gdma(w_outp[:, kt, :], io["outprojT"][kt * 128:(kt + 1) * 128, :])
            D_fm = sbC("t_D_fm", [128, 16], F32)
            p.dma(D_fm[:], io["D_fm"])
            y_c = sbC("y_c", [128, 16, Tc], BF16)
            xc2 = sbC("xc2", [128, 16, Tc], BF16)
            sz2 = sbC("sz2", [128, 16, Tc], BF16)
            x_c2 = sbC("x_c2", [128, NF, Tc], BF16)
            yg = sbC("yg", [128, 16, Tc], BF16)
            gtmp = [sbC("gtmp", [128, Tc], F32) for _ in range(2)]
            r1_c = sbC("r1_c", [128, NF, Tc], BF16)
            s2_c = sbC("s2_c", [128, NF, Tc], BF16)
            p_o = [psC("p_o", [128, Tc]) for _ in range(2)]
            for c in range(T // Tc):
                t0 = c * Tc
                p.dma(y_c[:], io["scr_y"][:, :, t0:t0 + Tc])
                p.dma(xc2[:], io["scr_xc"][:, :, t0:t0 + Tc])
                p.dma(sz2[:], io["scr_sz"][:, :, t0:t0 + Tc])
                for ft in range(NF):
                    p.gdma(x_c2[:, ft, :], io["xT"][ft * 128:(ft + 1) * 128, t0:t0 + Tc])
                for dti in range(16):
                    gt = gtmp[dti % 2]
                    p.v(lambda e, dti=dti, gt=gt: e.scalar_tensor_tensor(
                        gt[:], xc2[:, dti, :], D_fm[:, dti:dti + 1], y_c[:, dti, :],
                        ALU.mult, ALU.add))
                    p.op("gpsimd", lambda e, dti=dti, gt=gt: e.tensor_tensor(yg[:, dti, :], gt[:],
                                                           sz2[:, dti, :], ALU.mult))
                for mt in range(NF):
                    po = p_o[mt % 2]
                    for kt in range(16):
                        p.t(lambda e, mt=mt, kt=kt, po=po: e.matmul(
                            po[:], w_outp[:, kt, mt * 128:(mt + 1) * 128], yg[:, kt, :],
                            start=(kt == 0), stop=(kt == 15)))
                    p.v(lambda e, mt=mt, po=po: e.tensor_tensor(r1_c[:, mt, :], x_c2[:, mt, :],
                                                         po[:], ALU.add))
                    p.op("gpsimd", lambda e, mt=mt: e.tensor_tensor(s2_c[:, mt, :], r1_c[:, mt, :],
                                                         x_c2[:, mt, :], ALU.add))
                p.dma(io["scr_r1"][:, :, t0:t0 + Tc], r1_c[:, :, :])
                p.dma(io["scr_s2"][:, :, t0:t0 + Tc], s2_c[:, :, :])

        with ExitStack() as stD:
            sbD = lambda n, s, d: stD.enter_context(nc.sbuf_tensor(p.uname(n), s, d))
            psD = lambda n, s, d=F32: stD.enter_context(nc.psum_tensor(p.uname(n), s, d))
            Tc = Tc_d
            w1 = sbD("w1", [128, NF, 4 * D], BF16)
            for kt in range(NF):
                p.gdma(w1[:, kt, :], io["fw1T"][kt * 128:(kt + 1) * 128, :])
            w2 = sbD("w2", [128, 32, D], BF16)
            for kt in range(32):
                p.gdma(w2[:, kt, :], io["fw2T"][kt * 128:(kt + 1) * 128, :])
            ln2g = sbD("t_ln2g", [128, NF], F32)
            ln2b = sbD("t_ln2b", [128, NF], F32)
            fb1 = sbD("t_fb1", [128, 32], F32)
            fb2 = sbD("t_fb2", [128, NF], F32)
            p.dma(ln2g[:], io["ln2g"])
            p.dma(ln2b[:], io["ln2b"])
            p.dma(fb1[:], io["fb1"])
            p.dma(fb2[:], io["fb2"])
            r1_d = sbD("r1_d", [128, NF, Tc], BF16)
            s2_d = sbD("s2_d", [128, NF, Tc], BF16)
            h_d = sbD("h_d", [128, NF, Tc], BF16)
            hr = sbD("hr", [128, 32, Tc], BF16)
            u_d = sbD("u_d", [128, NF, Tc], F32)
            xo = sbD("xo", [128, NF, Tc], BF16)
            xsqd = [sbD("xsqd", [128, Tc], F32) for _ in range(2)]
            lntmp2 = [sbD("lntmp2", [128, Tc], F32) for _ in range(2)]
            p_sumD = psD("p_sumD", [1, Tc])
            p_sqD = psD("p_sqD", [1, Tc])
            p_b1d = psD("p_b1d", [128, Tc])
            p_b2d = psD("p_b2d", [128, Tc])
            p_m = [psD("p_m", [128, Tc]) for _ in range(2)]
            for c in range(T // Tc):
                t0 = c * Tc
                p.dma(r1_d[:], io["scr_r1"][:, :, t0:t0 + Tc])
                p.dma(s2_d[:], io["scr_s2"][:, :, t0:t0 + Tc])
                for ft in range(NF):
                    p.t(lambda e, ft=ft: e.matmul(p_sumD[:], ones_bf[:], r1_d[:, ft, :],
                                                  start=(ft == 0), stop=(ft == NF - 1)))
                for ft in range(NF):
                    p.s(lambda e, ft=ft: e.activation(xsqd[ft % 2][:], r1_d[:, ft, :], AF.Square))
                    p.t(lambda e, ft=ft: e.matmul(p_sqD[:], ones_f32[:], xsqd[ft % 2][:],
                                                  start=(ft == 0), stop=(ft == NF - 1)))
                _ln_rows(p, p_sumD[:], p_sqD[:], Tc, 1e-5, rows)
                p.t(lambda e: e.matmul(p_b1d[:], ones1_f32[:], rows[1][0:1, 0:Tc], start=True, stop=True))
                p.t(lambda e: e.matmul(p_b2d[:], ones1_f32[:], rows[2][0:1, 0:Tc], start=True, stop=True))
                for ft in range(NF):
                    lt = lntmp2[ft % 2]
                    p.v(lambda e, ft=ft, lt=lt: e.tensor_tensor(lt[:], r1_d[:, ft, :], p_b1d[:], ALU.mult))
                    p.v(lambda e, lt=lt: e.tensor_tensor(lt[:], lt[:], p_b2d[:], ALU.add))
                    p.v(lambda e, ft=ft, lt=lt: e.tensor_scalar(h_d[:, ft, :], lt[:],
                        ln2g[:, ft:ft + 1], ln2b[:, ft:ft + 1], ALU.mult, ALU.add))
                for mt in range(32):
                    pm = p_m[mt % 2]
                    for kt in range(NF):
                        p.t(lambda e, mt=mt, kt=kt, pm=pm: e.matmul(
                            pm[:], w1[:, kt, mt * 128:(mt + 1) * 128], h_d[:, kt, :],
                            start=(kt == 0), stop=(kt == NF - 1)))
                    p.v(lambda e, mt=mt, pm=pm: e.tensor_scalar(hr[:, mt, :], pm[:], fb1[:, mt:mt + 1],
                                                         0.0, ALU.add, ALU.max))
                for mt in range(NF):
                    pm = p_m[mt % 2]
                    for kt in range(32):
                        p.t(lambda e, mt=mt, kt=kt, pm=pm: e.matmul(
                            pm[:], w2[:, kt, mt * 128:(mt + 1) * 128], hr[:, kt, :],
                            start=(kt == 0), stop=(kt == 31)))
                    p.v(lambda e, mt=mt, pm=pm: e.scalar_tensor_tensor(
                        u_d[:, mt, :], pm[:], fb2[:, mt:mt + 1], s2_d[:, mt, :],
                        ALU.add, ALU.add))
                for ft in range(NF):
                    p.t(lambda e, ft=ft: e.matmul(p_sumD[:], ones_f32[:], u_d[:, ft, :],
                                                  start=(ft == 0), stop=(ft == NF - 1)))
                for ft in range(NF):
                    p.s(lambda e, ft=ft: e.activation(xsqd[ft % 2][:], u_d[:, ft, :], AF.Square))
                    p.t(lambda e, ft=ft: e.matmul(p_sqD[:], ones_f32[:], xsqd[ft % 2][:],
                                                  start=(ft == 0), stop=(ft == NF - 1)))
                _ln_rows(p, p_sumD[:], p_sqD[:], Tc, 1e-5, rows)
                p.t(lambda e: e.matmul(p_b1d[:], ones1_f32[:], rows[1][0:1, 0:Tc], start=True, stop=True))
                p.t(lambda e: e.matmul(p_b2d[:], ones1_f32[:], rows[2][0:1, 0:Tc], start=True, stop=True))
                for ft in range(NF):
                    lt = lntmp2[ft % 2]
                    p.v(lambda e, ft=ft, lt=lt: e.tensor_tensor(lt[:], u_d[:, ft, :], p_b1d[:], ALU.mult))
                    p.v(lambda e, ft=ft, lt=lt: e.tensor_tensor(xo[:, ft, :], lt[:], p_b2d[:], ALU.add))
                p.dma(io["xout"][:, :, t0:t0 + Tc], xo[:, :, :])



def prep_mamba_inputs(xb, w):
    c = np.ascontiguousarray
    DI = 2048
    SEL16 = np.zeros((16, 16 * 128), np.float32)
    for n in range(16):
        SEL16[n, n * 128:(n + 1) * 128] = 1.0
    fm = lambda v, nt: c(v.reshape(nt, 128).T)
    fm3 = lambda m, nt: c(m.reshape(nt, 128, m.shape[-1]).transpose(1, 0, 2))
    return {
        "xT": c(xb.T),
        "inprojT": c(w["L0_attn_in_proj_w"].T),
        "convw": fm3(w["L0_attn_conv_w"][:, 0, :], 16),
        "convb": fm(w["L0_attn_conv_b"], 16),
        "xprojT_dt": fm3(c(w["L0_attn_x_proj_w"][0:64].T), 16),
        "xprojT_B": fm3(c(w["L0_attn_x_proj_w"][64:80].T), 16),
        "xprojT_C": fm3(c(w["L0_attn_x_proj_w"][80:96].T), 16),
        "dtprojT": c(w["L0_attn_dt_proj_w"].T),
        "dtbias": fm(w["L0_attn_dt_proj_b"], 16),
        "Alog_fm": fm3(w["L0_attn_A_log"], 16),
        "D_fm": fm(w["L0_attn_D"], 16),
        "outprojT": c(w["L0_attn_out_proj_w"].T),
        "ln1g": fm(w["L0_attn_ln1_g"], NF),
        "ln1b": fm(w["L0_attn_ln1_b"], NF),
        "ln2g": fm(w["L0_attn_ln2_g"], NF),
        "ln2b": fm(w["L0_attn_ln2_b"], NF),
        "fw1T": c(w["L0_attn_ffn_w1"].T),
        "fb1": fm(w["L0_attn_ffn_b1"], 32),
        "fw2T": c(w["L0_attn_ffn_w2"].T),
        "fb2": fm(w["L0_attn_ffn_b2"], NF),
        "SEL16": SEL16,
    }


def build_mamba_program(T=T_FULL):
    nc = bass.Bass("TRN2", target_bir_lowering=False)
    DI = 2048
    dram = lambda n, s, dt, k: nc.dram_tensor(n, s, dt, kind=k).ap()
    io = {
        "xT": dram("xT", [D, T], F32, "ExternalInput"),
        "inprojT": dram("inprojT", [D, 2 * DI], F32, "ExternalInput"),
        "convw": dram("convw", [128, 16, 4], F32, "ExternalInput"),
        "convb": dram("convb", [128, 16], F32, "ExternalInput"),
        "xprojT_dt": dram("xprojT_dt", [128, 16, 64], F32, "ExternalInput"),
        "xprojT_B": dram("xprojT_B", [128, 16, 16], F32, "ExternalInput"),
        "xprojT_C": dram("xprojT_C", [128, 16, 16], F32, "ExternalInput"),
        "dtprojT": dram("dtprojT", [64, DI], F32, "ExternalInput"),
        "dtbias": dram("dtbias", [128, 16], F32, "ExternalInput"),
        "Alog_fm": dram("Alog_fm", [128, 16, 16], F32, "ExternalInput"),
        "D_fm": dram("D_fm", [128, 16], F32, "ExternalInput"),
        "outprojT": dram("outprojT", [DI, D], F32, "ExternalInput"),
        "ln1g": dram("ln1g", [128, NF], F32, "ExternalInput"),
        "ln1b": dram("ln1b", [128, NF], F32, "ExternalInput"),
        "ln2g": dram("ln2g", [128, NF], F32, "ExternalInput"),
        "ln2b": dram("ln2b", [128, NF], F32, "ExternalInput"),
        "fw1T": dram("fw1T", [D, 4 * D], F32, "ExternalInput"),
        "fb1": dram("fb1", [128, 32], F32, "ExternalInput"),
        "fw2T": dram("fw2T", [4 * D, D], F32, "ExternalInput"),
        "fb2": dram("fb2", [128, NF], F32, "ExternalInput"),
        "SEL16": dram("SEL16", [16, 2048], F32, "ExternalInput"),
        "scr_xc": dram("scr_xc", [128, 16, T], BF16, "Internal"),
        "scr_dt": dram("scr_dt", [128, 16, T], BF16, "Internal"),
        "scr_sz": dram("scr_sz", [128, 16, T], BF16, "Internal"),
        "scr_B": dram("scr_B", [16, T], BF16, "Internal"),
        "scr_C": dram("scr_C", [16, T], BF16, "Internal"),
        "scr_y": dram("scr_y", [128, 16, T], BF16, "Internal"),
        "scr_r1": dram("scr_r1", [128, NF, T], BF16, "Internal"),
        "scr_s2": dram("scr_s2", [128, NF, T], BF16, "Internal"),
        "xout": dram("xout", [128, NF, T], BF16, "ExternalOutput"),
    }
    with ExitStack() as stack:
        p = Prog(nc, stack)
        emit_mamba(p, io, T=T, Tc_a=min(512, T), Tc_b=min(1024, T), Tc_d=min(256, T))
        p.finalize(strict=STRICT)
    return nc





def emit_moe(p, io, T=T_FULL, Tc=512):
    nc = p.nc
    NE, DE = 6, 512
    GC1 = 0.0713548162726, 1.5957691216057308

    with ExitStack() as st:
        sb = lambda n, s, d: st.enter_context(nc.sbuf_tensor(p.uname(n), s, d))
        ps = lambda n, s, d=F32: st.enter_context(nc.psum_tensor(p.uname(n), s, d))

        ones_bf = sb("m_ones_bf", [128, 1], BF16)
        ones_f32 = sb("m_ones_f32", [128, 1], F32)
        ones1_f32 = sb("m_ones1_f32", [1, 128], F32)
        p.v(lambda e: e.memset(ones_bf[:], 1.0))
        p.v(lambda e: e.memset(ones_f32[:], 1.0))
        p.v(lambda e: e.memset(ones1_f32[:], 1.0))
        rows = (sb("m_r0", [1, Tc], F32), sb("m_r1", [1, Tc], F32),
                sb("m_r2", [1, Tc], F32), sb("m_r3", [1, Tc], F32))

        w1 = sb("m_w1", [128, NE, NF, DE], BF16)
        w2 = sb("m_w2", [128, NE, 4, D], BF16)
        for e_ in range(NE):
            p.gdma(w1[:, e_, :, :], io["moe_w1"][:, e_, :, :])
            p.gdma(w2[:, e_, :, :], io["moe_w2"][:, e_, :, :])
        projT = sb("m_projT", [128, NF, NE], BF16)
        p.gdma(projT[:], io["moe_projT"])
        anchT = sb("m_anchT", [NE, NE], F32)
        p.dma(anchT[:], io["moe_anchT"])
        SEL6 = sb("m_SEL6", [NE, NE * 128], BF16)
        p.gdma(SEL6[:], io["SEL6"])
        identf = sb("m_identf", [128, 128], F32)
        p.dma(identf[:], io["IDENTF"])
        temp1 = sb("m_temp1", [1, 1], F32)
        p.dma(temp1[:], io["moe_temp"])
        rthc = sb("m_rthc", [128, 1], F32)
        zeros6 = sb("m_zeros6", [128, NE], F32)
        p.v(lambda e: e.memset(zeros6[:], 0.0))

        p.v(lambda e: e.tensor_single_scalar(temp1[:], temp1[:], 0.1, ALU.max))
        p.v(lambda e: e.reciprocal(temp1[:], temp1[:]))
        p.v(lambda e: e.tensor_scalar_mul(temp1[:], temp1[:], 0.5))
        p_tr = ps("m_p_tr", [128, 128])
        p.t(lambda e: e.matmul(p_tr[:, 0:1], ones1_f32[:], temp1[:], start=True, stop=True))
        p.v(lambda e: e.tensor_copy(rthc[:], p_tr[:, 0:1]))

        x_c = sb("m_x", [128, NF, Tc], BF16)
        qrow = sb("m_qrow", [NE, Tc], F32)
        dotrow = sb("m_dotrow", [NE, Tc], F32)
        dotT = sb("m_dotT", [128, ceil_div(Tc, 128), NE], F32)
        eqa = sb("m_eqa", [128, NE], F32)
        eqb = sb("m_eqb", [128, NE], F32)
        pre = sb("m_pre", [128, NE], F32)
        first1 = sb("m_first1", [128, NE], F32)
        first2 = sb("m_first2", [128, NE], F32)
        m1c = sb("m_m1c", [128, 1], F32)
        m2c = sb("m_m2c", [128, 1], F32)
        wac = sb("m_wac", [128, 1], F32)
        wbc = sb("m_wbc", [128, 1], F32)
        ewT = sb("m_ewT", [128, ceil_div(Tc, 128), NE], F32)
        ew_bf = sb("m_ewbf", [NE, Tc], BF16)
        ewb = [sb("m_ewb", [128, Tc], BF16) for _ in range(2)]
        g1 = [sb("m_g1", [128, Tc], F32) for _ in range(2)]
        g2 = [sb("m_g2", [128, Tc], F32) for _ in range(2)]
        g3 = [sb("m_g3", [128, Tc], F32) for _ in range(2)]
        u_all = sb("m_u", [128, NE, 4, Tc], BF16)
        upre = sb("m_upre", [128, NF, Tc], F32)
        xo = sb("m_xo", [128, NF, Tc], BF16)
        xsq = [sb("m_xsq", [128, Tc], F32) for _ in range(2)]
        lnt = [sb("m_lnt", [128, Tc], F32) for _ in range(2)]

        p_z = ps("m_p_z", [NE, Tc])
        p_dot = ps("m_p_dot", [NE, Tc])
        p_h = [ps("m_p_h", [128, Tc]) for _ in range(2)]
        p_ew = ps("m_p_ew", [128, Tc])
        p_out = [ps("m_p_out", [128, Tc]) for _ in range(2)]
        p_b1 = p_h[0]
        p_b2 = p_h[1]
        p_sum = p_z[0:1, :]
        p_sq = p_dot[0:1, :]

        ntt = Tc // 128
        for c in range(T // Tc):
            t0 = c * Tc
            p.dma(x_c[:], io["xin"][:, :, t0:t0 + Tc])
            for kt in range(NF):
                p.t(lambda e, kt=kt: e.matmul(p_z[:], projT[:, kt, :], x_c[:, kt, :],
                                              start=(kt == 0), stop=(kt == NF - 1)))
            p.s(lambda e: e.activation(qrow[:], p_z[:], AF.Tanh, scale=np.float32(1.0 / 0.3)))
            p.t(lambda e: e.matmul(p_dot[:], anchT[:], qrow[:], start=True, stop=True))
            p.v(lambda e: e.tensor_copy(dotrow[:], p_dot[:]))
            for tt in range(ntt):
                p.t(lambda e, tt=tt: e.transpose(p_tr[:, 0:NE], dotrow[0:NE, tt * 128:(tt + 1) * 128],
                                                 identf[0:NE, 0:NE]))
                p.v(lambda e, tt=tt: e.tensor_copy(dotT[:, tt, :], p_tr[:, 0:NE]))
            for tt in range(ntt):
                d6 = dotT[:, tt, :]
                p.v(lambda e, d6=d6: e.reduce_max(m1c[:], d6, AX.X))
                p.v(lambda e, d6=d6: e.tensor_scalar(eqa[:], d6, m1c[:], None, ALU.is_ge))
                p.v(lambda e: e.tensor_tensor_scan(pre[:], eqa[:], zeros6[:], 0.0, ALU.add, ALU.add))
                p.v(lambda e: e.tensor_single_scalar(pre[:], pre[:], 1.0, ALU.is_equal))
                p.v(lambda e: e.tensor_mul(first1[:], eqa[:], pre[:]))
                p.v(lambda e, d6=d6: e.scalar_tensor_tensor(eqb[:], first1[:], -1e30, d6,
                                                            ALU.mult, ALU.add))
                p.v(lambda e: e.reduce_max(m2c[:], eqb[:], AX.X))
                p.v(lambda e: e.tensor_scalar(eqb[:], eqb[:], m2c[:], None, ALU.is_ge))
                p.v(lambda e: e.tensor_tensor_scan(pre[:], eqb[:], zeros6[:], 0.0, ALU.add, ALU.add))
                p.v(lambda e: e.tensor_single_scalar(pre[:], pre[:], 1.0, ALU.is_equal))
                p.v(lambda e: e.tensor_mul(first2[:], eqb[:], pre[:]))
                p.v(lambda e: e.tensor_sub(m1c[:], m1c[:], m2c[:]))
                p.s(lambda e: e.activation(wac[:], m1c[:], AF.Sigmoid, scale=rthc[:]))
                p.v(lambda e: e.tensor_scalar(wbc[:], wac[:], -1.0, 1.0, ALU.mult, ALU.add))
                p.v(lambda e: e.tensor_scalar(first1[:], first1[:], wac[:], None, ALU.mult))
                p.v(lambda e, tt=tt: e.scalar_tensor_tensor(ewT[:, tt, :], first2[:], wbc[:],
                                                            first1[:], ALU.mult, ALU.add))
            for tt in range(ntt):
                p.t(lambda e, tt=tt: e.transpose(p_tr[0:NE, 0:128], ewT[:, tt, :], identf[:]))
                p.v(lambda e, tt=tt: e.tensor_copy(ew_bf[:, tt * 128:(tt + 1) * 128],
                                                   p_tr[0:NE, 0:128]))
            for e_ in range(NE):
                eb = ewb[e_ % 2]
                p.t(lambda e, e_=e_: e.matmul(p_ew[:], SEL6[:, e_ * 128:(e_ + 1) * 128],
                                              ew_bf[:], start=True, stop=True))
                p.v(lambda e, eb=eb: e.tensor_copy(eb[:], p_ew[:]))
                for jt in range(4):
                    par = (e_ * 4 + jt) % 2
                    ph, a1, a2, a3 = p_h[par], g1[par], g2[par], g3[par]
                    for kt in range(NF):
                        p.t(lambda e, e_=e_, jt=jt, kt=kt, ph=ph: e.matmul(
                            ph[:], w1[:, e_, kt, jt * 128:(jt + 1) * 128], x_c[:, kt, :],
                            start=(kt == 0), stop=(kt == NF - 1)))
                    p.s(lambda e, ph=ph, a1=a1: e.activation(a1[:], ph[:], AF.Square))
                    p.v(lambda e, a1=a1: e.tensor_scalar(a1[:], a1[:], GC1[0], GC1[1], ALU.mult, ALU.add))
                    p.v(lambda e, ph=ph, a1=a1, a2=a2: e.tensor_tensor(a2[:], a1[:], ph[:], ALU.mult))
                    p.s(lambda e, a2=a2: e.activation(a2[:], a2[:], AF.Sigmoid))
                    p.v(lambda e, ph=ph, a2=a2, a3=a3: e.tensor_tensor(a3[:], a2[:], ph[:], ALU.mult))
                    p.op("gpsimd", lambda e, e_=e_, jt=jt, a3=a3, eb=eb: e.tensor_tensor(
                        u_all[:, e_, jt, :], a3[:], eb[:], ALU.mult))
            for mt in range(NF):
                po = p_out[mt % 2]
                first = True
                for e_ in range(NE):
                    for jt in range(4):
                        p.t(lambda e, e_=e_, jt=jt, mt=mt, first=first, po=po: e.matmul(
                            po[:], w2[:, e_, jt, mt * 128:(mt + 1) * 128], u_all[:, e_, jt, :],
                            start=first, stop=(e_ == NE - 1 and jt == 3)))
                        first = False
                p.v(lambda e, mt=mt, po=po: e.tensor_tensor(upre[:, mt, :], x_c[:, mt, :], po[:],
                                                     ALU.add))
            for ft in range(NF):
                p.t(lambda e, ft=ft: e.matmul(p_sum, ones_f32[:], upre[:, ft, :],
                                              start=(ft == 0), stop=(ft == NF - 1)))
            for ft in range(NF):
                p.s(lambda e, ft=ft: e.activation(xsq[ft % 2][:], upre[:, ft, :], AF.Square))
                p.t(lambda e, ft=ft: e.matmul(p_sq, ones_f32[:], xsq[ft % 2][:],
                                              start=(ft == 0), stop=(ft == NF - 1)))
            _ln_rows(p, p_sum, p_sq, Tc, 1e-5, rows)
            p.t(lambda e: e.matmul(p_b1[:], ones1_f32[:], rows[1][0:1, 0:Tc], start=True, stop=True))
            p.t(lambda e: e.matmul(p_b2[:], ones1_f32[:], rows[2][0:1, 0:Tc], start=True, stop=True))
            for ft in range(NF):
                lt = lnt[ft % 2]
                p.v(lambda e, ft=ft, lt=lt: e.tensor_tensor(lt[:], upre[:, ft, :], p_b1[:], ALU.mult))
                p.v(lambda e, ft=ft, lt=lt: e.tensor_tensor(xo[:, ft, :], lt[:], p_b2[:], ALU.add))
            p.dma(io["xout"][:, :, t0:t0 + Tc], xo[:, :, :])


def prep_moe_inputs(w):
    c = np.ascontiguousarray
    SEL6 = np.zeros((6, 6 * 128), np.float32)
    for n in range(6):
        SEL6[n, n * 128:(n + 1) * 128] = 1.0
    return {
        "moe_w1": c(w["L0_ffn_w1"].reshape(6, 8, 128, 512).transpose(2, 0, 1, 3)),
        "moe_w2": c(w["L0_ffn_w2"].reshape(6, 4, 128, 1024).transpose(2, 0, 1, 3)),
        "moe_projT": c(w["L0_ffn_proj_w"].T.reshape(8, 128, 6).transpose(1, 0, 2)),
        "moe_anchT": c(w["L0_ffn_anchors"].T),
        "moe_temp": np.asarray(w["L0_ffn_routing_temp"], np.float32).reshape(1, 1),
        "SEL6": SEL6,
        "IDENTF": np.eye(128, dtype=np.float32),
    }


def build_moe_program(T=T_FULL):
    nc = bass.Bass("TRN2", target_bir_lowering=False)
    dram = lambda n, s, dt, k: nc.dram_tensor(n, s, dt, kind=k).ap()
    io = {
        "xin": dram("xin", [128, NF, T], BF16, "ExternalInput"),
        "moe_w1": dram("moe_w1", [128, 6, NF, 512], F32, "ExternalInput"),
        "moe_w2": dram("moe_w2", [128, 6, 4, D], F32, "ExternalInput"),
        "moe_projT": dram("moe_projT", [128, NF, 6], F32, "ExternalInput"),
        "moe_anchT": dram("moe_anchT", [6, 6], F32, "ExternalInput"),
        "moe_temp": dram("moe_temp", [1, 1], F32, "ExternalInput"),
        "SEL6": dram("SEL6", [6, 768], F32, "ExternalInput"),
        "IDENTF": dram("IDENTF", [128, 128], F32, "ExternalInput"),
        "xout": dram("xout", [128, NF, T], BF16, "ExternalOutput"),
    }
    with ExitStack() as stack:
        p = Prog(nc, stack)
        emit_moe(p, io, T=T, Tc=min(512, T))
        p.finalize(strict=STRICT)
    return nc




def emit_attn(p, io, T=T_FULL, Tc=512):
    nc = p.nc
    NH, DH = 16, 64
    nq = T // Tc

    with ExitStack() as st:
        sb = lambda n, s, d: st.enter_context(nc.sbuf_tensor(p.uname(n), s, d))
        ps = lambda n, s, d=F32: st.enter_context(nc.psum_tensor(p.uname(n), s, d))

        ones_f32 = sb("a_ones_f32", [128, 1], F32)
        ones1_f32 = sb("a_ones1_f32", [1, 128], F32)
        p.v(lambda e: e.memset(ones_f32[:], 1.0))
        p.v(lambda e: e.memset(ones1_f32[:], 1.0))
        rows = (sb("a_r0", [1, Tc], F32), sb("a_r1", [1, Tc], F32),
                sb("a_r2", [1, Tc], F32), sb("a_r3", [1, Tc], F32))

        x_res = sb("a_x", [128, NF, T], BF16)
        p.dma(x_res[:], io["xin"][:, :, :])
        qwT = sb("a_qwT", [128, NF, D], BF16)
        kwT = sb("a_kwT", [128, NF, D], BF16)
        vwT = sb("a_vwT", [128, NF, D], BF16)
        for kt in range(NF):
            p.gdma(qwT[:, kt, :], io["qwT"][kt * 128:(kt + 1) * 128, :])
            p.gdma(kwT[:, kt, :], io["kwT"][kt * 128:(kt + 1) * 128, :])
            p.gdma(vwT[:, kt, :], io["vwT"][kt * 128:(kt + 1) * 128, :])
        qb_c = sb("a_qb", [64, NH], F32)
        kb_c = sb("a_kb", [64, NH], F32)
        p.dma(qb_c[:], io["qb_c"])
        p.dma(kb_c[:], io["kb_c"])
        vb_rep = sb("a_vbrep", [128, NH, DH], BF16)
        p.gdma(vb_rep[:], io["vb_rep"])

        sh = ExitStack()
        psh = lambda n, s_, d=F32: sh.enter_context(nc.psum_tensor(p.uname(n), s_, d))
        sbh = lambda n, s_, d: sh.enter_context(nc.sbuf_tensor(p.uname(n), s_, d))
        Q_h = sbh("a_Q", [64, T], BF16)
        K_h = sbh("a_K", [64, T], BF16)
        V_h = sbh("a_V", [128, T // 128, DH + 1], BF16)
        qsq = [sbh("a_qsq", [64, Tc], F32) for _ in range(2)]
        mrow = sbh("a_mrow", [1, 2 * (T // Tc)], F32)
        bscal = sbh("a_bscal", [1, 2], F32)
        negb = sbh("a_negb", [128, 1], F32)
        P_T = [sbh("a_PT", [128, Tc], BF16) for _ in range(2)]
        rb = [sbh("a_rb", [64, Tc], F32) for _ in range(2)]
        ctx_bf = [sbh("a_ctx", [64, Tc], BF16) for _ in range(2)]
        rrec = [sbh("a_rrec", [1, Tc], F32) for _ in range(2)]
        p_kq = [psh("a_p_kq", [64, Tc]) for _ in range(2)]
        p_v2t = psh("a_p_v", [128, 2, DH])
        p_v = [p_v2t[:, 0, :], p_v2t[:, 1, :]]
        p_n = psh("a_p_n", [1, Tc])
        p_s = [psh("a_p_s", [128, Tc]) for _ in range(2)]
        p_ctx = [psh("a_p_ctx", [65, Tc]) for _ in range(2)]

        p.v(lambda e: e.memset(V_h[:, :, DH:DH + 1], 1.0))

        nchunk = T // Tc
        for h in range(NH):
            hc = slice(h * DH, (h + 1) * DH)
            for c in range(nchunk):
                t0 = c * Tc
                pk, pq = p_kq[0], p_kq[1]
                for kt in range(NF):
                    p.t(lambda e, kt=kt, t0=t0, hc=hc, pk=pk: e.matmul(
                        pk[:], kwT[:, kt, hc], x_res[:, kt, t0:t0 + Tc],
                        start=(kt == 0), stop=(kt == NF - 1)))
                p.v(lambda e, t0=t0, h=h, pk=pk: e.tensor_scalar(
                    K_h[:, t0:t0 + Tc], pk[:], kb_c[:, h:h + 1], None, ALU.add))
                for kt in range(NF):
                    p.t(lambda e, kt=kt, t0=t0, hc=hc, pq=pq: e.matmul(
                        pq[:], qwT[:, kt, hc], x_res[:, kt, t0:t0 + Tc],
                        start=(kt == 0), stop=(kt == NF - 1)))
                p.v(lambda e, t0=t0, h=h, pq=pq: e.tensor_scalar(
                    Q_h[:, t0:t0 + Tc], pq[:], qb_c[:, h:h + 1], None, ALU.add))
            for tt in range(T // 128):
                pv = p_v[tt % 2]
                for kt in range(NF):
                    p.t(lambda e, kt=kt, tt=tt, hc=hc, pv=pv: e.matmul(
                        pv[:], x_res[:, kt, tt * 128:(tt + 1) * 128], vwT[:, kt, hc],
                        start=(kt == 0), stop=(kt == NF - 1)))
                p.v(lambda e, tt=tt, h=h, pv=pv: e.tensor_tensor(
                    V_h[:, tt, 0:DH], pv[:], vb_rep[:, h, :], ALU.add))
            for (src, off) in ((Q_h, 0), (K_h, nchunk)):
                for c in range(nchunk):
                    t0 = c * Tc
                    qs_ = qsq[c % 2]
                    p.s(lambda e, src=src, t0=t0, qs_=qs_: e.activation(qs_[:], src[:, t0:t0 + Tc], AF.Square))
                    p.t(lambda e, qs_=qs_: e.matmul(p_n[:], ones_f32[0:64, :], qs_[:], start=True, stop=True))
                    p.v(lambda e, c=c, off=off: e.reduce_max(mrow[0:1, off + c:off + c + 1],
                                                             p_n[:], AX.X))
            p.v(lambda e: e.reduce_max(bscal[0:1, 0:1], mrow[0:1, 0:nchunk], AX.X))
            p.v(lambda e: e.reduce_max(bscal[0:1, 1:2], mrow[0:1, nchunk:2 * nchunk], AX.X))
            p.v(lambda e: e.tensor_mul(bscal[0:1, 0:1], bscal[0:1, 0:1], bscal[0:1, 1:2]))
            p.s(lambda e: e.activation(bscal[0:1, 0:1], bscal[0:1, 0:1], AF.Ln))
            p.s(lambda e: e.activation(bscal[0:1, 0:1], bscal[0:1, 0:1], AF.Exp, scale=np.float32(0.5)))
            p.v(lambda e: e.tensor_scalar_mul(bscal[0:1, 0:1], bscal[0:1, 0:1], -0.125))
            p.t(lambda e: e.matmul(p_s[0][:, 0:1], ones1_f32[:], bscal[0:1, 0:1], start=True, stop=True))
            p.v(lambda e: e.tensor_copy(negb[:], p_s[0][:, 0:1]))
            for c in range(nchunk):
                t0 = c * Tc
                pc = p_ctx[c % 2]
                nkt = T // 128
                pend = None
                for kt in range(nkt):
                    ps_, pt_ = p_s[kt % 2], P_T[kt % 2]
                    p.t(lambda e, kt=kt, t0=t0, ps_=ps_: e.matmul(
                        ps_[:], K_h[:, kt * 128:(kt + 1) * 128], Q_h[:, t0:t0 + Tc],
                        start=True, stop=True))
                    p.s(lambda e, ps_=ps_, pt_=pt_: e.activation(pt_[:], ps_[:], AF.Exp,
                                               scale=np.float32(0.125), bias=negb[:]))
                    if pend is not None:
                        jkt, jpt = pend
                        p.t(lambda e, jkt=jkt, jpt=jpt, pc=pc: e.matmul(
                            pc[:], V_h[:, jkt, :], jpt[:],
                            start=(jkt == 0), stop=False))
                    pend = (kt, pt_)
                jkt, jpt = pend
                p.t(lambda e, jkt=jkt, jpt=jpt, pc=pc: e.matmul(
                    pc[:], V_h[:, jkt, :], jpt[:], start=(jkt == 0), stop=True))
                rr = rrec[c % 2]
                p.s(lambda e, pc=pc, rr=rr: e.activation(rr[0:1, 0:Tc], pc[64:65, :], AF.Ln))
                p.s(lambda e, rr=rr: e.activation(rr[0:1, 0:Tc], rr[0:1, 0:Tc], AF.Exp,
                                                  scale=np.float32(-1.0)))
                pk2 = p_kq[c % 2]
                p.t(lambda e, rr=rr, pk2=pk2: e.matmul(pk2[:], ones1_f32[0:1, 0:64], rr[0:1, 0:Tc],
                                       start=True, stop=True))
                rbc = rb[c % 2]
                p.v(lambda e, rbc=rbc, pk2=pk2: e.tensor_copy(rbc[:], pk2[:]))
                cbc = ctx_bf[c % 2]
                p.v(lambda e, pc=pc, rbc=rbc, cbc=cbc: e.tensor_tensor(cbc[:], pc[0:64, :], rbc[:], ALU.mult))
                p.dma(io["scr_ctx"][(h % 2) * 64:(h % 2) * 64 + 64, h // 2, t0:t0 + Tc], cbc[:])

        sh.close()
        with ExitStack() as st2:
            sb2 = lambda n, s, d: st2.enter_context(nc.sbuf_tensor(p.uname(n), s, d))
            outT = sb2("a_outT", [128, NF, D], BF16)
            for kt in range(NF):
                p.gdma(outT[:, kt, :], io["outwT"][kt * 128:(kt + 1) * 128, :])
            outb = sb2("a_outb", [128, NF], F32)
            p.dma(outb[:], io["outb_c"])
            ctx_c = sb2("a_ctxc", [128, NF, Tc], BF16)
            upre = sb2("a_upre", [128, NF, Tc], F32)
            xo = sb2("a_xo", [128, NF, Tc], BF16)
            xsq2 = [sb2("a_xsq2", [128, Tc], F32) for _ in range(2)]
            lnt = [sb2("a_lnt", [128, Tc], F32) for _ in range(2)]
            ps2 = lambda n, s_, d=F32: st2.enter_context(nc.psum_tensor(p.uname(n), s_, d))
            p_o = [ps2("a_p_o", [128, Tc]) for _ in range(2)]
            p_sum = ps2("a_p_sum2", [1, Tc])
            p_sq = ps2("a_p_sq2", [1, Tc])
            p_b1f = ps2("a_p_b1f", [128, Tc])
            p_b2f = ps2("a_p_b2f", [128, Tc])
            for c in range(nq):
                t0 = c * Tc
                p.dma(ctx_c[:], io["scr_ctx"][:, :, t0:t0 + Tc])
                for mt in range(NF):
                    po = p_o[mt % 2]
                    for kt in range(NF):
                        p.t(lambda e, mt=mt, kt=kt, po=po: e.matmul(
                            po[:], outT[:, kt, mt * 128:(mt + 1) * 128], ctx_c[:, kt, :],
                            start=(kt == 0), stop=(kt == NF - 1)))
                    p.v(lambda e, mt=mt, t0=t0, po=po: e.scalar_tensor_tensor(
                        upre[:, mt, :], po[:], outb[:, mt:mt + 1], x_res[:, mt, t0:t0 + Tc],
                        ALU.add, ALU.add))
                for ft in range(NF):
                    p.t(lambda e, ft=ft: e.matmul(p_sum[:], ones_f32[:], upre[:, ft, :],
                                                  start=(ft == 0), stop=(ft == NF - 1)))
                for ft in range(NF):
                    p.s(lambda e, ft=ft: e.activation(xsq2[ft % 2][:], upre[:, ft, :], AF.Square))
                    p.t(lambda e, ft=ft: e.matmul(p_sq[:], ones_f32[:], xsq2[ft % 2][:],
                                                  start=(ft == 0), stop=(ft == NF - 1)))
                _ln_rows(p, p_sum[:], p_sq[:], Tc, 1e-5, rows)
                p.t(lambda e: e.matmul(p_b1f[:], ones1_f32[:], rows[1][0:1, 0:Tc], start=True, stop=True))
                p.t(lambda e: e.matmul(p_b2f[:], ones1_f32[:], rows[2][0:1, 0:Tc], start=True, stop=True))
                for ft in range(NF):
                    lt = lnt[ft % 2]
                    p.v(lambda e, ft=ft, lt=lt: e.tensor_tensor(lt[:], upre[:, ft, :], p_b1f[:], ALU.mult))
                    p.v(lambda e, ft=ft, lt=lt: e.tensor_tensor(xo[:, ft, :], lt[:], p_b2f[:], ALU.add))
                p.dma(io["xout"][:, :, t0:t0 + Tc], xo[:, :, :])


def prep_attn_inputs(w):
    c = np.ascontiguousarray
    qb = w["L1_attn_q_b"]
    kb = w["L1_attn_k_b"]
    vb = w["L1_attn_v_b"]
    return {
        "qwT": c(w["L1_attn_q_w"].T),
        "kwT": c(w["L1_attn_k_w"].T),
        "vwT": c(w["L1_attn_v_w"].T),
        "qb_c": c(qb.reshape(16, 64).T),
        "kb_c": c(kb.reshape(16, 64).T),
        "vb_rep": c(np.broadcast_to(vb.reshape(1, 16, 64), (128, 16, 64))),
        "outwT": c(w["L1_attn_out_w"].T),
        "outb_c": c(w["L1_attn_out_b"].reshape(8, 128).T),
    }


def build_attn_program(T=T_FULL):
    nc = bass.Bass("TRN2", target_bir_lowering=False)
    dram = lambda n, s, dt, k: nc.dram_tensor(n, s, dt, kind=k).ap()
    io = {
        "xin": dram("xin", [128, NF, T], BF16, "ExternalInput"),
        "qwT": dram("qwT", [D, D], F32, "ExternalInput"),
        "kwT": dram("kwT", [D, D], F32, "ExternalInput"),
        "vwT": dram("vwT", [D, D], F32, "ExternalInput"),
        "qb_c": dram("qb_c", [64, 16], F32, "ExternalInput"),
        "kb_c": dram("kb_c", [64, 16], F32, "ExternalInput"),
        "vb_rep": dram("vb_rep", [128, 16, 64], F32, "ExternalInput"),
        "outwT": dram("outwT", [D, D], F32, "ExternalInput"),
        "outb_c": dram("outb_c", [128, NF], F32, "ExternalInput"),
        "scr_ctx": dram("scr_ctx", [128, NF, T], BF16, "Internal"),
        "xout": dram("xout", [128, NF, T], BF16, "ExternalOutput"),
    }
    with ExitStack() as stack:
        p = Prog(nc, stack)
        emit_attn(p, io, T=T, Tc=min(512, T))
        p.finalize(strict=STRICT)
    return nc




def emit_l1ffn_layer(p, io, li, T=T_FULL, Tc=512, Tcf=512):
    nc = p.nc
    NH, DH, W = 16, 64, 256
    nw = T // W
    nwp = nw // 2
    key = lambda nm: io[f"l{li}_{nm}"]

    with ExitStack() as st:
        sb = lambda n, s, d: st.enter_context(nc.sbuf_tensor(p.uname(n), s, d))
        ps = lambda n, s, d=F32: st.enter_context(nc.psum_tensor(p.uname(n), s, d))
        ones_f32 = sb("f_ones_f32", [128, 1], F32)
        ones_bf = sb("f_ones_bf", [128, 1], BF16)
        ones1_f32 = sb("f_ones1_f32", [1, 128], F32)
        p.v(lambda e: e.memset(ones_f32[:], 1.0))
        p.v(lambda e: e.memset(ones_bf[:], 1.0))
        p.v(lambda e: e.memset(ones1_f32[:], 1.0))
        rows = (sb("f_r0", [1, max(Tc, Tcf)], F32), sb("f_r1", [1, max(Tc, Tcf)], F32),
                sb("f_r2", [1, max(Tc, Tcf)], F32), sb("f_r3", [1, max(Tc, Tcf)], F32))

        sx = ExitStack()
        x = sx.enter_context(nc.sbuf_tensor(p.uname("f_x"), [128, NF, T + 1], BF16))
        p.v(lambda e: e.memset(x[:, :, 0:1], 0.0))
        p.dma(x[:, :, 1:T + 1], io["xin"][:, :, :])
        xs = [x[:, kt, 1:T + 1] if kt < 4 else x[:, kt, 0:T] for kt in range(NF)]

        with ExitStack() as sa:
            sbA = lambda n, s, d: sa.enter_context(nc.sbuf_tensor(p.uname(n), s, d))
            psA = lambda n, s, d=F32: sa.enter_context(nc.psum_tensor(p.uname(n), s, d))

            wqkv = sbA("f_wqkv", [128, NF, 3 * D], BF16)
            for kt in range(NF):
                p.gdma(wqkv[:, kt, :], key("WqkvT")[kt * 128:(kt + 1) * 128, :])
            CS2 = sbA("f_CS2", [128, T], BF16)
            SS2 = sbA("f_SS2", [128, T], BF16)
            CI2 = sbA("f_CI2", [128, T], BF16)
            SI2 = sbA("f_SI2", [128, T], BF16)
            p.gdma(CS2[:], io["CS2"])
            p.gdma(SS2[:], io["SS2"])
            p.gdma(CI2[:], io["CI2"])
            p.gdma(SI2[:], io["SI2"])
            MASK6 = sbA("f_MASK6", [128, 6, 2 * W], BF16)
            p.gdma(MASK6[:], io["MASK6"])
            HONES = sbA("f_HONES", [128, 2], BF16)
            p.gdma(HONES[:], io["HONES"])
            SEL2 = sbA("f_SEL2", [2, 128], BF16)
            p.gdma(SEL2[:], io["SEL2"])
            qs2 = sbA("f_qs2", [128, 1], F32)
            ks2 = sbA("f_ks2", [128, 1], F32)
            p.dma(qs2[:], key("qs2"))
            p.dma(ks2[:], key("ks2"))
            qs_row = sbA("f_qs_row", [1, DH], BF16)
            ks_row = sbA("f_ks_row", [1, DH], BF16)
            p.gdma(qs_row[:], key("qs_row"))
            p.gdma(ks_row[:], key("ks_row"))
            negb = sbA("f_negb", [128, 1], F32)
            bs = sbA("f_bs", [1, 2], F32)
            p_qk1 = psA("f_p_qk", [128, Tc])
            p_qk = [p_qk1, p_qk1]
            p_rb2 = psA("f_p_rb2", [128, Tc])
            p_v2 = psA("f_p_v2", [128, 256])
            p_sl = [psA("f_p_sl", [128, 2 * W]) for _ in range(3)]
            p_cl = [psA("f_p_cl", [65, 2 * W]) for _ in range(2)]
            p_n2 = p_rb2[0:2, :]
            p_rbl = p_rb2[0:64, :]
            p_misc = p_sl[0]

            p.v(lambda e: e.reduce_max(bs[0:1, 0:1], qs_row[:], AX.X, apply_absolute_value=True))
            p.v(lambda e: e.reduce_max(bs[0:1, 1:2], ks_row[:], AX.X, apply_absolute_value=True))
            p.v(lambda e: e.tensor_mul(bs[0:1, 0:1], bs[0:1, 0:1], bs[0:1, 1:2]))
            p.v(lambda e: e.tensor_scalar_mul(bs[0:1, 0:1], bs[0:1, 0:1], -8.0))
            p.t(lambda e: e.matmul(p_misc[:, 0:1], ones1_f32[:], bs[0:1, 0:1], start=True, stop=True))
            p.v(lambda e: e.tensor_copy(negb[:], p_misc[:, 0:1]))

            qraw_s = sbA("f_qraw", [128, Tc], BF16)
            qraw = [qraw_s, qraw_s]
            qn = [sbA("f_qn", [128, Tc], BF16) for _ in range(2)]
            rotq_s = sbA("f_rotq", [128, Tc], BF16)
            rotq = [rotq_s, rotq_s]
            qp = sbA("f_qp", [128, T], BF16)
            kp = sbA("f_kp", [128, T], BF16)
            V2 = sbA("f_V2", [128, T // 128, 260], BF16)
            sqf_s = sbA("f_sqf", [128, Tc], BF16)
            sqf = [sqf_s, sqf_s]
            rf = sbA("f_rf", [2, Tc], F32)
            rrow = [sbA("f_rrow", [2, Tc], BF16) for _ in range(2)]
            ones1_bf = sbA("f_ones1bf", [1, 128], BF16)
            p.v(lambda e: e.memset(ones1_bf[:], 1.0))

            P_l = [sbA("f_P", [128, 2 * W], BF16) for _ in range(3)]
            rbl_s = sbA("f_rbl", [64, 2 * W], BF16)
            rbl = [rbl_s, rbl_s]
            den_s = sbA("f_den", [1, 2 * W], BF16)
            den = [den_s, den_s]
            ctxl_s = sbA("f_ctxl", [64, 2 * W], BF16)
            ctxl = [ctxl_s, ctxl_s]

            for sl4 in range(4):
                p.v(lambda e, sl4=sl4: e.memset(V2[:, :, sl4 * 65 + 64:sl4 * 65 + 65], 1.0))

            nchunk = T // Tc
            for hp in range(NH // 2):
                h0 = 2 * hp
                if hp % 2 == 0:
                    for tt in range(T // 128):
                        for kt in range(NF):
                            p.t(lambda e, kt=kt, tt=tt, h0=h0: e.matmul(
                                p_v2[:], xs[kt][:, tt * 128:(tt + 1) * 128],
                                wqkv[:, kt, 2 * D + h0 * DH:2 * D + h0 * DH + 256],
                                start=(kt == 0), stop=(kt == NF - 1)))
                        for sl4 in range(4):
                            p.v(lambda e, tt=tt, sl4=sl4: e.tensor_copy(
                                V2[:, tt, sl4 * 65:sl4 * 65 + 64],
                                p_v2[:, sl4 * 64:sl4 * 64 + 64]))
                for (base, dstp, scol, CT, ST) in (
                        (0, qp, qs2, CS2, SS2), (D, kp, ks2, CI2, SI2)):
                    cc = slice(base + h0 * DH, base + h0 * DH + 128)
                    raw = qraw
                    for c in range(nchunk):
                        t0 = c * Tc
                        sl = slice(t0, t0 + Tc)
                        pqk, rw_, sq_, rr_, qn_, rq_ = (p_qk[c % 2], raw[c % 2],
                            sqf[c % 2], rrow[c % 2], qn[c % 2], rotq[c % 2])
                        for kt in range(NF):
                            p.t(lambda e, kt=kt, t0=t0, cc=cc, pqk=pqk: e.matmul(
                                pqk[:], wqkv[:, kt, cc], xs[kt][:, t0:t0 + Tc],
                                start=(kt == 0), stop=(kt == NF - 1)))
                        p.v(lambda e, rw_=rw_, pqk=pqk: e.tensor_copy(rw_[:], pqk[:]))
                        p.s(lambda e, rw_=rw_, sq_=sq_: e.activation(sq_[:], rw_[:], AF.Square))
                        p.t(lambda e, sq_=sq_: e.matmul(p_n2[:], HONES[:], sq_[:], start=True, stop=True))
                        p.v(lambda e: e.tensor_single_scalar(rf[:], p_n2[:], 1e-24, ALU.max))
                        p.s(lambda e: e.activation(rf[:], rf[:], AF.Ln))
                        p.s(lambda e, rr_=rr_: e.activation(rr_[:], rf[:], AF.Exp, scale=np.float32(-0.5)))
                        p.t(lambda e, rr_=rr_: e.matmul(p_rb2[:], SEL2[:], rr_[:],
                                               start=True, stop=True))
                        p.v(lambda e, rw_=rw_, qn_=qn_, scol=scol: e.scalar_tensor_tensor(
                            qn_[:], rw_[:], scol[:], p_rb2[:], ALU.mult, ALU.mult))
                        p.v(lambda e, qn_=qn_, rq_=rq_: e.tensor_scalar_mul(rq_[0:32, :], qn_[32:64, :], -1.0))
                        p.v(lambda e, qn_=qn_, rq_=rq_: e.tensor_copy(rq_[32:64, :], qn_[0:32, :]))
                        p.v(lambda e, qn_=qn_, rq_=rq_: e.tensor_scalar_mul(rq_[64:96, :], qn_[96:128, :], -1.0))
                        p.v(lambda e, qn_=qn_, rq_=rq_: e.tensor_copy(rq_[96:128, :], qn_[64:96, :]))
                        p.op("gpsimd", lambda e, dstp=dstp, CT=CT, sl=sl, qn_=qn_: e.tensor_tensor(
                            dstp[:, sl], qn_[:], CT[:, sl], ALU.mult))
                        p.v(lambda e, ST=ST, sl=sl, rq_=rq_: e.tensor_tensor(rq_[:], rq_[:], ST[:, sl], ALU.mult))
                        p.v(lambda e, dstp=dstp, sl=sl, rq_=rq_: e.tensor_tensor(
                            dstp[:, sl], dstp[:, sl], rq_[:], ALU.add))
                for hh in range(2):
                    h = h0 + hh
                    hs = slice(hh * 64, hh * 64 + 64)
                    vs = slice((hp % 2) * 130 + hh * 65, (hp % 2) * 130 + hh * 65 + 65)
                    for wp in range(nwp):
                        q0 = wp * 2 * W
                        kts = range(2, 6) if wp == 0 else range(6)
                        kl = list(kts)
                        pcw = p_cl[wp % 2]
                        pend = []

                        def ctx_emit(pcw=pcw, vs=vs):
                            jgt, jpp, jfirst, jlast = pend.pop(0)
                            p.t(lambda e, jgt=jgt, jpp=jpp, jfirst=jfirst, jlast=jlast: e.matmul(
                                pcw[:], V2[:, jgt, vs], jpp[:],
                                start=jfirst, stop=jlast))

                        for idx, kt in enumerate(kl):
                            gt = (4 * wp - 2) + kt
                            psl, pp = p_sl[idx % 3], P_l[idx % 3]
                            p.t(lambda e, gt=gt, q0=q0, hs=hs, psl=psl: e.matmul(
                                psl[:], kp[hs, gt * 128:(gt + 1) * 128], qp[hs, q0:q0 + 2 * W],
                                start=True, stop=True))
                            p.v(lambda e, kt=kt, psl=psl: e.tensor_tensor(psl[:], psl[:],
                                                                 MASK6[:, kt, :], ALU.add))
                            p.s(lambda e, psl=psl, pp=pp: e.activation(pp[:], psl[:], AF.Exp,
                                                       scale=np.float32(8.0), bias=negb[:]))
                            pend.append((gt, pp, idx == 0, idx == len(kl) - 1))
                            if len(pend) > 2:
                                ctx_emit()
                        while pend:
                            ctx_emit()
                        rl, cl_, dn = rbl[wp % 2], ctxl[wp % 2], den[wp % 2]
                        p.s(lambda e, pcw=pcw: e.activation(rows[3][0:1, 0:2 * W], pcw[64:65, :], AF.Ln))
                        p.s(lambda e, dn=dn: e.activation(dn[:], rows[3][0:1, 0:2 * W],
                                                   AF.Exp, scale=np.float32(-1.0)))
                        p.t(lambda e, dn=dn: e.matmul(p_rbl[:], ones1_bf[0:1, 0:64],
                                               dn[:], start=True, stop=True))
                        p.v(lambda e, rl=rl: e.tensor_copy(rl[:], p_rbl[:]))
                        p.v(lambda e, pcw=pcw, rl=rl, cl_=cl_: e.tensor_tensor(cl_[:], pcw[0:64, :], rl[:], ALU.mult))
                        p.dma(io["scr_ctx"][(h % 2) * 64:(h % 2) * 64 + 64, h // 2, q0:q0 + 2 * W],
                              cl_[:])
        with ExitStack() as so:
            sbO = lambda n, s, d: so.enter_context(nc.sbuf_tensor(p.uname(n), s, d))
            p_wo = [so.enter_context(nc.psum_tensor(p.uname("f_p_wo"), [128, Tc], F32)) for _ in range(2)]
            wo = sbO("f_wo", [128, NF, D], BF16)
            for kt in range(NF):
                p.gdma(wo[:, kt, :], key("WoT")[kt * 128:(kt + 1) * 128, :])
            ctx_c = sbO("f_ctxc", [128, NF, Tc], BF16)
            for c in range(T // Tc):
                t0 = c * Tc
                p.dma(ctx_c[:], io["scr_ctx"][:, :, t0:t0 + Tc])
                for mt in range(NF):
                    pw = p_wo[mt % 2]
                    for kt in range(NF):
                        p.t(lambda e, mt=mt, kt=kt, pw=pw: e.matmul(
                            pw[:], wo[:, kt, mt * 128:(mt + 1) * 128], ctx_c[:, kt, :],
                            start=(kt == 0), stop=(kt == NF - 1)))
                    p.v(lambda e, mt=mt, t0=t0, pw=pw: e.tensor_tensor(
                        x[:, mt, 1 + t0:1 + t0 + Tc], x[:, mt, 1 + t0:1 + t0 + Tc],
                        pw[:], ALU.add))
            p.dma(io["scr_x"][:, :, :], x[:, :, :])
        sx.close()

        with ExitStack() as sf:
            sbF = lambda n, s, d: sf.enter_context(nc.sbuf_tensor(p.uname(n), s, d))
            psF = lambda n, s, d=F32: sf.enter_context(nc.psum_tensor(p.uname(n), s, d))
            NJ = 22
            w1 = sbF("f_w1", [128, NF, 5460], BF16)
            for kt in range(NF):
                p.gdma(w1[:, kt, :], key("W1")[kt * 128:(kt + 1) * 128, :])
            w2 = sbF("f_w2", [128, NJ, D], BF16)
            for kt in range(NJ):
                p.gdma(w2[:, kt, :], key("W2p")[kt * 128:(kt + 1) * 128, :])
            lng = sbF("f_lng", [128, NF], F32)
            lnb = sbF("f_lnb", [128, NF], F32)
            b1a = sbF("f_b1a", [128, NJ], F32)
            b1g = sbF("f_b1g", [128, NJ], F32)
            b2 = sbF("f_b2", [128, NF], F32)
            p.dma(lng[:], key("lng"))
            p.dma(lnb[:], key("lnb"))
            p.dma(b1a[:], key("b1a"))
            p.dma(b1g[:], key("b1g"))
            p.dma(b2[:], key("b2"))
            xw = sbF("f_xw", [128, NF, Tcf + 1], BF16)
            xn = sbF("f_xn", [128, NF, Tcf], BF16)
            gel = [sbF("f_gel", [128, Tcf], BF16) for _ in range(2)]
            u = sbF("f_u", [128, NJ, Tcf], BF16)
            xo = sbF("f_xo", [128, NF, Tcf], BF16)
            sqf2 = [sbF("f_sqf2", [128, Tcf], F32) for _ in range(2)]
            lnt2 = [sbF("f_lnt2", [128, Tcf], F32) for _ in range(2)]
            p_sum = psF("f_p_sum", [1, Tcf])
            p_sq = psF("f_p_sq", [1, Tcf])
            p_a = [psF("f_p_a", [128, Tcf]) for _ in range(2)]
            p_g = [psF("f_p_g", [128, Tcf]) for _ in range(2)]
            p_m = [psF("f_p_m", [128, Tcf]) for _ in range(2)]
            p_b1 = p_m[0]
            p_b2 = p_m[1]
            for c in range(T // Tcf):
                t0 = c * Tcf
                p.dma(xw[:], io["scr_x"][:, :, t0:t0 + Tcf + 1])
                xsw = [xw[:, kt, 1:Tcf + 1] if kt < 4 else xw[:, kt, 0:Tcf] for kt in range(NF)]
                for kt in range(NF):
                    p.t(lambda e, kt=kt: e.matmul(p_sum[:], ones_bf[:], xsw[kt],
                                                  start=(kt == 0), stop=(kt == NF - 1)))
                for kt in range(NF):
                    p.s(lambda e, kt=kt: e.activation(sqf2[kt % 2][:], xsw[kt], AF.Square))
                    p.t(lambda e, kt=kt: e.matmul(p_sq[:], ones_f32[:], sqf2[kt % 2][:],
                                                  start=(kt == 0), stop=(kt == NF - 1)))
                _ln_rows(p, p_sum[:], p_sq[:], Tcf, 1e-5, rows)
                p.t(lambda e: e.matmul(p_b1[:], ones1_f32[:], rows[1][0:1, 0:Tcf], start=True, stop=True))
                p.t(lambda e: e.matmul(p_b2[:], ones1_f32[:], rows[2][0:1, 0:Tcf], start=True, stop=True))
                for kt in range(NF):
                    lt = lnt2[kt % 2]
                    p.v(lambda e, kt=kt, lt=lt: e.tensor_tensor(lt[:], xsw[kt], p_b1[:], ALU.mult))
                    p.v(lambda e, lt=lt: e.tensor_tensor(lt[:], lt[:], p_b2[:], ALU.add))
                    p.v(lambda e, kt=kt, lt=lt: e.tensor_scalar(xn[:, kt, :], lt[:],
                                                         lng[:, kt:kt + 1], lnb[:, kt:kt + 1],
                                                         ALU.mult, ALU.add))
                for jt in range(NJ):
                    mj = 128 if jt < NJ - 1 else 42
                    pa, pg, gl = p_a[jt % 2], p_g[jt % 2], gel[jt % 2]
                    for kt in range(NF):
                        p.t(lambda e, jt=jt, kt=kt, mj=mj, pa=pa: e.matmul(
                            pa[0:mj, :], w1[:, kt, jt * 128:jt * 128 + mj], xn[:, kt, :],
                            start=(kt == 0), stop=(kt == NF - 1)))
                    for kt in range(NF):
                        p.t(lambda e, jt=jt, kt=kt, mj=mj, pg=pg: e.matmul(
                            pg[0:mj, :], w1[:, kt, 2730 + jt * 128:2730 + jt * 128 + mj],
                            xn[:, kt, :], start=(kt == 0), stop=(kt == NF - 1)))
                    p.s(lambda e, jt=jt, mj=mj, pg=pg, gl=gl: e.activation(gl[0:mj, :], pg[0:mj, :], AF.Gelu,
                                                             bias=b1g[0:mj, jt:jt + 1]))
                    p.v(lambda e, jt=jt, mj=mj, pa=pa, gl=gl: e.scalar_tensor_tensor(
                        u[0:mj, jt, :], pa[0:mj, :], b1a[0:mj, jt:jt + 1], gl[0:mj, :],
                        ALU.add, ALU.mult))
                for mt in range(NF):
                    pm = p_m[mt % 2]
                    for jt in range(NJ):
                        mj = 128 if jt < NJ - 1 else 42
                        p.t(lambda e, mt=mt, jt=jt, mj=mj, pm=pm: e.matmul(
                            pm[:], w2[0:mj, jt, mt * 128:(mt + 1) * 128], u[0:mj, jt, :],
                            start=(jt == 0), stop=(jt == NJ - 1)))
                    p.v(lambda e, mt=mt, pm=pm: e.scalar_tensor_tensor(
                        xo[:, mt, :], pm[:], b2[:, mt:mt + 1], xw[:, mt, 1:Tcf + 1],
                        ALU.add, ALU.add))
                p.dma(io["xout"][:, :, t0:t0 + Tcf], xo[:, :, :])


def emit_addln(p, io, T=T_FULL, Tc=512):
    nc = p.nc
    with ExitStack() as st:
        sb = lambda n, s, d: st.enter_context(nc.sbuf_tensor(p.uname(n), s, d))
        ps = lambda n, s, d=F32: st.enter_context(nc.psum_tensor(p.uname(n), s, d))
        ones_f32 = sb("g_ones_f32", [128, 1], F32)
        ones1_f32 = sb("g_ones1_f32", [1, 128], F32)
        identf = sb("g_identf", [128, 128], F32)
        p.dma(identf[:], io["IDENTF"])
        p.v(lambda e: e.memset(ones_f32[:], 1.0))
        p.v(lambda e: e.memset(ones1_f32[:], 1.0))
        rows = (sb("g_r0", [1, Tc], F32), sb("g_r1", [1, Tc], F32),
                sb("g_r2", [1, Tc], F32), sb("g_r3", [1, Tc], F32))
        a_c = sb("g_a", [128, NF, Tc], BF16)
        b_c = sb("g_b", [128, NF, Tc], BF16)
        u_c = sb("g_u", [128, NF, Tc], F32)
        xo = sb("g_xo", [128, NF, Tc], F32)
        sq = sb("g_sq", [128, Tc], F32)
        otile = sb("g_otile", [128, 128], F32)
        p_sum = ps("g_p_sum", [1, Tc])
        p_sq = ps("g_p_sq", [1, Tc])
        p_b1 = ps("g_p_b1", [128, Tc])
        p_b2 = ps("g_p_b2", [128, Tc])
        p_tr = ps("g_p_tr", [128, 128])
        for c in range(T // Tc):
            t0 = c * Tc
            p.dma(a_c[:], io["ain"][:, :, t0:t0 + Tc])
            p.dma(b_c[:], io["bin"][:, :, t0:t0 + Tc])
            for ft in range(NF):
                p.v(lambda e, ft=ft: e.tensor_tensor(u_c[:, ft, :], a_c[:, ft, :],
                                                     b_c[:, ft, :], ALU.add))
            for ft in range(NF):
                p.t(lambda e, ft=ft: e.matmul(p_sum[:], ones_f32[:], u_c[:, ft, :],
                                              start=(ft == 0), stop=(ft == NF - 1)))
            for ft in range(NF):
                p.s(lambda e, ft=ft: e.activation(sq[:], u_c[:, ft, :], AF.Square))
                p.t(lambda e, ft=ft: e.matmul(p_sq[:], ones_f32[:], sq[:],
                                              start=(ft == 0), stop=(ft == NF - 1)))
            _ln_rows(p, p_sum[:], p_sq[:], Tc, 1e-5, rows)
            p.t(lambda e: e.matmul(p_b1[:], ones1_f32[:], rows[1][0:1, 0:Tc], start=True, stop=True))
            p.t(lambda e: e.matmul(p_b2[:], ones1_f32[:], rows[2][0:1, 0:Tc], start=True, stop=True))
            for ft in range(NF):
                p.v(lambda e, ft=ft: e.tensor_tensor(sq[:], u_c[:, ft, :], p_b1[:], ALU.mult))
                p.v(lambda e, ft=ft: e.tensor_tensor(xo[:, ft, :], sq[:], p_b2[:], ALU.add))
            for ft in range(NF):
                for tt in range(Tc // 128):
                    p.t(lambda e, ft=ft, tt=tt: e.transpose(
                        p_tr[:], xo[:, ft, tt * 128:(tt + 1) * 128], identf[:]))
                    p.v(lambda e: e.tensor_copy(otile[:], p_tr[:]))
                    p.dma(io["out"][t0 + tt * 128:t0 + (tt + 1) * 128,
                                    ft * 128:(ft + 1) * 128], otile[:])


def _rotary_tables(T):
    DIMH, W = 64, 256
    half = np.arange(0, DIMH, 2, dtype=np.float64)
    inv_freq = 1.0 / 10000.0 ** (half / DIMH)
    t = np.arange(T, dtype=np.float64)
    freqs = t[:, None] * inv_freq[None, :]
    freqs = np.concatenate([freqs, freqs], -1)
    base_scale = (half + 0.4 * DIMH) / (1.4 * DIMH)
    power = (t - T // 2) / (W // 2)
    scale = base_scale[None, :] ** power[:, None]
    scale = np.concatenate([scale, scale], -1)
    cos, sin = np.cos(freqs), np.sin(freqs)
    CS = (cos * scale).T.astype(np.float32)
    SS = (sin * scale).T.astype(np.float32)
    CI = (cos / scale).T.astype(np.float32)
    SI = (sin / scale).T.astype(np.float32)
    two = lambda M: np.ascontiguousarray(np.concatenate([M, M], 0))
    return two(CS), two(SS), two(CI), two(SI)


def _l1ffn_masks():
    W = 256
    M = np.zeros((128, 6, 2 * W), np.float32)
    for kt in range(6):
        ks = kt * 128 + np.arange(128)
        qi = np.arange(2 * W)
        ok = (qi[None, :] <= ks[:, None]) & (qi[None, :] >= ks[:, None] - W)
        M[:, kt, :] = np.where(ok, 0.0, -1e30)
    return M


def prep_l1ffn_inputs(w, T=T_FULL):
    c = np.ascontiguousarray
    CS2, SS2, CI2, SI2 = _rotary_tables(T)
    HONES = np.zeros((128, 2), np.float32)
    HONES[0:64, 0] = 1.0
    HONES[64:128, 1] = 1.0
    SEL2 = np.zeros((2, 128), np.float32)
    SEL2[0, 0:64] = 1.0
    SEL2[1, 64:128] = 1.0
    d = {
        "CS2": CS2, "SS2": SS2, "CI2": CI2, "SI2": SI2,
        "MASK6": _l1ffn_masks(), "HONES": HONES, "SEL2": SEL2,
        "IDENTF": np.eye(128, dtype=np.float32),
    }
    for li in range(6):
        W2p = np.zeros((22 * 128, 1024), np.float32)
        W2p[0:2730] = w["L1_ffn_W2"][li]
        b1 = w["L1_ffn_b1"][li]
        b1a = np.zeros((22 * 128,), np.float32)
        b1a[0:2730] = b1[0:2730]
        b1g = np.zeros((22 * 128,), np.float32)
        b1g[0:2730] = b1[2730:5460]
        qs = w["L1_ffn_q_scale"][li]
        ks = w["L1_ffn_k_scale"][li]
        d.update({
            f"l{li}_WqkvT": c(w["L1_ffn_Wqkv"][li]),
            f"l{li}_WoT": c(w["L1_ffn_Wo"][li]),
            f"l{li}_W1": c(w["L1_ffn_W1"][li]),
            f"l{li}_W2p": W2p,
            f"l{li}_qs2": c(np.concatenate([qs, qs]).reshape(128, 1)),
            f"l{li}_ks2": c(np.concatenate([ks, ks]).reshape(128, 1)),
            f"l{li}_qs_row": qs.reshape(1, 64).copy(),
            f"l{li}_ks_row": ks.reshape(1, 64).copy(),
            f"l{li}_lng": c(w["L1_ffn_ln_g"][li].reshape(8, 128).T),
            f"l{li}_lnb": c(w["L1_ffn_ln_b"][li].reshape(8, 128).T),
            f"l{li}_b1a": c(b1a.reshape(22, 128).T),
            f"l{li}_b1g": c(b1g.reshape(22, 128).T),
            f"l{li}_b2": c(w["L1_ffn_b2"][li].reshape(8, 128).T),
        })
    return d


def build_l1ffn_program(li="X", T=T_FULL):
    nc = bass.Bass("TRN2", target_bir_lowering=False)
    dram = lambda n, s, dt, k: nc.dram_tensor(n, s, dt, kind=k).ap()
    io = {
        "xin": dram("xin", [128, NF, T], BF16, "ExternalInput"),
        "CS2": dram("CS2", [128, T], F32, "ExternalInput"),
        "SS2": dram("SS2", [128, T], F32, "ExternalInput"),
        "CI2": dram("CI2", [128, T], F32, "ExternalInput"),
        "SI2": dram("SI2", [128, T], F32, "ExternalInput"),
        "MASK6": dram("MASK6", [128, 6, 512], F32, "ExternalInput"),
        "HONES": dram("HONES", [128, 2], F32, "ExternalInput"),
        "SEL2": dram("SEL2", [2, 128], F32, "ExternalInput"),
        "scr_ctx": dram("scr_ctx", [128, NF, T], BF16, "Internal"),
        "scr_x": dram("scr_x", [128, NF, T + 1], BF16, "Internal"),
        "xout": dram("xout", [128, NF, T], BF16, "ExternalOutput"),
    }
    li_keys = ["WqkvT", "WoT", "W1", "W2p", "qs2", "ks2", "qs_row", "ks_row",
               "lng", "lnb", "b1a", "b1g", "b2"]
    shapes = {
        "WqkvT": [D, 3 * D], "WoT": [D, D], "W1": [D, 5460], "W2p": [22 * 128, D],
        "qs2": [128, 1], "ks2": [128, 1], "qs_row": [1, 64], "ks_row": [1, 64],
        "lng": [128, NF], "lnb": [128, NF], "b1a": [128, 22], "b1g": [128, 22],
        "b2": [128, NF],
    }
    for nm in li_keys:
        io[f"l{li}_{nm}"] = dram(f"l{li}_{nm}", shapes[nm], F32, "ExternalInput")
    with ExitStack() as stack:
        p = Prog(nc, stack)
        emit_l1ffn_layer(p, io, li, T=T, Tc=min(512, T), Tcf=min(512, T))
        p.finalize(strict=STRICT)
    return nc

def build_addln_program(T=T_FULL):
    nc = bass.Bass("TRN2", target_bir_lowering=False)
    dram = lambda n, s, dt, k: nc.dram_tensor(n, s, dt, kind=k).ap()
    io = {
        "ain": dram("ain", [128, NF, T], BF16, "ExternalInput"),
        "bin": dram("bin", [128, NF, T], BF16, "ExternalInput"),
        "IDENTF": dram("IDENTF", [128, 128], F32, "ExternalInput"),
        "out": dram("out", [T, D], F32, "ExternalOutput"),
    }
    with ExitStack() as stack:
        p = Prog(nc, stack)
        emit_addln(p, io, T=T, Tc=min(512, T))
        p.finalize(strict=STRICT)
    return nc





def build_fused_program(T=T_FULL):
    nc = bass.Bass("TRN2", target_bir_lowering=False)
    DI = 2048
    dram = lambda n, s, dt, k: nc.dram_tensor(n, s, dt, kind=k).ap()

    def stream(name):
        return dram(name, [128, NF, T], BF16, "Internal")

    io_m = {
        "xT": dram("xT", [D, T], F32, "ExternalInput"),
        "inprojT": dram("inprojT", [D, 2 * DI], F32, "ExternalInput"),
        "convw": dram("convw", [128, 16, 4], F32, "ExternalInput"),
        "convb": dram("convb", [128, 16], F32, "ExternalInput"),
        "xprojT_dt": dram("xprojT_dt", [128, 16, 64], F32, "ExternalInput"),
        "xprojT_B": dram("xprojT_B", [128, 16, 16], F32, "ExternalInput"),
        "xprojT_C": dram("xprojT_C", [128, 16, 16], F32, "ExternalInput"),
        "dtprojT": dram("dtprojT", [64, DI], F32, "ExternalInput"),
        "dtbias": dram("dtbias", [128, 16], F32, "ExternalInput"),
        "Alog_fm": dram("Alog_fm", [128, 16, 16], F32, "ExternalInput"),
        "D_fm": dram("D_fm", [128, 16], F32, "ExternalInput"),
        "outprojT": dram("outprojT", [DI, D], F32, "ExternalInput"),
        "ln1g": dram("ln1g", [128, NF], F32, "ExternalInput"),
        "ln1b": dram("ln1b", [128, NF], F32, "ExternalInput"),
        "ln2g": dram("ln2g", [128, NF], F32, "ExternalInput"),
        "ln2b": dram("ln2b", [128, NF], F32, "ExternalInput"),
        "fw1T": dram("fw1T", [D, 4 * D], F32, "ExternalInput"),
        "fb1": dram("fb1", [128, 32], F32, "ExternalInput"),
        "fw2T": dram("fw2T", [4 * D, D], F32, "ExternalInput"),
        "fb2": dram("fb2", [128, NF], F32, "ExternalInput"),
        "SEL16": dram("SEL16", [16, 2048], F32, "ExternalInput"),
        "scr_xc": dram("scr_xc", [128, 16, T], BF16, "Internal"),
        "scr_dt": dram("scr_dt", [128, 16, T], BF16, "Internal"),
        "scr_sz": dram("scr_sz", [128, 16, T], BF16, "Internal"),
        "scr_B": dram("scr_B", [16, T], BF16, "Internal"),
        "scr_C": dram("scr_C", [16, T], BF16, "Internal"),
        "scr_y": dram("scr_y", [128, 16, T], BF16, "Internal"),
        "scr_r1": dram("scr_r1", [128, NF, T], BF16, "Internal"),
        "scr_s2": dram("scr_s2", [128, NF, T], BF16, "Internal"),
        "xout": stream("str0"),
    }
    identf = dram("IDENTF", [128, 128], F32, "ExternalInput")
    io_moe = {
        "xin": io_m["xout"],
        "moe_w1": dram("moe_w1", [128, 6, NF, 512], F32, "ExternalInput"),
        "moe_w2": dram("moe_w2", [128, 6, 4, D], F32, "ExternalInput"),
        "moe_projT": dram("moe_projT", [128, NF, 6], F32, "ExternalInput"),
        "moe_anchT": dram("moe_anchT", [6, 6], F32, "ExternalInput"),
        "moe_temp": dram("moe_temp", [1, 1], F32, "ExternalInput"),
        "SEL6": dram("SEL6", [6, 768], F32, "ExternalInput"),
        "IDENTF": identf,
        "xout": stream("str1"),
    }
    io_a = {
        "xin": io_moe["xout"],
        "qwT": dram("qwT", [D, D], F32, "ExternalInput"),
        "kwT": dram("kwT", [D, D], F32, "ExternalInput"),
        "vwT": dram("vwT", [D, D], F32, "ExternalInput"),
        "qb_c": dram("qb_c", [64, 16], F32, "ExternalInput"),
        "kb_c": dram("kb_c", [64, 16], F32, "ExternalInput"),
        "vb_rep": dram("vb_rep", [128, 16, 64], F32, "ExternalInput"),
        "outwT": dram("outwT", [D, D], F32, "ExternalInput"),
        "outb_c": dram("outb_c", [128, NF], F32, "ExternalInput"),
        "scr_ctx": dram("a_scr_ctx", [128, NF, T], BF16, "Internal"),
        "xout": stream("str2"),
    }
    io_f = {
        "CS2": dram("CS2", [128, T], F32, "ExternalInput"),
        "SS2": dram("SS2", [128, T], F32, "ExternalInput"),
        "CI2": dram("CI2", [128, T], F32, "ExternalInput"),
        "SI2": dram("SI2", [128, T], F32, "ExternalInput"),
        "MASK6": dram("MASK6", [128, 6, 512], F32, "ExternalInput"),
        "HONES": dram("HONES", [128, 2], F32, "ExternalInput"),
        "SEL2": dram("SEL2", [2, 128], F32, "ExternalInput"),
        "scr_ctx": dram("f_scr_ctx", [128, NF, T], BF16, "Internal"),
        "scr_x": dram("f_scr_x", [128, NF, T + 1], BF16, "Internal"),
    }
    li_keys = ["WqkvT", "WoT", "W1", "W2p", "qs2", "ks2", "qs_row", "ks_row",
               "lng", "lnb", "b1a", "b1g", "b2"]
    shapes = {
        "WqkvT": [D, 3 * D], "WoT": [D, D], "W1": [D, 5460], "W2p": [22 * 128, D],
        "qs2": [128, 1], "ks2": [128, 1], "qs_row": [1, 64], "ks_row": [1, 64],
        "lng": [128, NF], "lnb": [128, NF], "b1a": [128, 22], "b1g": [128, 22],
        "b2": [128, NF],
    }
    for li in range(6):
        for nm in li_keys:
            io_f[f"l{li}_{nm}"] = dram(f"l{li}_{nm}", shapes[nm], F32, "ExternalInput")
    lstreams = [io_a["xout"]] + [stream(f"str{3 + i}") for i in range(6)]
    io_g = {
        "ain": io_a["xout"],
        "bin": lstreams[6],
        "IDENTF": identf,
        "out": dram("out", [T, D], F32, "ExternalOutput"),
    }

    with ExitStack() as stack:
        p = Prog(nc, stack)
        emit_mamba(p, io_m, T=T, Tc_a=min(512, T), Tc_b=min(1024, T), Tc_d=min(256, T))
        emit_moe(p, io_moe, T=T, Tc=min(512, T))
        emit_attn(p, io_a, T=T, Tc=min(512, T))
        for li in range(6):
            iof = dict(io_f)
            iof["xin"] = lstreams[li]
            iof["xout"] = lstreams[li + 1]
            emit_l1ffn_layer(p, iof, li, T=T, Tc=min(512, T), Tcf=min(512, T))
        emit_addln(p, io_g, T=T, Tc=min(512, T))
        p.finalize(strict=STRICT)
    return nc


def kernel(**inputs):
    w = inputs
    T = T_FULL
    _ = np.asarray(inputs["L1_attn_mask"])
    x = np.asarray(w["L0_attn_x"], np.float32)
    nc = _get_prog("fused", lambda: build_fused_program(T=T))
    base = {}
    base.update(prep_moe_inputs(w))
    base.update(prep_attn_inputs(w))
    base.update(prep_l1ffn_inputs(w, T=T))
    maps = []
    for b in range(NCORES):
        m = dict(base)
        m.update(prep_mamba_inputs(np.ascontiguousarray(x[b]), w))
        maps.append(m)
    r = _launch(nc, maps)
    return np.stack([np.asarray(rr["out"], dtype=np.float32) for rr in r], axis=0)


_prog_cache = {}


def _get_prog(name, builder):
    if name not in _prog_cache:
        _prog_cache[name] = builder()
    return _prog_cache[name]


def _launch(nc, in_maps):
    res = run_bass_kernel_spmd(nc, in_maps, core_ids=list(range(NCORES)))
    return res.results


def _filter(nc, m):
    names = set()
    for t in nc.m.tensors:
        names.add(t.name)
    return m


def kernel_unfused(**inputs):
    w = inputs
    T = T_FULL
    _ = np.asarray(inputs["L1_attn_mask"])
    x = np.asarray(w["L0_attn_x"], np.float32)

    nc1 = _get_prog("mamba", lambda: build_mamba_program(T=T))
    maps = []
    shared1 = None
    for b in range(NCORES):
        m = prep_mamba_inputs(np.ascontiguousarray(x[b]), w)
        maps.append(m)
    r = _launch(nc1, maps)
    streams = [np.asarray(rr["xout"]) for rr in r]

    nc2 = _get_prog("moe", lambda: build_moe_program(T=T))
    base2 = prep_moe_inputs(w)
    maps = [dict(base2, xin=streams[b]) for b in range(NCORES)]
    r = _launch(nc2, maps)
    streams = [np.asarray(rr["xout"]) for rr in r]

    nc3 = _get_prog("attn", lambda: build_attn_program(T=T))
    base3 = prep_attn_inputs(w)
    maps = [dict(base3, xin=streams[b]) for b in range(NCORES)]
    r = _launch(nc3, maps)
    streams = [np.asarray(rr["xout"]) for rr in r]
    pre_l1ffn = streams

    nc4 = _get_prog("l1ffn", lambda: build_l1ffn_program("X", T=T))
    base4 = prep_l1ffn_inputs(w, T=T)
    shared_keys = ["CS2", "SS2", "CI2", "SI2", "MASK6", "HONES", "SEL2"]
    li_keys = ["WqkvT", "WoT", "W1", "W2p", "qs2", "ks2", "qs_row", "ks_row",
               "lng", "lnb", "b1a", "b1g", "b2"]
    for li in range(6):
        m0 = {k: base4[k] for k in shared_keys}
        for nm in li_keys:
            m0[f"lX_{nm}"] = base4[f"l{li}_{nm}"]
        maps = [dict(m0, xin=streams[b]) for b in range(NCORES)]
        r = _launch(nc4, maps)
        streams = [np.asarray(rr["xout"]) for rr in r]

    nc5 = _get_prog("addln", lambda: build_addln_program(T=T))
    ident = np.eye(128, dtype=np.float32)
    maps = [dict(ain=pre_l1ffn[b], bin=streams[b], IDENTF=ident)
            for b in range(NCORES)]
    r = _launch(nc5, maps)
    out = np.stack([np.asarray(rr["out"], dtype=np.float32) for rr in r], axis=0)
    return out
